# version 09
# speedup vs baseline: 1.3839x; 1.0049x over previous
import contextlib
import numpy as np
import ml_dtypes
import concourse.bass as bass
import concourse.mybir as mybir
from concourse.bass_utils import run_bass_kernel_spmd

F32 = mybir.dt.float32
BF16 = mybir.dt.bfloat16
I32 = mybir.dt.int32
AF = mybir.ActivationFunctionType
ALU = mybir.AluOpType
AX = mybir.AxisListType.X

NCORE = 8
D = 1024
TCORE = 2048
HALO = 64
W1 = HALO + TCORE
BF = ml_dtypes.bfloat16
EPS = float(1e-5)


def chunks_of(width, step=512):
    out = []
    o = 0
    while o < width:
        n = min(step, width - o)
        out.append((o, n))
        o += n
    return out


class P:

    def __init__(self, ndev=1):
        self.nc = bass.Bass("TRN2", target_bir_lowering=False, debug=False,
                            num_devices=ndev)
        self.st = contextlib.ExitStack()
        self.root_st = self.st
        self.cnt = {}
        self.sems = {}
        self._names = {}
        self.block = self.st.enter_context(self.nc.Block())

    def _uname(self, name):
        k = self._names.get(name, 0)
        self._names[name] = k + 1
        return name if k == 0 else f"{name}_{k}"

    @contextlib.contextmanager
    def scope(self):
        old = self.st
        self.st = contextlib.ExitStack()
        try:
            yield
        finally:
            self.st.close()
            self.st = old

    def dram(self, name, shape, dt=F32):
        return self.nc.dram_tensor(self._uname("dr_" + name), shape, dt,
                                   kind="Internal").ap()

    def barrier(self):
        names = [nm for nm in self.cnt if self.cnt[nm] > 0]
        for sec in (self.T, self.V, self.S, self.SY, self.G):
            def w(eng, names=tuple(names)):
                self.wait(eng, *names)
            sec(w)

    def sem(self, name):
        if name not in self.sems:
            self.sems[name] = self.st.enter_context(self.nc.semaphore(name))
            self.cnt[name] = 0
        return self.sems[name]

    def sb(self, name, shape, dt=F32):
        return self.st.enter_context(
            self.nc.sbuf_tensor(self._uname("sb_" + name), shape, dt))[:]

    def ps(self, name, shape, dt=F32):
        return self.st.enter_context(
            self.nc.psum_tensor(self._uname("ps_" + name), shape, dt))[:]

    io = None

    def din(self, name, shape, dt=F32):
        if self.io is not None and name in self.io:
            return self.io[name]
        return self.nc.dram_tensor(self._uname(name), shape, dt,
                                   kind="ExternalInput").ap()

    def dout(self, name, shape, dt=F32):
        if self.io is not None and name in self.io:
            return self.io[name]
        return self.nc.dram_tensor(self._uname(name), shape, dt,
                                   kind="ExternalOutput").ap()

    def inc(self, ins, name, n=1):
        ins.then_inc(self.sem(name), n)
        self.cnt[name] += n
        return ins

    def cop(self, eng, name, ins):
        self.inc(ins, name)
        eng.wait_ge(self.sems[name], self.cnt[name])
        return ins

    def wait_tok(self, eng, name, count):
        if count > 0:
            eng.wait_ge(self.sem(name), count)

    def wait(self, eng, *names):
        for nm in names:
            if nm in self.cnt and self.cnt[nm] > 0:
                eng.wait_ge(self.sem(nm), self.cnt[nm])

    def T(self, emit):
        @self.block.tensor
        def _(tensor):
            emit(tensor)

    def V(self, emit):
        @self.block.vector
        def _(vector):
            emit(vector)

    def S(self, emit):
        @self.block.scalar
        def _(scalar):
            emit(scalar)

    def SY(self, emit):
        @self.block.sync
        def _(sync):
            emit(sync)

    def G(self, emit):
        @self.block.gpsimd
        def _(gpsimd):
            emit(gpsimd)

    def done(self):
        self.st.close()
        return self.nc



def build_prep(p=None, io=None):
    standalone = p is None
    if standalone:
        p = P()
    p.io = io or {}
    i_WA = p.din("WA", [D, D]); i_WB = p.din("WB", [D, D]); i_WC = p.din("WC", [D, D])
    i_qw = p.din("qw", [6, D, D], I32)
    i_qs = p.din("qs", [6, D, 64])
    i_la = p.din("la", [6, 32, D])
    i_lb = p.din("lb", [6, D, 32])
    i_inwf = p.din("inwf", [4096, D]); i_inwb = p.din("inwb", [4096, D])
    i_outwf = p.din("outwf", [D, 2048]); i_outwb = p.din("outwb", [D, 2048])
    i_mw = p.din("mw", [D, 2048])
    i_xpf = p.din("xpf", [96, 2048]); i_xpb = p.din("xpb", [96, 2048])
    i_dtf = p.din("dtf", [2048, 64]); i_dtb_w = p.din("dtb_w", [2048, 64])
    i_gw = p.din("gw", [8, D])
    i_bias = p.din("bias", [6, D])
    i_dtbf = p.din("dtbf", [2048]); i_dtbb = p.din("dtbb", [2048])
    i_iden = p.din("iden", [128, 128], BF16)
    i_ones = p.din("onescol", [128, 1], BF16)
    i_onesf = p.din("onescolf", [128, 1], F32)
    i_onesrf = p.din("onesrowf", [1, 128], F32)

    o_qwT = p.dout("qwT", [6, D, D], BF16)
    o_laT = p.dout("laT", [6, D, 32], BF16)
    o_lbT = p.dout("lbT", [6, 32, D], BF16)
    o_bwT = p.dout("bwT", [D, D], BF16)
    o_cT = p.dout("cT", [D, D], BF16)
    o_bpow = p.dout("bpow", [5, D, D], BF16)
    o_inwTf = p.dout("inwTf", [D, 4096], BF16)
    o_inwTb = p.dout("inwTb", [D, 4096], BF16)
    o_combTf = p.dout("combTf", [2048, D], BF16)
    o_combTb = p.dout("combTb", [2048, D], BF16)
    o_xpTf = p.dout("xpTf", [2048, 96], BF16)
    o_xpTb = p.dout("xpTb", [2048, 96], BF16)
    o_dtwTf = p.dout("dtwTf", [64, 2048], BF16)
    o_dtwTb = p.dout("dtwTb", [64, 2048], BF16)
    o_gwT = p.dout("gwT", [D, 8], BF16)
    o_brow = p.dout("brow", [1, 6 * D], BF16)
    o_dtbrf = p.dout("dtbrf", [1, 2048], BF16)
    o_dtbrb = p.dout("dtbrb", [1, 2048], BF16)
    o_sig = p.dout("sig", [1, 4])

    src = p.sb("src", [128, 16 * 1024])
    srcb = p.sb("srcb", [128, 16 * 1024], BF16)
    dst = p.sb("dstb", [128, 16 * 1024], BF16)
    iden = p.sb("iden", [128, 128], BF16)
    onescol = p.sb("onescol", [128, 1], BF16)
    onescol_f = p.sb("onescol_f", [128, 1], F32)
    onesrow_f = p.sb("onesrow_f", [1, 128], F32)
    wbf = p.sb("wbf", [128, 8 * 1024], BF16)
    wf32 = src[:, 0:8 * 1024]
    dqf32 = src[:, 8 * 1024:16 * 1024]
    mp = p.sb("mp", [128, 8 * 1024], BF16)
    b1 = p.sb("b1", [128, 8 * 1024], BF16)
    b2 = p.sb("b2", [128, 8 * 1024], BF16)
    vec = p.sb("vecs", [128, 64])
    vecb = p.sb("vecsb", [128, 16], BF16)
    row = p.sb("rowsc", [1, 64])
    isig = p.sb("isig", [128, 4])
    qscale = p.sb("qscale", [128, 8 * 64])
    ps0 = p.ps("ps0", [128, 512])
    ps1 = p.ps("ps1", [128, 512])
    psT = p.ps("psT", [128, 128], BF16)
    psT2 = p.ps("psT2", [128, 128], BF16)
    psv = p.ps("psv", [128, 8])

    def din_dma(dst_ap, src_ap, wait=("v", "pe")):
        def f(sync):
            p.wait(sync, *wait)
            p.inc(sync.dma_start(out=dst_ap, in_=src_ap), "sy", 16)
        p.SY(f)

    def dout_dma(dst_ap, src_ap):
        def f(sync):
            p.wait(sync, "v")
            p.inc(sync.dma_start(out=dst_ap, in_=src_ap), "so", 16)
        p.SY(f)

    din_dma(iden, i_iden, wait=())
    din_dma(onescol, i_ones, wait=())
    din_dma(onescol_f, i_onesf, wait=())
    din_dma(onesrow_f, i_onesrf, wait=())

    def transpose_job(mat_in, R, C, out_dram, scale_ap=None):
        if R > 2048:
            for g0 in range(0, R, 2048):
                transpose_job_part(mat_in[g0:g0 + 2048, :], 2048, C,
                                   out_dram[:, g0:g0 + 2048], scale_ap)
        else:
            transpose_job_part(mat_in, R, C, out_dram, scale_ap)

    def transpose_job_part(mat_in, R, C, out_dram, scale_ap=None):
        rtiles = [(i * 128, min(128, R - i * 128)) for i in range((R + 127) // 128)]
        ctiles = [(i * 128, min(128, C - i * 128)) for i in range((C + 127) // 128)]
        nr = len(rtiles)
        Crnd = len(ctiles) * 128
        if R >= 128:
            din_dma(src[:, 0:nr * C].rearrange("p (a c) -> p a c", a=nr),
                    mat_in.rearrange("(a p) c -> p a c", p=128), wait=("v", "so"))
        else:
            din_dma(src[0:R, 0:C], mat_in, wait=("v", "so"))

        def cast(vector):
            p.wait(vector, "sy")
            if scale_ap is None:
                p.cop(vector, "v", vector.tensor_copy(out=srcb[0:min(R, 128), 0:nr * C],
                                         in_=src[0:min(R, 128), 0:nr * C]))
            else:
                p.cop(vector, "v", vector.tensor_scalar(out=srcb[0:min(R, 128), 0:nr * C],
                                           in0=src[0:min(R, 128), 0:nr * C],
                                           scalar1=scale_ap, scalar2=None,
                                           op0=ALU.mult))
        p.V(cast)

        t_v = {-2: p.cnt["v"], -1: p.cnt["v"]}
        t_pe = {}
        gi = 0
        for r, (r0, rd) in enumerate(rtiles):
            for (c0, cd) in ctiles:
                pst = psT if gi % 2 == 0 else psT2

                def tmm(tensor, r=r, rd=rd, c0=c0, cd=cd, gi=gi, pst=pst):
                    if gi < 2:
                        p.wait(tensor, "v")
                    else:
                        p.wait_tok(tensor, "v", t_v[gi - 2])
                    p.inc(tensor.transpose(pst[0:cd, 0:rd],
                                           srcb[0:rd, r * C + c0: r * C + c0 + cd],
                                           iden[0:rd, 0:rd]), "pe")
                    t_pe[gi] = p.cnt["pe"]
                p.T(tmm)

                def evac(vector, r=r, rd=rd, c0=c0, cd=cd, gi=gi, pst=pst):
                    p.wait_tok(vector, "pe", t_pe[gi])
                    p.cop(vector, "v", vector.tensor_copy(
                        out=dst[:, 0:len(ctiles) * ((nr * 128))].rearrange(
                            "p (a q) -> p a q", a=len(ctiles))[0:cd, c0 // 128,
                                                              r * 128: r * 128 + rd],
                        in_=pst[0:cd, 0:rd]))
                    t_v[gi] = p.cnt["v"]
                p.V(evac)
                gi += 1

        Rrnd = nr * 128
        na = len(ctiles)
        if C % 128 == 0:
            dout_dma(out_dram.rearrange("(a p) q -> p a q", p=128),
                     dst[:, 0:na * Rrnd].rearrange("p (a q) -> p a q", a=na)[:, :, 0:R])
        else:
            assert na == 1
            dout_dma(out_dram, dst[0:C, 0:R])

    def mm_square(out_sb, lh, rh):
        sq_v = {-2: p.cnt["v"], -1: p.cnt["v"]}
        sq_pe = {}
        for g, (di, half) in enumerate([(a_, b_) for a_ in range(8) for b_ in range(2)]):
            pss = ps0 if g % 2 == 0 else ps1

            def tmm(tensor, di=di, half=half, pss=pss, g=g):
                if g < 2:
                    p.wait(tensor, "v")
                else:
                    p.wait_tok(tensor, "v", sq_v[g - 2])
                for k in range(8):
                    ins = tensor.matmul(
                        pss,
                        lh[:, k * 1024 + di * 128: k * 1024 + (di + 1) * 128],
                        rh[:, k * 1024 + half * 512: k * 1024 + half * 512 + 512],
                        start=(k == 0), stop=(k == 7))
                p.inc(ins, "pe")
                sq_pe[g] = p.cnt["pe"]
            p.T(tmm)

            def evac(vector, di=di, half=half, pss=pss, g=g):
                p.wait_tok(vector, "pe", sq_pe[g])
                p.cop(vector, "v", vector.tensor_copy(
                    out=out_sb[:, di * 1024 + half * 512: di * 1024 + half * 512 + 512],
                    in_=pss))
                sq_v[g] = p.cnt["v"]
            p.V(evac)

    def sigma_chain(i_W, jcol):
        din_dma(wf32.rearrange("p (a c) -> p a c", a=8),
                i_W.rearrange("(a p) c -> p a c", p=128), wait=("v", "pe"))

        def cast(vector):
            p.wait(vector, "sy")
            p.cop(vector, "v", vector.tensor_copy(out=wbf, in_=wf32))
        p.V(cast)

        mm_square(mp, wbf, wbf)

        def t0mm(tensor):
            p.wait(tensor, "v")
            for di in range(8):
                for k in range(8):
                    ins = tensor.matmul(
                        psv[:, di:di + 1],
                        wbf[:, k * 1024 + di * 128: k * 1024 + (di + 1) * 128],
                        onescol, start=(k == 0), stop=(k == 7))
                p.inc(ins, "pe")
        p.T(t0mm)

        def t0ev(vector):
            p.wait(vector, "pe")
            p.cop(vector, "v", vector.tensor_copy(out=vec[:, 0:8], in_=psv[:, 0:8]))
            p.cop(vector, "v", vector.tensor_copy(out=vecb[:, 0:8], in_=vec[:, 0:8]))
        p.V(t0ev)

        def matvec(mat, outcol):
            def tmv(tensor):
                p.wait(tensor, "v")
                for di in range(8):
                    for k in range(8):
                        ins = tensor.matmul(
                            psv[:, di:di + 1],
                            mat[:, k * 1024 + di * 128: k * 1024 + (di + 1) * 128],
                            vecb[:, k:k + 1],
                            start=(k == 0), stop=(k == 7))
                    p.inc(ins, "pe")
            p.T(tmv)

            def emv(vector):
                p.wait(vector, "pe")
                p.cop(vector, "v", vector.tensor_copy(out=vec[:, outcol:outcol + 8],
                                         in_=psv[:, 0:8]))
                p.cop(vector, "v", vector.tensor_copy(out=vecb[:, 8:16],
                                         in_=vec[:, outcol:outcol + 8]))
                p.cop(vector, "v", vector.tensor_copy(out=vecb[:, 0:8], in_=vecb[:, 8:16]))
            p.V(emv)

        matvec(mp, 8)
        mm_square(b1, mp, mp)
        mm_square(b2, b1, b1)
        mm_square(b1, b2, b2)
        matvec(b1, 16)
        matvec(mp, 24)

        def dots(vector):
            p.cop(vector, "v", vector.tensor_mul(out=vec[:, 32:40], in0=vec[:, 24:32],
                                    in1=vec[:, 24:32]))
            p.cop(vector, "v", vector.tensor_mul(out=vec[:, 40:48], in0=vec[:, 16:24],
                                    in1=vec[:, 24:32]))
            p.cop(vector, "v", vector.reduce_sum(out=vec[:, 48:49], in_=vec[:, 32:40], axis=AX))
            p.cop(vector, "v", vector.reduce_sum(out=vec[:, 49:50], in_=vec[:, 40:48], axis=AX))
        p.V(dots)

        for col, dstc in ((48, 0), (49, 1)):
            def psum_sum(tensor, col=col):
                p.wait(tensor, "v")
                p.inc(tensor.matmul(psv[0:1, 0:1], vec[:, col:col + 1], onescol_f,
                                    start=True, stop=True), "pe")
            p.T(psum_sum)

            def getv(vector, dstc=dstc):
                p.wait(vector, "pe")
                p.cop(vector, "v", vector.tensor_copy(out=row[0:1, dstc:dstc + 1],
                                         in_=psv[0:1, 0:1]))
            p.V(getv)

        def sigmath(vector):
            p.cop(vector, "v", vector.reciprocal(out=row[0:1, 2:3], in_=row[0:1, 1:2]))
            p.cop(vector, "v", vector.tensor_mul(out=row[0:1, 3:4], in0=row[0:1, 0:1],
                                    in1=row[0:1, 2:3]))
        p.V(sigmath)

        def sq(scalar):
            p.wait(scalar, "v")
            p.cop(scalar, "s", scalar.activation(out=row[0:1, 4:5], in_=row[0:1, 3:4],
                                    func=AF.Sqrt))
        p.S(sq)

        def inv(vector):
            p.wait(vector, "s")
            p.cop(vector, "v", vector.reciprocal(out=row[0:1, 5:6], in_=row[0:1, 4:5]))
            p.cop(vector, "v", vector.tensor_copy(out=row[0:1, 8 + jcol: 9 + jcol],
                                     in_=row[0:1, 4:5]))
        p.V(inv)

        def bcast(tensor):
            p.wait(tensor, "v")
            p.inc(tensor.matmul(psv[:, 0:1], onesrow_f, row[0:1, 5:6],
                                start=True, stop=True), "pe")
        p.T(bcast)

        def getb(vector):
            p.wait(vector, "pe")
            p.cop(vector, "v", vector.tensor_copy(out=isig[:, jcol:jcol + 1], in_=psv[:, 0:1]))
        p.V(getb)

    sigma_chain(i_WB, 1)
    transpose_job(i_WB, D, D, o_bwT, scale_ap=isig[:, 1:2])
    sigma_chain(i_WC, 2)
    transpose_job(i_WC, D, D, o_cT, scale_ap=isig[:, 2:3])
    sigma_chain(i_WA, 0)

    def mkA(vector):
        p.cop(vector, "v", vector.tensor_scalar(out=b2, in0=wf32, scalar1=isig[:, 0:1],
                                   scalar2=None, op0=ALU.mult))
    p.V(mkA)

    for r in range(8):
        for c in range(8):
            def tmm(tensor, r=r, c=c):
                p.wait(tensor, "v")
                p.inc(tensor.transpose(
                    psT, b2[:, r * 1024 + c * 128: r * 1024 + (c + 1) * 128],
                    iden), "pe")
            p.T(tmm)

            def evac(vector, r=r, c=c):
                p.wait(vector, "pe")
                p.cop(vector, "v", vector.tensor_copy(
                    out=b1[:, c * 1024 + r * 128: c * 1024 + (r + 1) * 128],
                    in_=psT))
            p.V(evac)

    dout_dma(o_bpow[0].rearrange("(a p) c -> p a c", p=128),
             b1.rearrange("p (a c) -> p a c", a=8))

    abuf, bbuf, t1buf, t2buf = b2, b1, mp, wbf.bitcast(BF16)[:, 0:8 * 1024]
    t2buf = wbf
    for lvl in range(1, 5):
        mm_square(t1buf, bbuf, abuf)
        mm_square(t2buf, abuf, bbuf)
        abuf, bbuf, t1buf, t2buf = t1buf, t2buf, abuf, bbuf
        dout_dma(o_bpow[lvl].rearrange("(a p) c -> p a c", p=128),
                 bbuf.rearrange("p (a c) -> p a c", a=8))

    dout_dma(o_sig, row[0:1, 8:12])

    for j in range(6):
        din_dma(src.bitcast(I32)[:, 0:8 * 1024].rearrange("p (a c) -> p a c", a=8),
                i_qw[j].rearrange("(a p) c -> p a c", p=128), wait=("v", "pe", "so"))
        din_dma(qscale.rearrange("p (a c) -> p a c", a=8),
                i_qs[j].rearrange("(a p) c -> p a c", p=128), wait=())

        def dq(vector):
            p.wait(vector, "sy")
            p.cop(vector, "v", vector.tensor_copy(out=dqf32, in_=src.bitcast(I32)[:, 0:8 * 1024]))
            p.cop(vector, "v", vector.tensor_scalar(out=dqf32, in0=dqf32,
                                       scalar1=float(1.0 / 7.5), scalar2=-1.0,
                                       op0=ALU.mult, op1=ALU.add))
            for a in range(8):
                src_ap = dqf32[:, a * 1024:(a + 1) * 1024].rearrange(
                    "p (g o) -> p g o", o=16)
                scl_ap = qscale[:, a * 64:(a + 1) * 64].unsqueeze(2).broadcast_to(
                    (128, 64, 16))
                p.cop(vector, "v", vector.tensor_tensor(
                    out=srcb[:, a * 1024:(a + 1) * 1024].rearrange(
                        "p (g o) -> p g o", o=16),
                    in0=src_ap, in1=scl_ap, op=ALU.mult))
        p.V(dq)

        t_v = {-2: p.cnt["v"], -1: p.cnt["v"]}
        t_pe = {}
        gi = 0
        for r in range(8):
            for c in range(8):
                pst = psT if gi % 2 == 0 else psT2

                def tmm(tensor, r=r, c=c, gi=gi, pst=pst):
                    if gi < 2:
                        p.wait(tensor, "v")
                    else:
                        p.wait_tok(tensor, "v", t_v[gi - 2])
                    p.inc(tensor.transpose(
                        pst, srcb[:, r * 1024 + c * 128: r * 1024 + (c + 1) * 128],
                        iden), "pe")
                    t_pe[gi] = p.cnt["pe"]
                p.T(tmm)

                def evac(vector, r=r, c=c, gi=gi, pst=pst):
                    p.wait_tok(vector, "pe", t_pe[gi])
                    p.cop(vector, "v", vector.tensor_copy(
                        out=dst[:, c * 1024 + r * 128: c * 1024 + (r + 1) * 128],
                        in_=pst))
                    t_v[gi] = p.cnt["v"]
                p.V(evac)
                gi += 1

        dout_dma(o_qwT[j].rearrange("(a p) c -> p a c", p=128),
                 dst[:, 0:8 * 1024].rearrange("p (a c) -> p a c", a=8))

        din_dma(src[0:32, 0:1024], i_la[j], wait=("v", "pe", "so"))
        din_dma(src[0:128, 2048:2048 + 256].rearrange("p (a c) -> p a c", a=8),
                i_lb[j].rearrange("(a p) c -> p a c", p=128), wait=())

        def lcast(vector):
            p.wait(vector, "sy")
            p.cop(vector, "v", vector.tensor_copy(out=srcb[0:32, 0:1024], in_=src[0:32, 0:1024]))
            p.cop(vector, "v", vector.tensor_copy(out=srcb[:, 2048:2048 + 256],
                                     in_=src[:, 2048:2048 + 256]))
        p.V(lcast)

        for c in range(8):
            def tmm(tensor, c=c):
                p.wait(tensor, "v")
                p.inc(tensor.transpose(psT[0:128, 0:32],
                                       srcb[0:32, c * 128:(c + 1) * 128],
                                       iden[0:32, 0:32]), "pe")
            p.T(tmm)

            def evac(vector, c=c):
                p.wait(vector, "pe")
                p.cop(vector, "v", vector.tensor_copy(
                    out=dst[:, 8192 + c * 32: 8192 + (c + 1) * 32],
                    in_=psT[0:128, 0:32]))
            p.V(evac)
        for r in range(8):
            def tmm(tensor, r=r):
                p.wait(tensor, "v")
                p.inc(tensor.transpose(psT[0:32, 0:128],
                                       srcb[:, 2048 + r * 32: 2048 + (r + 1) * 32],
                                       iden), "pe")
            p.T(tmm)

            def evac(vector, r=r):
                p.wait(vector, "pe")
                p.cop(vector, "v", vector.tensor_copy(
                    out=dst[0:32, 9216 + r * 128: 9216 + (r + 1) * 128],
                    in_=psT[0:32, 0:128]))
            p.V(evac)

        dout_dma(o_laT[j].rearrange("(a p) c -> p a c", p=128),
                 dst[:, 8192:8192 + 256].rearrange("p (a c) -> p a c", a=8))
        dout_dma(o_lbT[j], dst[0:32, 9216:9216 + 1024])

    slim = bool((io or {}).get("slim"))
    if not slim:
        transpose_job(i_inwf, 4096, D, o_inwTf)
        transpose_job(i_inwb, 4096, D, o_inwTb)
        transpose_job(i_xpf, 96, 2048, o_xpTf)
        transpose_job(i_xpb, 96, 2048, o_xpTb)
        transpose_job(i_dtf, 2048, 64, o_dtwTf)
        transpose_job(i_dtb_w, 2048, 64, o_dtwTb)
    transpose_job(i_gw, 8, D, o_gwT)

    din_dma(src[:, 0:16 * 1024].rearrange("p (a c) -> p a c", a=8),
            i_mw.rearrange("(a p) c -> p a c", p=128), wait=("v", "pe", "so"))

    def mcast(vector):
        p.wait(vector, "sy")
        p.cop(vector, "v", vector.tensor_copy(out=srcb[:, 0:16 * 1024], in_=src[:, 0:16 * 1024]))
    p.V(mcast)

    for h, out_sb in ((0, b1), (1, b2)):
        for r in range(8):
            for c in range(8):
                def tmm(tensor, h=h, r=r, c=c):
                    p.wait(tensor, "v")
                    p.inc(tensor.transpose(
                        psT,
                        srcb[:, r * 2048 + h * 1024 + c * 128:
                             r * 2048 + h * 1024 + (c + 1) * 128],
                        iden), "pe")
                p.T(tmm)

                def evac(vector, out_sb=out_sb, r=r, c=c):
                    p.wait(vector, "pe")
                    p.cop(vector, "v", vector.tensor_copy(
                        out=out_sb[:, c * 1024 + r * 128: c * 1024 + (r + 1) * 128],
                        in_=psT))
                p.V(evac)

    o_mfT = p.dout("mfT", [D, D], BF16)
    o_mbT = p.dout("mbT", [D, D], BF16)
    dout_dma(o_mfT.rearrange("(a p) c -> p a c", p=128),
             b1.rearrange("p (a c) -> p a c", a=8))
    dout_dma(o_mbT.rearrange("(a p) c -> p a c", p=128),
             b2.rearrange("p (a c) -> p a c", a=8))

    for i_ow, mT, o_comb in (() if slim else
                             ((i_outwf, b1, o_combTf), (i_outwb, b2, o_combTb))):
        din_dma(src[:, 0:16 * 1024].rearrange("p (a c) -> p a c", a=8),
                i_ow.rearrange("(a p) c -> p a c", p=128), wait=("v", "pe", "so"))

        def ocast(vector):
            p.wait(vector, "sy")
            p.cop(vector, "v", vector.tensor_copy(out=srcb[:, 0:16 * 1024],
                                     in_=src[:, 0:16 * 1024]))
        p.V(ocast)

        for ct in range(16):
            for half in range(2):
                pss = ps0 if half == 0 else ps1

                def tmm(tensor, mT=mT, ct=ct, half=half, pss=pss):
                    p.wait(tensor, "v")
                    for k in range(8):
                        ins = tensor.matmul(
                            pss,
                            srcb[:, k * 2048 + ct * 128: k * 2048 + (ct + 1) * 128],
                            mT[:, k * 1024 + half * 512: k * 1024 + half * 512 + 512],
                            start=(k == 0), stop=(k == 7))
                    p.inc(ins, "pe")
                p.T(tmm)

                def evac(vector, ct=ct, half=half, pss=pss):
                    p.wait(vector, "pe")
                    p.cop(vector, "v", vector.tensor_copy(
                        out=dst[:, ct * 1024 + half * 512: ct * 1024 + half * 512 + 512],
                        in_=pss))
                p.V(evac)

        dout_dma(o_comb.rearrange("(a p) c -> p a c", p=128),
                 dst[:, 0:16 * 1024].rearrange("p (a c) -> p a c", a=16))

    din_dma(src[0:1, 0:6 * D], i_bias.rearrange("a b -> (a b)").rearrange(
        "(o a) -> o a", o=1), wait=("v", "pe", "so"))
    din_dma(src[0:1, 8192:8192 + 2048], i_dtbf.rearrange("(o a) -> o a", o=1), wait=())
    din_dma(src[0:1, 12288:12288 + 2048], i_dtbb.rearrange("(o a) -> o a", o=1), wait=())

    def bcastrows(vector):
        p.wait(vector, "sy")
        p.cop(vector, "v", vector.tensor_copy(out=srcb[0:1, 0:6 * D], in_=src[0:1, 0:6 * D]))
        p.cop(vector, "v", vector.tensor_copy(out=srcb[0:1, 8192:8192 + 2048],
                                 in_=src[0:1, 8192:8192 + 2048]))
        p.cop(vector, "v", vector.tensor_copy(out=srcb[0:1, 12288:12288 + 2048],
                                 in_=src[0:1, 12288:12288 + 2048]))
    p.V(bcastrows)

    dout_dma(o_brow, srcb[0:1, 0:6 * D])
    dout_dma(o_dtbrf, srcb[0:1, 8192:8192 + 2048])
    dout_dma(o_dtbrb, srcb[0:1, 12288:12288 + 2048])

    def fin(gpsimd):
        p.wait(gpsimd, "so")
    p.G(fin)

    return p.done()


def prep_inputs(z):
    return {
        "WA": z["L0_attn_W_A"], "WB": z["L0_attn_W_B"], "WC": z["L0_attn_W_C"],
        "qw": z["L0_ffn_qweights"], "qs": z["L0_ffn_scales"],
        "la": z["L0_ffn_lora_a"], "lb": z["L0_ffn_lora_b"],
        "inwf": z["L1_attn_fwd_in_w"], "inwb": z["L1_attn_bwd_in_w"],
        "outwf": z["L1_attn_fwd_out_w"], "outwb": z["L1_attn_bwd_out_w"],
        "mw": z["L1_attn_merge_w"],
        "xpf": z["L1_attn_fwd_xproj_w"], "xpb": z["L1_attn_bwd_xproj_w"],
        "dtf": z["L1_attn_fwd_dt_w"], "dtb_w": z["L1_attn_bwd_dt_w"],
        "gw": z["L1_ffn_gate_w"],
        "bias": z["L0_ffn_biases"],
        "dtbf": z["L1_attn_fwd_dt_b"], "dtbb": z["L1_attn_bwd_dt_b"],
        "iden": np.eye(128, dtype=BF),
        "onescol": np.ones((128, 1), dtype=BF),
        "onescolf": np.ones((128, 1), dtype=np.float32),
        "onesrowf": np.ones((1, 128), dtype=np.float32),
    }


def run_prep(z):
    nc = build_prep()
    res = run_bass_kernel_spmd(nc, [prep_inputs(z)], core_ids=[0])
    return res.results[0]






def build_ffn(W, p=None, io=None):
    standalone = p is None
    if standalone:
        p = P()
    p.io = io or {}
    i_x = p.din("xT", [D, W])
    i_qwT = p.din("qwT", [6, D, D], BF16)
    i_laT = p.din("laT", [6, D, 32], BF16)
    i_lbT = p.din("lbT", [6, 32, D], BF16)
    i_brow = p.din("brow", [1, 6 * D], BF16)
    i_gam = p.din("gam", [128, 24])
    i_bet = p.din("bet", [128, 24])
    i_ocb = p.din("onescolb", [128, 1], BF16)
    i_orb = p.din("onesrowb", [1, 512], BF16)
    i_orf = p.din("onesrowf", [1, 128])
    o_x = p.dout("oT", [D, W])

    qw = p.sb("qw", [128, 6 * 8 * 1024], BF16)
    laT = p.sb("laT", [128, 6 * 8 * 32], BF16)
    lbT = p.sb("lbT", [32, 6 * 1024], BF16)
    brow = p.sb("brow", [1, 6 * D], BF16)
    gam = p.sb("gam", [128, 24])
    bet = p.sb("bet", [128, 24])
    ocb = p.sb("ocb", [128, 1], BF16)
    orb = p.sb("orb", [1, 512], BF16)
    orf = p.sb("orf", [1, 128])
    xc = p.sb("xc", [128, 8 * 512])
    h = p.sb("h", [128, 8 * 512])
    y1 = p.sb("y1", [128, 8 * 512])
    hb = p.sb("hb", [128, 8 * 512], BF16)
    gb = p.sb("gb", [128, 8 * 512], BF16)
    sqb = gb
    t1 = p.sb("t1", [128, 512])
    ub = p.sb("ub", [32, 512], BF16)
    rows = p.sb("rows", [1, 5 * 512])
    rowb = p.sb("rowb", [1, 2 * 512], BF16)
    orfb = p.sb("orfb", [1, 128], BF16)
    cst = p.sb("cst", [1, 8])
    psA = p.ps("psA", [128, 512])
    psA2 = p.ps("psA2", [128, 512])
    psU = p.ps("psU", [32, 512])
    psS1 = p.ps("psS1", [1, 512])
    psS2 = p.ps("psS2", [1, 512])
    psBr = p.ps("psBr", [128, 512])
    psBm = p.ps("psBm", [128, 512])

    def ldma(sync, dst_ap, src_ap, *wait):
        p.wait(sync, *wait)
        p.inc(sync.dma_start(out=dst_ap, in_=src_ap), "sy", 16)

    def const_loads(sync):
        for j in range(6):
            ldma(sync, qw[:, j * 8192:(j + 1) * 8192].rearrange("p (a c) -> p a c", a=8),
                 i_qwT[j].rearrange("(a p) c -> p a c", p=128))
            ldma(sync, laT[:, j * 256:(j + 1) * 256].rearrange("p (a c) -> p a c", a=8),
                 i_laT[j].rearrange("(a p) c -> p a c", p=128))
            ldma(sync, lbT[:, j * 1024:(j + 1) * 1024], i_lbT[j])
        ldma(sync, brow, i_brow)
        ldma(sync, gam, i_gam)
        ldma(sync, bet, i_bet)
        ldma(sync, ocb, i_ocb)
        ldma(sync, orb, i_orb)
        ldma(sync, orf, i_orf)
    p.SY(const_loads)

    def mkconst(vector):
        p.cop(vector, "v", vector.memset(cst[0:1, 0:1], EPS))
        p.cop(vector, "v", vector.memset(cst[0:1, 1:2], -0.5))
        p.cop(vector, "v", vector.memset(orfb, 1.0))
    p.V(mkconst)

    def layernorm_from(src_sb, n, out_f32, out_b16, gj=None):
        def c1(vector):
            p.wait(vector, "pe", "s")
            p.cop(vector, "v", vector.tensor_copy(out=hb[:, 0:8 * n], in_=src_sb[:, 0:8 * n]))
            p.cop(vector, "v", vector.tensor_mul(out=sqb[:, 0:8 * n], in0=hb[:, 0:8 * n],
                                                 in1=hb[:, 0:8 * n]))
        p.V(c1)

        def stats(tensor):
            p.wait(tensor, "v")
            for k in range(8):
                i0 = tensor.matmul(psS1[0:1, 0:n], ocb, hb[:, k * n: (k + 1) * n],
                                   start=(k == 0), stop=(k == 7))
                i1 = tensor.matmul(psS2[0:1, 0:n], ocb, sqb[:, k * n: (k + 1) * n],
                                   start=(k == 0), stop=(k == 7))
            p.inc(i0, "pe")
            p.inc(i1, "pe")
        p.T(stats)

        def rowmath(vector):
            p.wait(vector, "pe")
            p.cop(vector, "v", vector.tensor_scalar(out=rows[0:1, 0:n], in0=psS1[0:1, 0:n],
                                                    scalar1=float(1.0 / D), scalar2=None,
                                                    op0=ALU.mult))
            p.cop(vector, "v", vector.tensor_scalar(out=rows[0:1, 512:512 + n],
                                                    in0=psS2[0:1, 0:n],
                                                    scalar1=float(1.0 / D), scalar2=None,
                                                    op0=ALU.mult))
            p.cop(vector, "v", vector.tensor_mul(out=rows[0:1, 1024:1024 + n],
                                                 in0=rows[0:1, 0:n], in1=rows[0:1, 0:n]))
            p.cop(vector, "v", vector.tensor_sub(out=rows[0:1, 512:512 + n],
                                                 in0=rows[0:1, 512:512 + n],
                                                 in1=rows[0:1, 1024:1024 + n]))
        p.V(rowmath)

        def rowact(scalar):
            p.wait(scalar, "v")
            p.cop(scalar, "s", scalar.activation(out=rows[0:1, 1024:1024 + n],
                                                 in_=rows[0:1, 512:512 + n],
                                                 func=AF.Ln, bias=cst[0:1, 0:1]))
            p.cop(scalar, "s", scalar.activation(out=rows[0:1, 1536:1536 + n],
                                                 in_=rows[0:1, 1024:1024 + n],
                                                 func=AF.Exp, scale=cst[0:1, 1:2]))
        p.S(rowact)

        def rown(vector):
            p.wait(vector, "s")
            p.cop(vector, "v", vector.tensor_mul(out=rows[0:1, 1024:1024 + n],
                                                 in0=rows[0:1, 0:n],
                                                 in1=rows[0:1, 1536:1536 + n]))
            p.cop(vector, "v", vector.tensor_scalar(out=rows[0:1, 2048:2048 + n],
                                                    in0=rows[0:1, 1024:1024 + n],
                                                    scalar1=-1.0, scalar2=None,
                                                    op0=ALU.mult))
            p.cop(vector, "v", vector.tensor_copy(out=rowb[0:1, 0:n],
                                                  in_=rows[0:1, 1536:1536 + n]))
            p.cop(vector, "v", vector.tensor_copy(out=rowb[0:1, 512:512 + n],
                                                  in_=rows[0:1, 2048:2048 + n]))
        p.V(rown)

        def bcast(tensor):
            p.wait(tensor, "v")
            p.inc(tensor.matmul(psBr[:, 0:n], orfb[0:1, :], rowb[0:1, 0:n],
                                start=True, stop=True), "pe")
            p.inc(tensor.matmul(psBm[:, 0:n], orfb[0:1, :], rowb[0:1, 512:512 + n],
                                start=True, stop=True), "pe")
        p.T(bcast)

        def apply(vector):
            p.wait(vector, "pe")
            for dt_i in range(8):
                sl = slice(dt_i * n, (dt_i + 1) * n)
                p.cop(vector, "v", vector.tensor_mul(out=t1[:, 0:n], in0=src_sb[:, sl],
                                                     in1=psBr[:, 0:n]))
                if out_f32 is not None:
                    p.cop(vector, "v", vector.tensor_add(out=out_f32[:, sl],
                                                         in0=t1[:, 0:n],
                                                         in1=psBm[:, 0:n]))
                    p.cop(vector, "v", vector.tensor_copy(out=out_b16[:, sl],
                                                          in_=out_f32[:, sl]))
                else:
                    p.cop(vector, "v", vector.tensor_add(out=y1[:, sl],
                                                         in0=t1[:, 0:n],
                                                         in1=psBm[:, 0:n]))
                    def gl(scalar, dt_i=dt_i, sl=sl):
                        p.wait(scalar, "v")
                        p.cop(scalar, "s", scalar.activation(
                            out=gb[:, sl], in_=y1[:, sl], func=AF.Gelu,
                            scale=gam[:, gj * 8 + dt_i: gj * 8 + dt_i + 1],
                            bias=bet[:, gj * 8 + dt_i: gj * 8 + dt_i + 1]))
                    p.S(gl)
        p.V(apply)

    def qlora_mm(jw, rhs_b, n, resid_to=None):
        def umm(tensor):
            p.wait(tensor, "v", "s")
            for k in range(8):
                ins = tensor.matmul(psU[:, 0:n],
                                    laT[:, jw * 256 + k * 32:(jw * 256 + (k + 1) * 32)],
                                    rhs_b[:, k * n:(k + 1) * n],
                                    start=(k == 0), stop=(k == 7))
            p.inc(ins, "pe")
        p.T(umm)

        def uev(vector):
            p.wait(vector, "pe")
            p.cop(vector, "v", vector.tensor_copy(out=ub[:, 0:n], in_=psU[:, 0:n]))
        p.V(uev)

        ev_tok = {}
        mm_tok = {}
        psA2l = [psA, psA2]
        for ot in range(8):
            def main(tensor, ot=ot):
                if ot < 2:
                    p.wait(tensor, "v")
                else:
                    p.wait_tok(tensor, "v", ev_tok.get(ot - 2, 0))
                psAx = psA2l[ot % 2]
                tensor.matmul(psAx[:, 0:n],
                              brow[0:1, jw * 1024 + ot * 128: jw * 1024 + (ot + 1) * 128],
                              orb[0:1, 0:n], start=True, stop=False)
                for k in range(8):
                    tensor.matmul(psAx[:, 0:n],
                                  qw[:, jw * 8192 + k * 1024 + ot * 128:
                                     jw * 8192 + k * 1024 + (ot + 1) * 128],
                                  rhs_b[:, k * n:(k + 1) * n],
                                  start=False, stop=False)
                ins = tensor.matmul(psAx[:, 0:n],
                                    lbT[:, jw * 1024 + ot * 128: jw * 1024 + (ot + 1) * 128],
                                    ub[:, 0:n], start=False, stop=True)
                p.inc(ins, "pe")
                mm_tok[ot] = p.cnt["pe"]
            p.T(main)

            def ev(vector, ot=ot):
                p.wait_tok(vector, "pe", mm_tok[ot])
                psAx = psA2l[ot % 2]
                sl = slice(ot * n, (ot + 1) * n)
                if resid_to is None:
                    p.cop(vector, "v", vector.tensor_copy(out=y1[:, sl], in_=psAx[:, 0:n]))
                else:
                    p.cop(vector, "v", vector.tensor_add(out=resid_to[:, sl],
                                                         in0=psAx[:, 0:n],
                                                         in1=resid_to[:, sl]))
                ev_tok[ot] = p.cnt["v"]
            p.V(ev)

    for (c0, n) in chunks_of(W):
        def xin(sync, c0=c0, n=n):
            p.wait(sync, "v")
            ldma(sync, xc[:, 0:8 * n].rearrange("p (a c) -> p a c", a=8),
                 i_x.rearrange("(a p) c -> p a c", p=128)[:, :, c0:c0 + n])
        p.SY(xin)

        def wsy(vector):
            p.wait(vector, "sy")
            pass
        p.V(wsy)

        layernorm_from(xc, n, h, hb)

        for blk in range(3):
            if blk > 0:
                def cast(vector, n=n):
                    p.wait(vector, "pe", "s")
                    p.cop(vector, "v", vector.tensor_copy(out=hb[:, 0:8 * n],
                                                          in_=h[:, 0:8 * n]))
                p.V(cast)
            qlora_mm(2 * blk, hb, n, resid_to=None)
            layernorm_from(y1, n, None, None, gj=blk)
            qlora_mm(2 * blk + 1, gb, n, resid_to=h)

        def fin(vector, n=n):
            p.wait(vector, "pe", "s")
            for dt_i in range(8):
                sl = slice(dt_i * n, (dt_i + 1) * n)
                p.cop(vector, "v", vector.scalar_tensor_tensor(
                    out=y1[:, sl], in0=h[:, sl], scalar=0.5, in1=xc[:, sl],
                    op0=ALU.mult, op1=ALU.add))
        p.V(fin)

        def xout(sync, c0=c0, n=n):
            p.wait(sync, "v")
            p.inc(sync.dma_start(
                out=o_x.rearrange("(a p) c -> p a c", p=128)[:, :, c0:c0 + n],
                in_=y1[:, 0:8 * n].rearrange("p (a c) -> p a c", a=8)), "so", 16)
        p.SY(xout)

    def g(gpsimd):
        p.wait(gpsimd, "so")
    p.G(g)
    p.io = None
    if standalone:
        return p.done()
    return None


def pcol(v, ntile):
    v = np.asarray(v, np.float32)
    return np.ascontiguousarray(v.reshape(ntile, 128).T)


FFN_CONST = None


def as_bf(a):
    a = np.asarray(a)
    if a.dtype == BF:
        return a
    return a.view(BF) if a.dtype.itemsize == 2 else a.astype(BF)


def ffn_consts(z, prep):
    return {
        "qwT": as_bf(prep["qwT"]),
        "laT": as_bf(prep["laT"]),
        "lbT": as_bf(prep["lbT"]),
        "brow": as_bf(prep["brow"]),
        "gam": np.concatenate([pcol(z["L0_ffn_ln_gamma"][b], 8) for b in range(3)], 1),
        "bet": np.concatenate([pcol(z["L0_ffn_ln_beta"][b], 8) for b in range(3)], 1),
        "onescolb": np.ones((128, 1), dtype=BF),
        "onesrowb": np.ones((1, 512), dtype=BF),
        "onesrowf": np.ones((1, 128), dtype=np.float32),
    }


def run_ffn(xT_slices, consts, W):
    nc = build_ffn(W)
    in_maps = [dict(consts, xT=np.ascontiguousarray(s)) for s in xT_slices]
    res = run_bass_kernel_spmd(nc, in_maps, core_ids=list(range(NCORE)))
    return [r["oT"] for r in res.results]


def x_slices_halo(xT_full):
    out = []
    for c in range(NCORE):
        b, q = divmod(c, 4)
        g0 = b * 8192 + q * TCORE
        sl = np.zeros((D, W1), np.float32)
        lo = g0 - HALO
        seq_lo = b * 8192
        pad = max(0, seq_lo - lo)
        sl[:, pad:] = xT_full[:, lo + pad: g0 + TCORE]
        out.append(sl)
    return out



PAD = 16
WB_ = PAD + W1


def build_attn0(p=None, io=None):
    standalone = p is None
    if standalone:
        p = P()
    p.io = io or {}
    i_x = p.din("xT", [D, W1])
    i_bwT = p.din("bwT", [D, D], BF16)
    i_cT = p.din("cT", [D, D], BF16)
    i_bpow = p.din("bpow", [5, D, D], BF16)
    i_umask = p.din("umask", [128, HALO])
    i_ocb = p.din("onescolb", [128, 1], BF16)
    i_orf = p.din("onesrowf", [1, 128])
    o_x = p.dout("oT", [D, TCORE])

    xf = p.sb("xf", [128, 8 * W1])
    hbuf = p.sb("hbuf", [128, 8 * WB_], BF16)
    wA = p.sb("wA", [128, 8 * 1024], BF16)
    xnb = p.sb("xnb", [128, 8 * 512], BF16)
    sqb = p.sb("sqb", [128, 8 * 512], BF16)
    umb = p.sb("umb", [128, HALO], BF16)
    umf = p.sb("umf", [128, HALO])
    ocb = p.sb("ocb", [128, 1], BF16)
    orf = p.sb("orf", [1, 128])
    t1 = p.sb("t1", [128, 512])
    outc = p.sb("outc", [128, 8 * 512])
    rows = p.sb("rows", [1, 5 * 512])
    rowb = p.sb("rowb", [1, 2 * 512], BF16)
    orfb = p.sb("orfb", [1, 128], BF16)
    cst = p.sb("cst", [1, 8])
    psD = [p.ps(f"psD{i}", [128, 512]) for i in range(8)]
    psS1 = psD[2][0:1, 0:512]
    psS2 = psD[3][0:1, 0:512]

    def ldma(sync, dst_ap, src_ap, *wait):
        p.wait(sync, *wait)
        p.inc(sync.dma_start(out=dst_ap, in_=src_ap), "sy", 16)

    def loads(sync):
        ldma(sync, xf.rearrange("p (a c) -> p a c", a=8),
             i_x.rearrange("(a p) c -> p a c", p=128))
        ldma(sync, umf, i_umask)
        ldma(sync, ocb, i_ocb)
        ldma(sync, orf, i_orf)
        ldma(sync, wA.rearrange("p (a c) -> p a c", a=8),
             i_bwT.rearrange("(a p) c -> p a c", p=128))
    p.SY(loads)

    def mkconst(vector):
        p.wait(vector, "sy")
        p.cop(vector, "v", vector.memset(cst[0:1, 0:1], EPS))
        p.cop(vector, "v", vector.memset(cst[0:1, 1:2], -0.5))
        p.cop(vector, "v", vector.memset(orfb, 1.0))
        p.cop(vector, "v", vector.tensor_copy(out=umb, in_=umf))
        for dt_i in range(8):
            p.cop(vector, "v", vector.memset(hbuf[:, dt_i * WB_: dt_i * WB_ + PAD], 0.0))
    p.V(mkconst)

    for (c0, n) in chunks_of(W1):
        def c1(vector, c0=c0, n=n):
            p.wait(vector, "pe", "s")
            for dt_i in range(8):
                p.cop(vector, "v", vector.tensor_copy(
                    out=xnb[:, dt_i * n:(dt_i + 1) * n],
                    in_=xf[:, dt_i * W1 + c0: dt_i * W1 + c0 + n]))
            p.cop(vector, "v", vector.tensor_mul(out=sqb[:, 0:8 * n],
                                                 in0=xnb[:, 0:8 * n],
                                                 in1=xnb[:, 0:8 * n]))
        p.V(c1)

        def stats(tensor, n=n):
            p.wait(tensor, "v")
            for k in range(8):
                i0 = tensor.matmul(psS1[0:1, 0:n], ocb, xnb[:, k * n:(k + 1) * n],
                                   start=(k == 0), stop=(k == 7))
                i1 = tensor.matmul(psS2[0:1, 0:n], ocb, sqb[:, k * n:(k + 1) * n],
                                   start=(k == 0), stop=(k == 7))
            p.inc(i0, "pe")
            p.inc(i1, "pe")
        p.T(stats)

        def rowmath(vector, n=n):
            p.wait(vector, "pe")
            p.cop(vector, "v", vector.tensor_scalar(out=rows[0:1, 0:n], in0=psS1[0:1, 0:n],
                                                    scalar1=float(1.0 / D), scalar2=None,
                                                    op0=ALU.mult))
            p.cop(vector, "v", vector.tensor_scalar(out=rows[0:1, 512:512 + n],
                                                    in0=psS2[0:1, 0:n],
                                                    scalar1=float(1.0 / D), scalar2=None,
                                                    op0=ALU.mult))
            p.cop(vector, "v", vector.tensor_mul(out=rows[0:1, 1024:1024 + n],
                                                 in0=rows[0:1, 0:n], in1=rows[0:1, 0:n]))
            p.cop(vector, "v", vector.tensor_sub(out=rows[0:1, 512:512 + n],
                                                 in0=rows[0:1, 512:512 + n],
                                                 in1=rows[0:1, 1024:1024 + n]))
        p.V(rowmath)

        def rowact(scalar, n=n):
            p.wait(scalar, "v")
            p.cop(scalar, "s", scalar.activation(out=rows[0:1, 1024:1024 + n],
                                                 in_=rows[0:1, 512:512 + n],
                                                 func=AF.Ln, bias=cst[0:1, 0:1]))
            p.cop(scalar, "s", scalar.activation(out=rows[0:1, 1536:1536 + n],
                                                 in_=rows[0:1, 1024:1024 + n],
                                                 func=AF.Exp, scale=cst[0:1, 1:2]))
        p.S(rowact)

        def rown(vector, n=n):
            p.wait(vector, "s")
            p.cop(vector, "v", vector.tensor_mul(out=rows[0:1, 1024:1024 + n],
                                                 in0=rows[0:1, 0:n],
                                                 in1=rows[0:1, 1536:1536 + n]))
            p.cop(vector, "v", vector.tensor_scalar(out=rows[0:1, 2048:2048 + n],
                                                    in0=rows[0:1, 1024:1024 + n],
                                                    scalar1=-1.0, scalar2=None,
                                                    op0=ALU.mult))
            p.cop(vector, "v", vector.tensor_copy(out=rowb[0:1, 0:n],
                                                  in_=rows[0:1, 1536:1536 + n]))
            p.cop(vector, "v", vector.tensor_copy(out=rowb[0:1, 512:512 + n],
                                                  in_=rows[0:1, 2048:2048 + n]))
        p.V(rown)

        def bcast(tensor, n=n):
            p.wait(tensor, "v")
            p.inc(tensor.matmul(psD[0][:, 0:n], orfb[0:1, :], rowb[0:1, 0:n],
                                start=True, stop=True), "pe")
            p.inc(tensor.matmul(psD[1][:, 0:n], orfb[0:1, :], rowb[0:1, 512:512 + n],
                                start=True, stop=True), "pe")
        p.T(bcast)

        def apply(vector, c0=c0, n=n):
            p.wait(vector, "pe")
            for dt_i in range(8):
                p.cop(vector, "v", vector.tensor_mul(
                    out=t1[:, 0:n], in0=xf[:, dt_i * W1 + c0: dt_i * W1 + c0 + n],
                    in1=psD[0][:, 0:n]))
                p.cop(vector, "v", vector.tensor_add(
                    out=xnb[:, dt_i * n:(dt_i + 1) * n], in0=t1[:, 0:n],
                    in1=psD[1][:, 0:n]))
        p.V(apply)

        def umm(tensor, n=n):
            p.wait(tensor, "v")
            for ot in range(8):
                for k in range(8):
                    ins = tensor.matmul(
                        psD[ot][:, 0:n],
                        wA[:, k * 1024 + ot * 128: k * 1024 + (ot + 1) * 128],
                        xnb[:, k * n:(k + 1) * n], start=(k == 0), stop=(k == 7))
                p.inc(ins, "pe")
        p.T(umm)

        def uev(vector, c0=c0, n=n):
            p.wait(vector, "pe")
            for ot in range(8):
                p.cop(vector, "v", vector.tensor_copy(
                    out=hbuf[:, ot * WB_ + PAD + c0: ot * WB_ + PAD + c0 + n],
                    in_=psD[ot][:, 0:n]))
        p.V(uev)

    def mask(vector):
        p.wait(vector, "pe")
        for dt_i in range(8):
            p.cop(vector, "v", vector.tensor_mul(
                out=hbuf[:, dt_i * WB_ + PAD: dt_i * WB_ + PAD + HALO],
                in0=hbuf[:, dt_i * WB_ + PAD: dt_i * WB_ + PAD + HALO],
                in1=umb))
    p.V(mask)

    for lvl in range(5):
        sh = 1 << lvl
        def wload(sync, lvl=lvl):
            p.wait(sync, "pe", "v")
            ldma(sync, wA.rearrange("p (a c) -> p a c", a=8),
                 i_bpow[lvl].rearrange("(a p) c -> p a c", p=128))
        p.SY(wload)

        def wsy(tensor):
            p.wait(tensor, "sy")
        p.T(wsy)

        for (c0, n) in reversed(chunks_of(W1)):
            def mms(tensor, c0=c0, n=n, sh=sh):
                p.wait(tensor, "v")
                for ot in range(8):
                    for k in range(8):
                        ins = tensor.matmul(
                            psD[ot][:, 0:n],
                            wA[:, k * 1024 + ot * 128: k * 1024 + (ot + 1) * 128],
                            hbuf[:, k * WB_ + PAD + c0 - sh: k * WB_ + PAD + c0 - sh + n],
                            start=(k == 0), stop=(k == 7))
                    p.inc(ins, "pe")
            p.T(mms)

            def evac(vector, c0=c0, n=n):
                p.wait(vector, "pe")
                for ot in range(8):
                    sl = slice(ot * WB_ + PAD + c0, ot * WB_ + PAD + c0 + n)
                    p.cop(vector, "v", vector.tensor_add(
                        out=hbuf[:, sl], in0=psD[ot][:, 0:n], in1=hbuf[:, sl]))
            p.V(evac)

    def cload(sync):
        p.wait(sync, "pe", "v")
        ldma(sync, wA.rearrange("p (a c) -> p a c", a=8),
             i_cT.rearrange("(a p) c -> p a c", p=128))
    p.SY(cload)

    def wsy2(tensor):
        p.wait(tensor, "sy")
    p.T(wsy2)

    for (c0, n) in chunks_of(TCORE):
        def ymm(tensor, c0=c0, n=n):
            p.wait(tensor, "v")
            for ot in range(8):
                for k in range(8):
                    ins = tensor.matmul(
                        psD[ot][:, 0:n],
                        wA[:, k * 1024 + ot * 128: k * 1024 + (ot + 1) * 128],
                        hbuf[:, k * WB_ + PAD + HALO + c0: k * WB_ + PAD + HALO + c0 + n],
                        start=(k == 0), stop=(k == 7))
                p.inc(ins, "pe")
        p.T(ymm)

        def yev(vector, c0=c0, n=n):
            p.wait(vector, "pe")
            for ot in range(8):
                p.cop(vector, "v", vector.tensor_add(
                    out=outc[:, ot * n:(ot + 1) * n], in0=psD[ot][:, 0:n],
                    in1=xf[:, ot * W1 + HALO + c0: ot * W1 + HALO + c0 + n]))
        p.V(yev)

        def xout(sync, c0=c0, n=n):
            p.wait(sync, "v")
            p.inc(sync.dma_start(
                out=o_x.rearrange("(a p) c -> p a c", p=128)[:, :, c0:c0 + n],
                in_=outc[:, 0:8 * n].rearrange("p (a c) -> p a c", a=8)), "so", 16)
        p.SY(xout)

    def g(gpsimd):
        p.wait(gpsimd, "so")
    p.G(g)
    p.io = None
    if standalone:
        return p.done()
    return None


def attn0_consts(prep):
    return {
        "bwT": as_bf(prep["bwT"]),
        "cT": as_bf(prep["cT"]),
        "bpow": as_bf(prep["bpow"]),
        "onescolb": np.ones((128, 1), dtype=BF),
        "onesrowf": np.ones((1, 128), dtype=np.float32),
    }


def run_attn0(x1_slices, consts):
    nc = build_attn0()
    in_maps = []
    for c in range(NCORE):
        q = c % 4
        um = np.zeros((128, HALO), np.float32) if q == 0 else np.ones((128, HALO), np.float32)
        in_maps.append(dict(consts, xT=np.ascontiguousarray(x1_slices[c]), umask=um))
    res = run_bass_kernel_spmd(nc, in_maps, core_ids=list(range(NCORE)))
    return [r["oT"] for r in res.results]



def build_moe(mode, p=None, io=None):
    W = TCORE
    standalone = p is None
    if standalone:
        p = P()
    p.io = io or {}
    i_x = p.din("xT", [D, W])
    i_w1 = p.din("w1", [8, D, 2816])
    i_w2 = p.din("w2", [8, 1408, D])
    i_gwT = p.din("gwT", [D, 8], BF16)
    i_ocb = p.din("onescolb", [128, 1], BF16)
    i_orf = p.din("onesrowf", [1, 128])
    i_idf = p.din("idenf", [128, 128])
    o_x = p.dout("oT", [D, W])
    if mode == "first":
        i_lng = p.din("lng", [128, 8])
        i_lnb = p.din("lnb", [128, 8])
        o_xno = p.dout("xnoT", [D, W])
        o_v = p.dout("vT", [D, W], BF16)
    else:
        single_part = io is not None and "part1" in io
        i_parts = p.din("part1", [D, W]) if single_part else p.din("parts", [4, D, W])
        i_xno = p.din("xnoT", [D, W])
        i_mb = p.din("mergeb", [128, 8])
        xs = p.nc.dram_tensor("xs_scratch", [D, W], F32, kind="Internal").ap()

    xnb = p.sb("xnb", [128, 8 * W], BF16)
    sact = p.sb("sact", [128, 11 * W], BF16)
    outacc = p.sb("outacc", [128, 8 * W])
    w1g = p.sb("w1g", [128, 8 * 128], BF16)
    w1u = p.sb("w1u", [128, 8 * 128], BF16)
    w2b = p.sb("w2b", [128, 11 * 128], BF16)
    sqs = sact[:, 0:8 * 512]
    gwT = p.sb("gwT", [128, 8 * 8], BF16)
    lgt = p.sb("lgt", [8, W])
    cmb = lgt
    crow = p.sb("crow", [1, 512])
    tl = p.sb("tl", [128, 8])
    tr = p.sb("trr", [128, 16])
    xc = p.sb("xc", [128, 8 * 512])
    sg = p.sb("sg", [128, 512])
    t1 = p.sb("t1", [128, 512])
    xnb2 = sact[:, 8 * 512:16 * 512]
    rows = p.sb("rows", [1, 5 * 512])
    rowb = p.sb("rowb", [1, 2 * 512], BF16)
    orfb = p.sb("orfb", [1, 128], BF16)
    cst = p.sb("cst", [1, 8])
    ocb = p.sb("ocb", [128, 1], BF16)
    orf = p.sb("orf", [1, 128])
    idf = p.sb("idf", [128, 128])
    if mode == "first":
        lng = p.sb("lng", [128, 8])
        lnb = p.sb("lnb", [128, 8])
    else:
        mb = p.sb("mb", [128, 8])
    psA = p.ps("psA", [128, 512])
    psB2 = p.ps("psB2", [128, 512])
    psC = p.ps("psC", [128, 512])
    psO = p.ps("psO", [128, 512])
    psBr = p.ps("psBr", [128, 512])
    psBm = p.ps("psBm", [128, 512])
    psS1 = p.ps("psS1", [1, 512])
    psS2 = p.ps("psS2", [1, 512])
    psL = psA[0:8, 0:512]
    psR = psB2[0:128, 0:8]
    psR2 = psC[0:8, 0:128]

    def ldma(sync, dst_ap, src_ap, *wait):
        p.wait(sync, *wait)
        p.inc(sync.dma_start(out=dst_ap, in_=src_ap), "sy", 16)

    def loads(sync):
        ldma(sync, gwT.rearrange("p (a c) -> p a c", a=8),
             i_gwT.rearrange("(a p) c -> p a c", p=128))
        ldma(sync, ocb, i_ocb)
        ldma(sync, orf, i_orf)
        ldma(sync, idf, i_idf)
        if mode == "first":
            ldma(sync, lng, i_lng)
            ldma(sync, lnb, i_lnb)
        else:
            ldma(sync, mb, i_mb)
    p.SY(loads)

    def mkconst(vector):
        p.wait(vector, "sy")
        p.cop(vector, "v", vector.memset(cst[0:1, 0:1], EPS))
        p.cop(vector, "v", vector.memset(cst[0:1, 1:2], -0.5))
        p.cop(vector, "v", vector.memset(orfb, 1.0))
    p.V(mkconst)

    xsrc = i_x
    if mode == "second":
        xsrc = xs
        sactf = sact.bitcast(F32)
        for (c0, n) in chunks_of(W):
            def pin(sync, c0=c0, n=n):
                p.wait(sync, "v", "cc")
                ldma(sync, xc[:, 0:8 * n].rearrange("p (a c) -> p a c", a=8),
                     i_x.rearrange("(a p) c -> p a c", p=128)[:, :, c0:c0 + n])
                ldma(sync, sactf[:, 0:8 * n].rearrange("p (a c) -> p a c", a=8),
                     i_xno.rearrange("(a p) c -> p a c", p=128)[:, :, c0:c0 + n])
                if single_part:
                    ldma(sync, outacc[:, 0:8 * n].rearrange("p (a c) -> p a c", a=8),
                         i_parts.rearrange("(a p) c -> p a c", p=128)[:, :, c0:c0 + n])
                else:
                    for q in range(4):
                        ldma(sync, outacc[:, q * 8 * n:(q + 1) * 8 * n].rearrange(
                            "p (a c) -> p a c", a=8),
                            i_parts[q].rearrange("(a p) c -> p a c", p=128)[:, :, c0:c0 + n])
            p.SY(pin)

            def comb(vector, c0=c0, n=n):
                p.wait(vector, "sy")
                for dt_i in range(8):
                    sl = slice(dt_i * n, (dt_i + 1) * n)
                    a0 = outacc[:, 0 * 8 * n + dt_i * n: 0 * 8 * n + (dt_i + 1) * n]
                    if single_part:
                        p.cop(vector, "v", vector.tensor_add(out=t1, in0=a0,
                                                             in1=sactf[:, sl]))
                    else:
                        a1 = outacc[:, 1 * 8 * n + dt_i * n: 1 * 8 * n + (dt_i + 1) * n]
                        a2 = outacc[:, 2 * 8 * n + dt_i * n: 2 * 8 * n + (dt_i + 1) * n]
                        a3 = outacc[:, 3 * 8 * n + dt_i * n: 3 * 8 * n + (dt_i + 1) * n]
                        p.cop(vector, "v", vector.tensor_add(out=t1, in0=a0, in1=a1))
                        p.cop(vector, "v", vector.tensor_add(out=t1, in0=t1, in1=a2))
                        p.cop(vector, "v", vector.tensor_add(out=t1, in0=t1, in1=a3))
                        p.cop(vector, "v", vector.tensor_add(out=t1, in0=t1,
                                                             in1=sactf[:, sl]))
                    p.cop(vector, "v", vector.tensor_add(out=t1, in0=t1, in1=xc[:, sl]))
                    p.cop(vector, "v", vector.tensor_scalar(
                        out=xc[:, sl], in0=t1, scalar1=mb[:, dt_i:dt_i + 1],
                        scalar2=None, op0=ALU.add))
            p.V(comb)

            def pout(sync, c0=c0, n=n):
                p.wait(sync, "v")
                p.inc(sync.dma_start(
                    out=xs.rearrange("(a p) c -> p a c", p=128)[:, :, c0:c0 + n],
                    in_=xc[:, 0:8 * n].rearrange("p (a c) -> p a c", a=8)), "so", 16)
            p.SY(pout)

        def psync(vector):
            p.wait(vector, "so")
        p.V(psync)

    def emit_ln_chunk(src_getter, c0, n, out_sb, out_off, gain=None, extra_cast=None):
        def c1(vector):
            p.wait(vector, "pe", "s", "sy")
            for dt_i in range(8):
                p.cop(vector, "v", vector.tensor_copy(
                    out=xnb2[:, dt_i * n:(dt_i + 1) * n], in_=src_getter(dt_i)))
            p.cop(vector, "v", vector.tensor_mul(out=sqs[:, 0:8 * n],
                                                 in0=xnb2[:, 0:8 * n],
                                                 in1=xnb2[:, 0:8 * n]))
        p.V(c1)

        def stats(tensor):
            p.wait(tensor, "v")
            for k in range(8):
                i0 = tensor.matmul(psS1[0:1, 0:n], ocb, xnb2[:, k * n:(k + 1) * n],
                                   start=(k == 0), stop=(k == 7))
                i1 = tensor.matmul(psS2[0:1, 0:n], ocb, sqs[:, k * n:(k + 1) * n],
                                   start=(k == 0), stop=(k == 7))
            p.inc(i0, "pe")
            p.inc(i1, "pe")
        p.T(stats)

        def rowmath(vector):
            p.wait(vector, "pe")
            p.cop(vector, "v", vector.tensor_scalar(out=rows[0:1, 0:n],
                                                    in0=psS1[0:1, 0:n],
                                                    scalar1=float(1.0 / D),
                                                    scalar2=None, op0=ALU.mult))
            p.cop(vector, "v", vector.tensor_scalar(out=rows[0:1, 512:512 + n],
                                                    in0=psS2[0:1, 0:n],
                                                    scalar1=float(1.0 / D),
                                                    scalar2=None, op0=ALU.mult))
            p.cop(vector, "v", vector.tensor_mul(out=rows[0:1, 1024:1024 + n],
                                                 in0=rows[0:1, 0:n],
                                                 in1=rows[0:1, 0:n]))
            p.cop(vector, "v", vector.tensor_sub(out=rows[0:1, 512:512 + n],
                                                 in0=rows[0:1, 512:512 + n],
                                                 in1=rows[0:1, 1024:1024 + n]))
        p.V(rowmath)

        def rowact(scalar):
            p.wait(scalar, "v")
            p.cop(scalar, "s", scalar.activation(out=rows[0:1, 1024:1024 + n],
                                                 in_=rows[0:1, 512:512 + n],
                                                 func=AF.Ln, bias=cst[0:1, 0:1]))
            p.cop(scalar, "s", scalar.activation(out=rows[0:1, 1536:1536 + n],
                                                 in_=rows[0:1, 1024:1024 + n],
                                                 func=AF.Exp, scale=cst[0:1, 1:2]))
        p.S(rowact)

        def rown(vector):
            p.wait(vector, "s")
            p.cop(vector, "v", vector.tensor_mul(out=rows[0:1, 1024:1024 + n],
                                                 in0=rows[0:1, 0:n],
                                                 in1=rows[0:1, 1536:1536 + n]))
            p.cop(vector, "v", vector.tensor_scalar(out=rows[0:1, 2048:2048 + n],
                                                    in0=rows[0:1, 1024:1024 + n],
                                                    scalar1=-1.0, scalar2=None,
                                                    op0=ALU.mult))
            p.cop(vector, "v", vector.tensor_copy(out=rowb[0:1, 0:n],
                                                  in_=rows[0:1, 1536:1536 + n]))
            p.cop(vector, "v", vector.tensor_copy(out=rowb[0:1, 512:512 + n],
                                                  in_=rows[0:1, 2048:2048 + n]))
        p.V(rown)

        def bcast(tensor):
            p.wait(tensor, "v")
            p.inc(tensor.matmul(psBr[:, 0:n], orfb[0:1, :], rowb[0:1, 0:n],
                                start=True, stop=True), "pe")
            p.inc(tensor.matmul(psBm[:, 0:n], orfb[0:1, :], rowb[0:1, 512:512 + n],
                                start=True, stop=True), "pe")
        p.T(bcast)

        def apply(vector):
            p.wait(vector, "pe")
            for dt_i in range(8):
                p.cop(vector, "v", vector.tensor_mul(out=t1[:, 0:n],
                                                     in0=src_getter(dt_i),
                                                     in1=psBr[:, 0:n]))
                if gain is None:
                    p.cop(vector, "v", vector.tensor_add(
                        out=out_sb[:, out_off + dt_i * (W if out_sb is xnb else n):
                                   out_off + dt_i * (W if out_sb is xnb else n) + n],
                        in0=t1[:, 0:n], in1=psBm[:, 0:n]))
                else:
                    p.cop(vector, "v", vector.tensor_add(out=xc[:, dt_i * n:
                                                             (dt_i + 1) * n],
                                                         in0=t1[:, 0:n],
                                                         in1=psBm[:, 0:n]))
                    p.cop(vector, "v", vector.tensor_scalar(
                        out=out_sb[:, out_off + dt_i * n: out_off + dt_i * n + n],
                        in0=xc[:, dt_i * n:(dt_i + 1) * n],
                        scalar1=gain[0][:, dt_i:dt_i + 1],
                        scalar2=gain[1][:, dt_i:dt_i + 1],
                        op0=ALU.mult, op1=ALU.add))
        p.V(apply)

    for (c0, n) in chunks_of(W):
        def xin(sync, c0=c0, n=n):
            p.wait(sync, "v")
            ldma(sync, xc[:, 0:8 * n].rearrange("p (a c) -> p a c", a=8),
                 xsrc.rearrange("(a p) c -> p a c", p=128)[:, :, c0:c0 + n])
        p.SY(xin)

        emit_ln_chunk(lambda dt_i, n=n: xc[:, dt_i * n:(dt_i + 1) * n],
                      c0, n, xnb, c0)

        def gmm(tensor, c0=c0, n=n):
            p.wait(tensor, "v")
            for k in range(8):
                ins = tensor.matmul(psL[:, 0:n], gwT[:, k * 8:(k + 1) * 8],
                                    xnb[:, k * W + c0: k * W + c0 + n],
                                    start=(k == 0), stop=(k == 7))
            p.inc(ins, "pe")
        p.T(gmm)

        def gev(vector, c0=c0, n=n):
            p.wait(vector, "pe")
            p.cop(vector, "v", vector.tensor_copy(out=lgt[0:8, c0:c0 + n],
                                                  in_=psL[:, 0:n]))
        p.V(gev)

    for g0 in range(0, W, 128):
        def rt(tensor, g0=g0):
            p.wait(tensor, "v")
            p.inc(tensor.transpose(psR, lgt[0:8, g0:g0 + 128], idf[0:8, 0:8]), "pe")
        p.T(rt)

        def rmath(vector, g0=g0):
            p.wait(vector, "pe")
            p.cop(vector, "v", vector.tensor_copy(out=tl, in_=psR))
            p.cop(vector, "v", vector.reduce_max(out=tr[:, 0:1], in_=tl, axis=AX))
            p.cop(vector, "v", vector.tensor_single_scalar(
                out=tr[:, 8:16], in_=tl, scalar=tr[:, 0:1], op=ALU.is_equal))
            p.cop(vector, "v", vector.scalar_tensor_tensor(
                out=tr[:, 8:16], in0=tr[:, 8:16], scalar=-1e30, in1=tl,
                op0=ALU.mult, op1=ALU.add))
            p.cop(vector, "v", vector.reduce_max(out=tr[:, 1:2], in_=tr[:, 8:16],
                                                 axis=AX))
            p.cop(vector, "v", vector.tensor_sub(out=tr[:, 2:3], in0=tr[:, 0:1],
                                                 in1=tr[:, 1:2]))
        p.V(rmath)

        def rsig(scalar, g0=g0):
            p.wait(scalar, "v")
            p.cop(scalar, "s", scalar.activation(out=tr[:, 3:4], in_=tr[:, 2:3],
                                                 func=AF.Sigmoid))
        p.S(rsig)

        def rfin(vector, g0=g0):
            p.wait(vector, "s")
            p.cop(vector, "v", vector.tensor_scalar(out=tr[:, 4:5], in0=tr[:, 3:4],
                                                    scalar1=-1.0, scalar2=1.0,
                                                    op0=ALU.mult, op1=ALU.add))
            p.cop(vector, "v", vector.tensor_single_scalar(
                out=tr[:, 8:16], in_=tl, scalar=tr[:, 0:1], op=ALU.is_equal))
            p.cop(vector, "v", vector.tensor_scalar(out=tr[:, 8:16], in0=tr[:, 8:16],
                                                    scalar1=tr[:, 3:4], scalar2=None,
                                                    op0=ALU.mult))
            p.cop(vector, "v", vector.tensor_single_scalar(
                out=tl, in_=tl, scalar=tr[:, 1:2], op=ALU.is_equal))
            p.cop(vector, "v", vector.scalar_tensor_tensor(
                out=tl, in0=tl, scalar=tr[:, 4:5], in1=tr[:, 8:16],
                op0=ALU.mult, op1=ALU.add))
        p.V(rfin)

        def rt2(tensor, g0=g0):
            p.wait(tensor, "v")
            p.inc(tensor.transpose(psR2, tl, idf), "pe")
        p.T(rt2)

        def rev2(vector, g0=g0):
            p.wait(vector, "pe")
            p.cop(vector, "v", vector.tensor_copy(out=cmb[0:8, g0:g0 + 128],
                                                  in_=psR2))
        p.V(rev2)

    w1g2 = [w1g, p.sb("w1gB", [128, 8 * 128], BF16)]
    w1u2 = [w1u, p.sb("w1uB", [128, 8 * 128], BF16)]
    sg2 = [sg, p.sb("sgB", [128, 512])]
    t1b2 = [t1, p.sb("t1B", [128, 512])]
    crow2 = [crow, p.sb("crowB", [1, 512])]
    w2b2 = [w2b, p.sb("w2bB", [128, 11 * 128], BF16)]
    gA = [psA, psC]
    uA = [psB2, psO]
    cA = [psBr, psBm]
    g_tok = {}
    pe_tok = {}
    s_tok = {}
    v_tok = {}
    sy_tok = {}
    it = 0
    boundary_v = [p.cnt["v"]]
    for e in range(8):
        for jp in range(11):
            grp = it // 4
            wbk = grp % 2

            def w1load(gpsimd, e=e, jp=jp, wbk=wbk, it=it, grp=grp):
                p.wait_tok(gpsimd, "pe", pe_tok.get((grp - 2) * 4 + 3, 0))
                p.inc(gpsimd.dma_start(
                    out=w1g2[wbk].rearrange("p (a c) -> p a c", a=8),
                    in_=i_w1[e].rearrange("(a p) c -> p a c", p=128)[
                        :, :, jp * 128:(jp + 1) * 128]), "g", 16)
                p.inc(gpsimd.dma_start(
                    out=w1u2[wbk].rearrange("p (a c) -> p a c", a=8),
                    in_=i_w1[e].rearrange("(a p) c -> p a c", p=128)[
                        :, :, (11 + jp) * 128:(12 + jp) * 128]), "g", 16)
                g_tok[it] = p.cnt["g"]
            p.G(w1load)

            for (c0, n) in chunks_of(W):
                bk = it % 2

                def crowd(sync, e=e, c0=c0, n=n, bk=bk, it=it):
                    p.wait_tok(sync, "pe", pe_tok.get(it - 2, 0))
                    p.inc(sync.dma_start(out=crow2[bk][0:1, 0:n],
                                         in_=cmb[e:e + 1, c0:c0 + n]), "sy", 16)
                    sy_tok[it] = p.cnt["sy"]
                p.SY(crowd)

                def fmm(tensor, e=e, jp=jp, c0=c0, n=n, bk=bk, wbk=wbk, it=it):
                    p.wait_tok(tensor, "g", g_tok[(it // 4) * 4])
                    p.wait_tok(tensor, "sy", sy_tok[it])
                    p.wait_tok(tensor, "s", s_tok.get(it - 2, 0))
                    p.wait_tok(tensor, "v", max(v_tok.get(it - 2, 0), boundary_v[0]))
                    for k in range(8):
                        i0 = tensor.matmul(gA[bk][:, 0:n], w1g2[wbk][:, k * 128:(k + 1) * 128],
                                           xnb[:, k * W + c0: k * W + c0 + n],
                                           start=(k == 0), stop=(k == 7))
                        i1 = tensor.matmul(uA[bk][:, 0:n], w1u2[wbk][:, k * 128:(k + 1) * 128],
                                           xnb[:, k * W + c0: k * W + c0 + n],
                                           start=(k == 0), stop=(k == 7))
                    i2 = tensor.matmul(cA[bk][:, 0:n], orf[0:1, :],
                                       crow2[bk][0:1, 0:n], start=True, stop=True)
                    p.inc(i0, "pe")
                    p.inc(i1, "pe")
                    p.inc(i2, "pe")
                    pe_tok[it] = p.cnt["pe"]
                p.T(fmm)

                def gact(scalar, c0=c0, n=n, bk=bk, it=it):
                    p.wait_tok(scalar, "pe", pe_tok[it])
                    p.wait_tok(scalar, "v", v_tok.get(it - 2, 0))
                    p.cop(scalar, "s", scalar.activation(out=sg2[bk][:, 0:n],
                                                         in_=gA[bk][:, 0:n],
                                                         func=AF.Silu))
                    s_tok[it] = p.cnt["s"]
                p.S(gact)

                def gmul(vector, e=e, jp=jp, c0=c0, n=n, bk=bk, it=it):
                    p.wait_tok(vector, "s", s_tok[it])
                    p.cop(vector, "v", vector.tensor_mul(out=t1b2[bk][:, 0:n],
                                                         in0=sg2[bk][:, 0:n],
                                                         in1=uA[bk][:, 0:n]))
                    p.cop(vector, "v", vector.tensor_mul(
                        out=sact[:, jp * W + c0: jp * W + c0 + n],
                        in0=t1b2[bk][:, 0:n], in1=cA[bk][:, 0:n]))
                    v_tok[it] = p.cnt["v"]
                p.V(gmul)
                it += 1

        w2_tok = {}
        o_pe_tok = {}
        o_v_tok = {}
        oA = [psBr, psBm]
        for dm in range(8):
            wb = dm % 2

            def w2load(gpsimd, e=e, dm=dm, wb=wb):
                p.wait_tok(gpsimd, "pe", o_pe_tok.get((dm - 2) * 4 + 3, 0))
                p.inc(gpsimd.dma_start(
                    out=w2b2[wb].rearrange("p (a c) -> p a c", a=11),
                    in_=i_w2[e].rearrange("(a p) c -> p a c", p=128)[
                        :, :, dm * 128:(dm + 1) * 128]), "g", 16)
                w2_tok[dm] = p.cnt["g"]
            p.G(w2load)

            for ci, (c0, n) in enumerate(chunks_of(W)):
                ob = (dm * 4 + ci) % 2

                def omm(tensor, e=e, c0=c0, n=n, dm=dm, wb=wb, ob=ob, ci=ci):
                    p.wait_tok(tensor, "g", w2_tok[dm])
                    p.wait_tok(tensor, "v", o_v_tok.get((dm * 4 + ci) - 2, 0))
                    if dm == 0 and ci == 0:
                        p.wait(tensor, "v", "s")
                    for k in range(11):
                        ins = tensor.matmul(
                            oA[ob][:, 0:n], w2b2[wb][:, k * 128:(k + 1) * 128],
                            sact[:, k * W + c0: k * W + c0 + n],
                            start=(k == 0), stop=(k == 10))
                    p.inc(ins, "pe")
                    o_pe_tok[dm * 4 + ci] = p.cnt["pe"]
                p.T(omm)

                def oev(vector, e=e, c0=c0, n=n, dm=dm, ob=ob, ci=ci):
                    p.wait_tok(vector, "pe", o_pe_tok[dm * 4 + ci])
                    sl = slice(dm * W + c0, dm * W + c0 + n)
                    if e == 0:
                        p.cop(vector, "v", vector.tensor_copy(out=outacc[:, sl],
                                                              in_=oA[ob][:, 0:n]))
                    else:
                        p.cop(vector, "v", vector.tensor_add(out=outacc[:, sl],
                                                             in0=oA[ob][:, 0:n],
                                                             in1=outacc[:, sl]))
                    o_v_tok[dm * 4 + ci] = p.cnt["v"]
                p.V(oev)
        boundary_v[0] = p.cnt["v"]

    for (c0, n) in chunks_of(W):
        def fin_in(sync, c0=c0, n=n):
            p.wait(sync, "v")
            ldma(sync, xc[:, 0:8 * n].rearrange("p (a c) -> p a c", a=8),
                 xsrc.rearrange("(a p) c -> p a c", p=128)[:, :, c0:c0 + n])
        p.SY(fin_in)

        def fin(vector, c0=c0, n=n):
            p.wait(vector, "sy", "pe")
            for dt_i in range(8):
                sl = slice(dt_i * n, (dt_i + 1) * n)
                p.cop(vector, "v", vector.scalar_tensor_tensor(
                    out=xc[:, sl], in0=outacc[:, dt_i * W + c0: dt_i * W + c0 + n],
                    scalar=0.5, in1=xc[:, sl], op0=ALU.mult, op1=ALU.add))
        p.V(fin)

        def xout_d(sync, c0=c0, n=n):
            p.wait(sync, "v")
            p.inc(sync.dma_start(
                out=o_x.rearrange("(a p) c -> p a c", p=128)[:, :, c0:c0 + n],
                in_=xc[:, 0:8 * n].rearrange("p (a c) -> p a c", a=8)), "so", 16)
        p.SY(xout_d)

        if mode == "first":
            emit_ln_chunk(lambda dt_i, n=n: xc[:, dt_i * n:(dt_i + 1) * n],
                          c0, n, xnb2, 0)
            def xno_f(vector, c0=c0, n=n):
                p.wait(vector, "pe")
                for dt_i in range(8):
                    p.cop(vector, "v", vector.tensor_mul(
                        out=t1[:, 0:n], in0=xc[:, dt_i * n:(dt_i + 1) * n],
                        in1=psBr[:, 0:n]))
                    p.cop(vector, "v", vector.tensor_add(
                        out=outacc[:, dt_i * W + c0: dt_i * W + c0 + n],
                        in0=t1[:, 0:n], in1=psBm[:, 0:n]))
            p.V(xno_f)

            def xno_d(sync, c0=c0, n=n):
                p.wait(sync, "v")
                p.inc(sync.dma_start(
                    out=o_xno.rearrange("(a p) c -> p a c", p=128)[:, :, c0:c0 + n],
                    in_=outacc.rearrange("p (a c) -> p a c", a=8)[:, :, c0:c0 + n]),
                    "so", 16)
            p.SY(xno_d)

            emit_ln_chunk(
                lambda dt_i, n=n, c0=c0: outacc[:, dt_i * W + c0: dt_i * W + c0 + n],
                c0, n, xnb2, 0, gain=(lng, lnb))

            def v_d(sync, c0=c0, n=n):
                p.wait(sync, "s", "v")
                p.inc(sync.dma_start(
                    out=o_v.rearrange("(a p) c -> p a c", p=128)[:, :, c0:c0 + n],
                    in_=xnb2[:, 0:8 * n].rearrange("p (a c) -> p a c", a=8)), "so", 16)
            p.SY(v_d)

    def g(gpsimd):
        p.wait(gpsimd, "so")
    p.G(g)
    p.io = None
    if standalone:
        return p.done()
    return None


def moe_consts(z, prep, mode):
    c = {
        "w1": z["L1_ffn_w1"], "w2": z["L1_ffn_w2"],
        "gwT": as_bf(prep["gwT"]),
        "onescolb": np.ones((128, 1), dtype=BF),
        "onesrowf": np.ones((1, 128), dtype=np.float32),
        "idenf": np.eye(128, dtype=np.float32),
    }
    if mode == "first":
        c["lng"] = pcol(z["L1_attn_ln_g"], 8)
        c["lnb"] = pcol(z["L1_attn_ln_b"], 8)
    else:
        c["mergeb"] = pcol(z["L1_attn_merge_b"], 8)
    return c


def run_moe(x_slices, consts, mode, parts=None, xno=None):
    nc = build_moe(mode)
    in_maps = []
    for ci in range(NCORE):
        m = dict(consts, xT=np.ascontiguousarray(x_slices[ci]))
        if mode == "second":
            m["parts"] = np.ascontiguousarray(parts[ci])
            m["xnoT"] = np.ascontiguousarray(xno[ci])
        in_maps.append(m)
    res = run_bass_kernel_spmd(nc, in_maps, core_ids=list(range(NCORE)))
    return res.results



TSEQ = 8192


def build_m1a(p=None, io=None):
    standalone = p is None
    if standalone:
        p = P()
    p.io = io or {}
    vrank = io is not None and "vT4" in io
    rawsl = io is not None and "inwraw_f" in io
    i_v = p.din("vT4", [4, D, 2048], BF16) if vrank else p.din("vT", [D, TSEQ], BF16)
    if rawsl:
        i_inwR = [io["inwraw_f"], io["inwraw_b"]]
        i_xpR = [io["xpraw_f"], io["xpraw_b"]]
        i_iden = io["iden"]
    else:
        i_inw = [p.din("inwf_sl", [D, 1024], BF16), p.din("inwb_sl", [D, 1024], BF16)]
        i_xp = [p.din("xpf_sl", [512, 96], BF16), p.din("xpb_sl", [512, 96], BF16)]
    i_cw = [p.din("convwf", [128, 16]), p.din("convwb", [128, 16])]
    i_cb = [p.din("convbf", [128, 4]), p.din("convbb", [128, 4])]
    o_xi = p.dout("xiT", [2, 512, TSEQ])
    o_z = p.dout("zT", [2, 512, TSEQ])
    o_pp = p.dout("projp", [2, 96, TSEQ])

    vc2b = [p.sb("vcA", [128, 8 * 512], BF16), p.sb("vcB", [128, 8 * 512], BF16)]
    vc2 = p.sb("vc2", [128, 8 * 512], BF16)
    inw = [p.sb("inwf", [128, 8 * 1024], BF16), p.sb("inwb", [128, 8 * 1024], BF16)]
    cw = [p.sb("cwf", [128, 16]), p.sb("cwb", [128, 16])]
    cb = [p.sb("cbf", [128, 4]), p.sb("cbb", [128, 4])]
    xp = [p.sb("xpf", [128, 4 * 96], BF16), p.sb("xpb", [128, 4 * 96], BF16)]
    xze = p.sb("xze", [128, 4 * 516])
    last3 = p.sb("last3", [128, 4 * 4])
    xi = p.sb("xi", [128, 4 * 512])
    xib = p.sb("xib", [128, 4 * 512], BF16)
    zz = p.sb("zz", [128, 4 * 512])
    prj = p.sb("prj", [96, 512])
    psM = [p.ps("psM0", [128, 512]), p.ps("psM1", [128, 512])]
    psP = p.ps("psP", [96, 512])

    def ldma(sync, dst_ap, src_ap, *wait):
        p.wait(sync, *wait)
        p.inc(sync.dma_start(out=dst_ap, in_=src_ap), "sy", 16)

    if rawsl:
        iden_sb = p.sb("idenm", [128, 128], BF16)
        scr = p.sb("scrm", [128, 8 * 1024])
        scrb = p.sb("scrmb", [128, 8 * 1024], BF16)
        psT = p.ps("psTm", [128, 128], BF16)

        def lid(sync):
            ldma(sync, iden_sb, i_iden)
        p.SY(lid)

        for d in range(2):
            def lraw(sync, d=d):
                p.wait(sync, "pe")
                ldma(sync, scr.rearrange("p (a c) -> p a c", a=8),
                     i_inwR[d].rearrange("(a p) c -> p a c", p=128))
                ldma(sync, cw[d], i_cw[d])
                ldma(sync, cb[d], i_cb[d])
            p.SY(lraw)

            def cst_(vector, d=d):
                p.wait(vector, "sy")
                p.cop(vector, "v", vector.tensor_copy(out=scrb, in_=scr))
            p.V(cst_)

            for fr in range(8):
                for dc in range(8):
                    def tmm(tensor, d=d, fr=fr, dc=dc):
                        p.wait(tensor, "v")
                        p.inc(tensor.transpose(
                            psT, scrb[:, fr * 1024 + dc * 128: fr * 1024 + (dc + 1) * 128],
                            iden_sb), "pe")
                    p.T(tmm)

                    def tev(vector, d=d, fr=fr, dc=dc):
                        p.wait(vector, "pe")
                        p.cop(vector, "v", vector.tensor_copy(
                            out=inw[d][:, dc * 1024 + fr * 128: dc * 1024 + (fr + 1) * 128],
                            in_=psT), "v") if False else p.cop(
                            vector, "v", vector.tensor_copy(
                                out=inw[d][:, dc * 1024 + fr * 128:
                                           dc * 1024 + (fr + 1) * 128],
                                in_=psT))
                    p.V(tev)

            def lxp(sync, d=d):
                p.wait(sync, "pe", "v")
                ldma(sync, scr[0:96, 0:512], i_xpR[d])
            p.SY(lxp)

            def cxp(vector, d=d):
                p.wait(vector, "sy")
                p.cop(vector, "v", vector.tensor_copy(out=scrb[0:96, 0:512],
                                                      in_=scr[0:96, 0:512]))
            p.V(cxp)

            for a in range(4):
                def txp(tensor, d=d, a=a):
                    p.wait(tensor, "v")
                    p.inc(tensor.transpose(psT[0:128, 0:96],
                                           scrb[0:96, a * 128:(a + 1) * 128],
                                           iden_sb[0:96, 0:96]), "pe")
                p.T(txp)

                def exp_(vector, d=d, a=a):
                    p.wait(vector, "pe")
                    p.cop(vector, "v", vector.tensor_copy(
                        out=xp[d][:, a * 96:(a + 1) * 96], in_=psT[0:128, 0:96]))
                p.V(exp_)
    else:
        def loads(sync):
            for d in range(2):
                ldma(sync, inw[d].rearrange("p (a c) -> p a c", a=8),
                     i_inw[d].rearrange("(a p) c -> p a c", p=128))
                ldma(sync, cw[d], i_cw[d])
                ldma(sync, cb[d], i_cb[d])
                ldma(sync, xp[d].rearrange("p (a c) -> p a c", a=4),
                     i_xp[d].rearrange("(a p) c -> p a c", p=128))
        p.SY(loads)

    for d in range(2):
        def zhead(vector, d=d):
            p.wait(vector, "sy", "so")
            for pt in range(4):
                p.cop(vector, "v", vector.memset(xze[:, pt * 516: pt * 516 + 4], 0.0))
            p.cop(vector, "v", vector.memset(last3, 0.0))
        p.V(zhead)

        lastmm_tok = {-1: 0}
        vin_tok = {}
        ev_tok = {6: p.cnt["v"], 7: p.cnt["v"]}
        for tci in range(TSEQ // 512):
            c0 = tci * 512
            n = 512
            vcb = vc2b[tci % 2]

            def vin(sync, d=d, c0=c0, n=n, tci=tci, vcb=vcb):
                p.wait_tok(sync, "pe", lastmm_tok.get(tci - 2, 0))
                if d == 1:
                    p.wait(sync, "v")
                src_c0 = c0 if d == 0 else TSEQ - c0 - n
                if vrank:
                    q, off = divmod(src_c0, 2048)
                    ap = i_v[q].rearrange("(a p) c -> p a c", p=128)[:, :, off:off + n]
                else:
                    ap = i_v.rearrange("(a p) c -> p a c", p=128)[:, :, src_c0:src_c0 + n]
                ldma(sync, (vcb if d == 0 else vc2).rearrange("p (a c) -> p a c", a=8), ap)
                vin_tok[tci] = p.cnt["sy"]
            p.SY(vin)

            if d == 1:
                def vflip(vector, n=n, tci=tci, vcb=vcb):
                    p.wait_tok(vector, "sy", vin_tok[tci])
                    p.wait_tok(vector, "pe", lastmm_tok.get(tci - 2, 0))
                    for dt_i in range(8):
                        p.cop(vector, "v", vector.tensor_copy(
                            out=vcb[:, dt_i * n:(dt_i + 1) * n],
                            in_=vc2[:, dt_i * n:(dt_i + 1) * n][:, ::-1]))
                p.V(vflip)

            mm_tok = {}
            for m in range(8):
                def mm(tensor, d=d, m=m, n=n, tci=tci, vcb=vcb):
                    if m == 0:
                        if d == 0:
                            p.wait_tok(tensor, "sy", vin_tok[tci])
                        else:
                            p.wait(tensor, "v")
                    p.wait_tok(tensor, "v", ev_tok.get((m - 2) % 8, 0))
                    pss = psM[m % 2]
                    col = (m % 4) * 128 + (512 if m >= 4 else 0)
                    for k in range(8):
                        ins = tensor.matmul(
                            pss[:, 0:n], inw[d][:, k * 1024 + col: k * 1024 + col + 128],
                            vcb[:, k * n:(k + 1) * n], start=(k == 0), stop=(k == 7))
                    p.inc(ins, "pe")
                    mm_tok[m] = p.cnt["pe"]
                    lastmm_tok[tci] = p.cnt["pe"]
                p.T(mm)

                if m < 4:
                    def ev(vector, d=d, m=m, n=n):
                        p.wait_tok(vector, "pe", mm_tok[m])
                        pss = psM[m % 2]
                        p.cop(vector, "v", vector.tensor_copy(
                            out=xze[:, m * 516 + 1: m * 516 + 4],
                            in_=last3[:, m * 4: m * 4 + 3]))
                        p.cop(vector, "v", vector.tensor_copy(
                            out=xze[:, m * 516 + 4: m * 516 + 4 + n], in_=pss[:, 0:n]))
                        p.cop(vector, "v", vector.tensor_copy(
                            out=last3[:, m * 4: m * 4 + 3],
                            in_=xze[:, m * 516 + 1 + n: m * 516 + 4 + n]))
                        ev_tok[m] = p.cnt["v"]
                    p.V(ev)
                else:
                    def evz(vector, d=d, m=m, n=n):
                        p.wait_tok(vector, "pe", mm_tok[m])
                        pss = psM[m % 2]
                        p.cop(vector, "v", vector.tensor_copy(
                            out=zz[:, (m - 4) * n:(m - 3) * n], in_=pss[:, 0:n]))
                        ev_tok[m] = p.cnt["v"]
                    p.V(evz)

            def conv(vector, d=d, n=n):
                for pt in range(4):
                    base = pt * 516
                    p.cop(vector, "v", vector.tensor_scalar(
                        out=xi[:, pt * n:(pt + 1) * n],
                        in0=xze[:, base + 1: base + 1 + n],
                        scalar1=cw[d][:, pt * 4: pt * 4 + 1], scalar2=None,
                        op0=ALU.mult))
                    for k in range(1, 4):
                        p.cop(vector, "v", vector.scalar_tensor_tensor(
                            out=xi[:, pt * n:(pt + 1) * n],
                            in0=xze[:, base + 1 + k: base + 1 + k + n],
                            scalar=cw[d][:, pt * 4 + k: pt * 4 + k + 1],
                            in1=xi[:, pt * n:(pt + 1) * n],
                            op0=ALU.mult, op1=ALU.add))
            p.V(conv)

            def silu(scalar, d=d, n=n):
                p.wait(scalar, "v")
                for pt in range(4):
                    p.cop(scalar, "s", scalar.activation(
                        out=xi[:, pt * n:(pt + 1) * n], in_=xi[:, pt * n:(pt + 1) * n],
                        func=AF.Silu, bias=cb[d][:, pt:pt + 1]))
            p.S(silu)

            def cast(vector, n=n):
                p.wait(vector, "s")
                p.cop(vector, "v", vector.tensor_copy(out=xib[:, 0:4 * n],
                                                      in_=xi[:, 0:4 * n]))
            p.V(cast)

            def xpm(tensor, d=d, n=n):
                p.wait(tensor, "v")
                for k in range(4):
                    ins = tensor.matmul(psP[:, 0:n], xp[d][:, k * 96:(k + 1) * 96],
                                        xib[:, k * n:(k + 1) * n],
                                        start=(k == 0), stop=(k == 3))
                p.inc(ins, "pe")
            p.T(xpm)

            def pev(vector, n=n):
                p.wait(vector, "pe")
                p.cop(vector, "v", vector.tensor_copy(out=prj[:, 0:n], in_=psP[:, 0:n]))
            p.V(pev)

            def outs(sync, d=d, c0=c0, n=n):
                p.wait(sync, "v")
                p.inc(sync.dma_start(
                    out=o_xi[d].rearrange("(a p) c -> p a c", p=128)[:, :, c0:c0 + n],
                    in_=xi.rearrange("p (a c) -> p a c", a=4)), "so", 16)
                p.inc(sync.dma_start(
                    out=o_z[d].rearrange("(a p) c -> p a c", p=128)[:, :, c0:c0 + n],
                    in_=zz.rearrange("p (a c) -> p a c", a=4)), "so", 16)
                p.inc(sync.dma_start(out=o_pp[d][:, c0:c0 + n], in_=prj[:, 0:n]),
                      "so", 16)
            p.SY(outs)

            def war(vector):
                p.wait(vector, "so")
            p.V(war)

    def g(gpsimd):
        p.wait(gpsimd, "so")
    p.G(g)
    p.io = None
    if standalone:
        return p.done()
    return None


def m1a_inputs(z, prep, vT_full_by_b):
    maps = []
    inwT = {0: as_bf(prep["inwTf"]), 1: as_bf(prep["inwTb"])}
    xpT = {0: as_bf(prep["xpTf"]), 1: as_bf(prep["xpTb"])}
    cwk = {0: z["L1_attn_fwd_conv_w"], 1: z["L1_attn_bwd_conv_w"]}
    cbk = {0: z["L1_attn_fwd_conv_b"], 1: z["L1_attn_bwd_conv_b"]}
    for c in range(NCORE):
        b, s4 = divmod(c, 4)
        lo = 512 * s4
        m = {"vT": vT_full_by_b[b]}
        for d, tag in ((0, "f"), (1, "b")):
            iw = inwT[d]
            m[f"inw{tag}_sl"] = np.ascontiguousarray(
                np.concatenate([iw[:, lo:lo + 512], iw[:, 2048 + lo:2048 + lo + 512]], 1))
            cwr = cwk[d][lo:lo + 512]
            m[f"convw{tag}"] = np.ascontiguousarray(
                cwr.reshape(4, 128, 4).transpose(1, 0, 2).reshape(128, 16)).astype(np.float32)
            m[f"convb{tag}"] = pcol(cbk[d][lo:lo + 512], 4)
            m[f"xp{tag}_sl"] = np.ascontiguousarray(xpT[d][lo:lo + 512, :])
        maps.append(m)
    return maps



def build_m1b(p=None, io=None):
    standalone = p is None
    if standalone:
        p = P()
    p.io = io or {}
    i_xi = p.din("xiT", [2, 512, TSEQ])
    i_z = p.din("zT", [2, 512, TSEQ])
    fused_pp = io is not None and "pp_sum" in io
    i_pp = p.din("pp_sum", [2, 96, TSEQ]) if fused_pp else p.din("ppall", [4, 2, 96, TSEQ])
    yperm = io.get("yperm") if io is not None else None
    rawsl = io is not None and "dtwraw_f" in io
    if rawsl:
        i_dtwR = [io["dtwraw_f"], io["dtwraw_b"]]
        i_dtbR = [io["dtbraw_f"], io["dtbraw_b"]]
        i_owR = [io["outwraw_f"], io["outwraw_b"]]
        i_mdT = [io["mfT"], io["mbT"]]
        i_iden = io["iden"]
    else:
        i_dtw = [p.din("dtwf_sl", [64, 512], BF16), p.din("dtwb_sl", [64, 512], BF16)]
        i_dtb = [p.din("dtbf_r", [1, 512], BF16), p.din("dtbb_r", [1, 512], BF16)]
        i_cmb = [p.din("combf_sl", [512, 1024], BF16), p.din("combb_sl", [512, 1024], BF16)]
    i_al = [p.din("alf", [128, 64]), p.din("alb", [128, 64])]
    i_dd = [p.din("ddf", [128, 4]), p.din("ddb", [128, 4])]
    i_orb = p.din("onesrowb", [1, 512], BF16)
    o_y = p.dout("ypart", [D, TSEQ])
    dsc = p.nc.dram_tensor("delta_s", [2, 512, TSEQ], F32, kind="Internal").ap()
    gsc = p.nc.dram_tensor("g_s", [2, 512, TSEQ], BF16, kind="Internal").ap()

    proj = p.sb("proj", [96, TSEQ])
    prjb = p.sb("prjb", [64, 512], BF16)
    dtw = [p.sb("dtwf", [64, 512], BF16), p.sb("dtwb", [64, 512], BF16)]
    dtb = [p.sb("dtbf", [1, 512], BF16), p.sb("dtbb", [1, 512], BF16)]
    negA = [p.sb("negAf", [128, 64]), p.sb("negAb", [128, 64])]
    dd = [p.sb("ddf", [128, 4]), p.sb("ddb", [128, 4])]
    cmbT = [p.sb("cmbf", [128, 4 * 1024], BF16), p.sb("cmbb", [128, 4 * 1024], BF16)]
    orb = p.sb("orb", [1, 512], BF16)
    orf1 = p.sb("orf1", [1, 128], BF16)
    cst = p.sb("cst", [1, 8])
    cstp = p.sb("cstp", [128, 2])
    dchunk = p.sb("dchunk", [128, 4 * 2048])
    xichunk = p.sb("xichunk", [128, 4 * 2048])
    dx = p.sb("dx", [128, 4 * 2048])
    yacc = p.sb("yacc", [128, 4 * 2048])
    at = xichunk[:, 0:2048]
    bt = xichunk[:, 2048:4096]
    ht = xichunk[:, 4096:6144]
    carry = p.sb("carry", [128, 64])
    brow_s = p.sb("brow_s", [1, 2048], BF16)
    crow_s = p.sb("crow_s", [1, 2048], BF16)
    gb = p.sb("gb", [128, 4 * 2048], BF16)
    t128 = dx[:, 0:512]
    psB = p.ps("psB", [128, 2048])
    psC = p.ps("psC", [128, 2048])
    psD_ = psB[:, 0:512]
    psO = psC[:, 0:512]

    def ldma(sync, dst_ap, src_ap, *wait):
        p.wait(sync, *wait)
        p.inc(sync.dma_start(out=dst_ap, in_=src_ap), "sy", 16)

    def loads(sync):
        for d in range(2):
            if not rawsl:
                ldma(sync, dtw[d], i_dtw[d])
                ldma(sync, dtb[d], i_dtb[d])
                ldma(sync, cmbT[d].rearrange("p (a c) -> p a c", a=4),
                     i_cmb[d].rearrange("(a p) c -> p a c", p=128))
            ldma(sync, negA[d], i_al[d])
            ldma(sync, dd[d], i_dd[d])
        ldma(sync, orb, i_orb)
    p.SY(loads)

    if rawsl:
        iden_sb = p.sb("idenm", [128, 128], BF16)
        psT2 = psB.bitcast(BF16)[0:128, 0:128]
        psW = psC[:, 0:512]
        scw = dx[:, 0:512]
        scwb = gb
        mdt_sb = gb[:, 2048:2048 + 4096]

        def lid2(sync):
            ldma(sync, iden_sb, i_iden)
        p.SY(lid2)

        for d in range(2):
            def ldt(sync, d=d):
                p.wait(sync, "pe", "v")
                ldma(sync, dx[:, 0:4 * 64].rearrange("p (a c) -> p a c", a=4),
                     i_dtwR[d].rearrange("(a p) c -> p a c", p=128))
                ldma(sync, dx[:, 512:512 + 512][0:1, :], i_dtbR[d])
            p.SY(ldt)

            def cdt(vector, d=d):
                p.wait(vector, "sy")
                p.cop(vector, "v", vector.tensor_copy(
                    out=gb.bitcast(BF16)[:, 0:4 * 64], in_=dx[:, 0:4 * 64]))
                p.cop(vector, "v", vector.tensor_copy(
                    out=dtb[d], in_=dx[0:1, 512:1024]))
            p.V(cdt)

            for a in range(4):
                def tdt(tensor, d=d, a=a):
                    p.wait(tensor, "v")
                    p.inc(tensor.transpose(psT2[0:64, 0:128],
                                           gb[:, a * 64:(a + 1) * 64],
                                           iden_sb), "pe")
                p.T(tdt)

                def edt(vector, d=d, a=a):
                    p.wait(vector, "pe")
                    p.cop(vector, "v", vector.tensor_copy(
                        out=dtw[d][:, a * 128:(a + 1) * 128], in_=psT2[0:64, 0:128]))
                p.V(edt)

            for half in range(2):
                def lmd(sync, d=d, half=half):
                    p.wait(sync, "pe", "v")
                    ldma(sync, mdt_sb.rearrange("p (a c) -> p a c", a=8),
                         i_mdT[d].rearrange("(a p) c -> p a c", p=128)[
                             :, :, half * 512:(half + 1) * 512])
                p.SY(lmd)

                for ct in range(4):
                    def low(sync, d=d, ct=ct):
                        p.wait(sync, "pe", "v")
                        ldma(sync, dchunk[:, 0:8 * 128].rearrange("p (a c) -> p a c", a=8),
                             i_owR[d].rearrange("(a p) c -> p a c", p=128)[
                                 :, :, ct * 128:(ct + 1) * 128])
                    p.SY(low)

                    def cow(vector, d=d, ct=ct):
                        p.wait(vector, "sy")
                        p.cop(vector, "v", vector.tensor_copy(
                            out=gb[:, 0:8 * 128], in_=dchunk[:, 0:8 * 128]))
                    p.V(cow)

                    def pmm(tensor, d=d, ct=ct, half=half):
                        p.wait(tensor, "v")
                        for k in range(8):
                            ins = tensor.matmul(
                                psW, gb[:, k * 128:(k + 1) * 128],
                                mdt_sb[:, k * 512:(k + 1) * 512],
                                start=(k == 0), stop=(k == 7))
                        p.inc(ins, "pe")
                    p.T(pmm)

                    def pev2(vector, d=d, ct=ct, half=half):
                        p.wait(vector, "pe")
                        p.cop(vector, "v", vector.tensor_copy(
                            out=cmbT[d][:, ct * 1024 + half * 512:
                                        ct * 1024 + half * 512 + 512],
                            in_=psW))
                    p.V(pev2)

    def mkconst(vector):
        p.wait(vector, "sy")
        p.cop(vector, "v", vector.memset(cst[0:1, 0:1], -1.0))
        p.cop(vector, "v", vector.memset(cst[0:1, 1:2], 1.0))
        p.cop(vector, "v", vector.memset(cstp[:, 0:1], -1.0))
        p.cop(vector, "v", vector.memset(cstp[:, 1:2], 1.0))
        p.cop(vector, "v", vector.memset(orf1.bitcast(F32)[0:1, 0:64], 0.0))
        p.cop(vector, "v", vector.memset(orf1, 1.0))
    p.V(mkconst)

    def mkA(scalar):
        p.wait(scalar, "v")
        for d in range(2):
            p.cop(scalar, "s", scalar.activation(out=negA[d], in_=negA[d], func=AF.Exp))
    p.S(mkA)

    def negit(vector):
        p.wait(vector, "s")
        for d in range(2):
            p.cop(vector, "v", vector.tensor_scalar(out=negA[d], in0=negA[d],
                                                    scalar1=-1.0, scalar2=None,
                                                    op0=ALU.mult))
    p.V(negit)

    NCH = TSEQ // 2048

    for d in range(2):
        if fused_pp:
            def pin2(sync, d=d):
                p.wait(sync, "v", "cc")
                ldma(sync, proj, i_pp[d])
            p.SY(pin2)

            def psy(vector):
                p.wait(vector, "sy")
            p.V(psy)
        else:
          for tci in range(TSEQ // 512):
            c0 = tci * 512

            def pin(sync, d=d, c0=c0):
                p.wait(sync, "v")
                for q in range(4):
                    ldma(sync, dchunk[0:96, q * 512:(q + 1) * 512], i_pp[q, d][:, c0:c0 + 512])
            p.SY(pin)

            def padd(vector, c0=c0):
                p.wait(vector, "sy")
                p.cop(vector, "v", vector.tensor_add(out=proj[0:96, c0:c0 + 512],
                                                     in0=dchunk[0:96, 0:512],
                                                     in1=dchunk[0:96, 512:1024]))
                p.cop(vector, "v", vector.tensor_add(out=proj[0:96, c0:c0 + 512],
                                                     in0=proj[0:96, c0:c0 + 512],
                                                     in1=dchunk[0:96, 1024:1536]))
                p.cop(vector, "v", vector.tensor_add(out=proj[0:96, c0:c0 + 512],
                                                     in0=proj[0:96, c0:c0 + 512],
                                                     in1=dchunk[0:96, 1536:2048]))
            p.V(padd)

        dp_ps = [psD_, psC[:, 512:1024]]
        t1282 = [t128, dx[:, 512:1024]]
        dp_pe = {}
        dp_v = {-2: p.cnt["v"], -1: p.cnt["v"]}
        for tci in range(TSEQ // 512):
            c0 = tci * 512

            def dcast(vector, c0=c0):
                p.wait(vector, "pe")
                p.cop(vector, "v", vector.tensor_copy(out=prjb,
                                                      in_=proj[0:64, c0:c0 + 512]))
            p.V(dcast)

            for pt in range(4):
                def dmm(tensor, d=d, pt=pt, c0=c0):
                    if pt == 0:
                        p.wait(tensor, "v")
                    else:
                        p.wait_tok(tensor, "v", dp_v.get(pt - 2, 0))
                    psDx = dp_ps[pt % 2]
                    tensor.matmul(psDx, dtb[d][0:1, pt * 128:(pt + 1) * 128], orb,
                                  start=True, stop=False)
                    ins = tensor.matmul(psDx, dtw[d][:, pt * 128:(pt + 1) * 128],
                                        prjb, start=False, stop=True)
                    p.inc(ins, "pe")
                    dp_pe[pt] = p.cnt["pe"]
                p.T(dmm)

                def sp1(vector, pt=pt):
                    p.wait_tok(vector, "pe", dp_pe[pt])
                    psDx = dp_ps[pt % 2]
                    p.cop(vector, "v", vector.tensor_scalar_max(
                        out=t1282[pt % 2], in0=psDx, scalar1=0.0))
                p.V(sp1)

                def sp2(scalar, pt=pt):
                    p.wait(scalar, "v")
                    psDx = dp_ps[pt % 2]
                    p.cop(scalar, "s", scalar.activation(
                        out=dchunk[:, pt * 2048: pt * 2048 + 512], in_=psDx, func=AF.Abs))
                    p.cop(scalar, "s", scalar.activation(
                        out=dchunk[:, pt * 2048: pt * 2048 + 512],
                        in_=dchunk[:, pt * 2048: pt * 2048 + 512],
                        func=AF.Exp, scale=cstp[:, 0:1]))
                    p.cop(scalar, "s", scalar.activation(
                        out=dchunk[:, pt * 2048: pt * 2048 + 512],
                        in_=dchunk[:, pt * 2048: pt * 2048 + 512],
                        func=AF.Ln, bias=cstp[:, 1:2]))
                p.S(sp2)

                def sp3(vector, pt=pt):
                    p.wait(vector, "s")
                    p.cop(vector, "v", vector.tensor_add(
                        out=dchunk[:, pt * 2048: pt * 2048 + 512], in0=t1282[pt % 2],
                        in1=dchunk[:, pt * 2048: pt * 2048 + 512]))
                    dp_v[pt] = p.cnt["v"]
                p.V(sp3)

            def dout_(sync, d=d, c0=c0):
                p.wait(sync, "v")
                p.inc(sync.dma_start(
                    out=dsc[d].rearrange("(a p) c -> p a c", p=128)[:, :, c0:c0 + 512],
                    in_=dchunk.rearrange("p (a c) -> p a c", a=4)[:, :, 0:512]), "so", 16)
            p.SY(dout_)

            def war(vector):
                p.wait(vector, "so")
            p.V(war)

        def czero(vector, d=d):
            p.cop(vector, "v", vector.memset(carry, 0.0))
        p.V(czero)

        for cc in range(NCH):
            c0 = cc * 2048

            def sin(sync, d=d, c0=c0):
                p.wait(sync, "v")
                ldma(sync, dchunk.rearrange("p (a c) -> p a c", a=4),
                     dsc[d].rearrange("(a p) c -> p a c", p=128)[:, :, c0:c0 + 2048])
                ldma(sync, xichunk.rearrange("p (a c) -> p a c", a=4),
                     i_xi[d].rearrange("(a p) c -> p a c", p=128)[:, :, c0:c0 + 2048])
            p.SY(sin)

            def dxm(vector, d=d, c0=c0):
                p.wait(vector, "sy")
                p.cop(vector, "v", vector.tensor_mul(out=dx, in0=dchunk, in1=xichunk))
                for pt in range(4):
                    p.cop(vector, "v", vector.tensor_scalar(
                        out=yacc[:, pt * 2048:(pt + 1) * 2048],
                        in0=xichunk[:, pt * 2048:(pt + 1) * 2048],
                        scalar1=dd[d][:, pt:pt + 1], scalar2=None, op0=ALU.mult))
            p.V(dxm)

            at2 = [at, xichunk[:, 6144:8192]]
            sbB2 = [gb[:, 0:2048], gb[:, 2048:4096]]
            sbC2 = [gb[:, 4096:6144], gb[:, 6144:8192]]
            a_tok = {-2: p.cnt["v"], -1: p.cnt["v"]}
            v_tok2 = {-2: p.cnt["v"], -1: p.cnt["v"]}
            pe_tok2 = {-1: p.cnt["pe"]}
            s_tok2 = {-1: p.cnt["s"]}
            for s in range(16):
                def rowd(gpsimd, d=d, s=s, c0=c0):
                    p.wait_tok(gpsimd, "pe", pe_tok2.get(s - 1, 0))
                    p.inc(gpsimd.dma_start(out=brow_s,
                                           in_=proj[64 + s: 65 + s, c0:c0 + 2048]), "g", 16)
                    p.inc(gpsimd.dma_start(out=crow_s,
                                           in_=proj[80 + s: 81 + s, c0:c0 + 2048]), "g", 16)
                p.G(rowd)

                def bc(tensor, s=s):
                    p.wait(tensor, "g")
                    p.wait_tok(tensor, "s", s_tok2.get(s - 1, 0))
                    for q in range(4):
                        i0 = tensor.matmul(psB[:, q * 512:(q + 1) * 512], orf1,
                                           brow_s[0:1, q * 512:(q + 1) * 512],
                                           start=True, stop=True)
                        i1 = tensor.matmul(psC[:, q * 512:(q + 1) * 512], orf1,
                                           crow_s[0:1, q * 512:(q + 1) * 512],
                                           start=True, stop=True)
                    p.inc(i0, "pe")
                    p.inc(i1, "pe")
                    pe_tok2[s] = p.cnt["pe"]
                p.T(bc)

                def evacbc(scalar, s=s):
                    p.wait_tok(scalar, "pe", pe_tok2[s])
                    p.wait_tok(scalar, "v", v_tok2.get(s - 2, 0))
                    if s < 2:
                        p.wait(scalar, "so")
                    p.cop(scalar, "s", scalar.activation(out=sbB2[s % 2], in_=psB,
                                                         func=AF.Copy))
                    p.cop(scalar, "s", scalar.activation(out=sbC2[s % 2], in_=psC,
                                                         func=AF.Copy))
                    s_tok2[s] = p.cnt["s"]
                p.S(evacbc)

                for pt in range(4):
                    idx = s * 4 + pt

                    def apass(scalar, d=d, s=s, pt=pt, idx=idx):
                        p.wait_tok(scalar, "v", a_tok.get(idx - 2, 0))
                        p.cop(scalar, "s", scalar.activation(
                            out=at2[idx % 2], in_=dchunk[:, pt * 2048:(pt + 1) * 2048],
                            func=AF.Exp, scale=negA[d][:, pt * 16 + s: pt * 16 + s + 1]))
                    p.S(apass)

                    def bpass(vector, s=s, pt=pt, idx=idx):
                        p.wait(vector, "s")
                        p.cop(vector, "v", vector.tensor_mul(
                            out=bt, in0=dx[:, pt * 2048:(pt + 1) * 2048],
                            in1=sbB2[s % 2]))
                        p.cop(vector, "v", vector.tensor_tensor_scan(
                            out=bt, data0=at2[idx % 2], data1=bt,
                            initial=carry[:, pt * 16 + s: pt * 16 + s + 1],
                            op0=ALU.mult, op1=ALU.add))
                        a_tok[idx] = p.cnt["v"]
                        p.cop(vector, "v", vector.tensor_copy(
                            out=carry[:, pt * 16 + s: pt * 16 + s + 1],
                            in_=bt[:, 2047:2048]))
                        p.cop(vector, "v", vector.tensor_mul(out=bt, in0=bt,
                                                             in1=sbC2[s % 2]))
                        p.cop(vector, "v", vector.tensor_add(
                            out=yacc[:, pt * 2048:(pt + 1) * 2048],
                            in0=yacc[:, pt * 2048:(pt + 1) * 2048], in1=bt))
                        v_tok2[s] = p.cnt["v"]
                    p.V(bpass)

            def zin(sync, d=d, c0=c0):
                p.wait(sync, "v")
                ldma(sync, dchunk.rearrange("p (a c) -> p a c", a=4),
                     i_z[d].rearrange("(a p) c -> p a c", p=128)[:, :, c0:c0 + 2048])
            p.SY(zin)

            def ep2(scalar, d=d):
                p.wait(scalar, "v", "sy")
                for pt in range(4):
                    p.cop(scalar, "s", scalar.activation(
                        out=dchunk[:, pt * 2048:(pt + 1) * 2048],
                        in_=dchunk[:, pt * 2048:(pt + 1) * 2048], func=AF.Silu))
            p.S(ep2)

            def ep3(vector, d=d, c0=c0):
                p.wait(vector, "s")
                for pt in range(4):
                    for q in range(4):
                        sl_in = slice(pt * 2048 + q * 512, pt * 2048 + (q + 1) * 512)
                        if d == 0:
                            p.cop(vector, "v", vector.tensor_mul(
                                out=gb[:, (pt * 4 + q) * 512:(pt * 4 + q + 1) * 512],
                                in0=yacc[:, sl_in], in1=dchunk[:, sl_in]))
                        else:
                            p.cop(vector, "v", vector.tensor_mul(
                                out=t128, in0=yacc[:, sl_in], in1=dchunk[:, sl_in]))
                            p.cop(vector, "v", vector.tensor_copy(
                                out=gb[:, (pt * 4 + q) * 512:(pt * 4 + q + 1) * 512],
                                in_=t128[:, ::-1]))
            p.V(ep3)

            def gout(sync, d=d, c0=c0):
                p.wait(sync, "v")
                if d == 0:
                    p.inc(sync.dma_start(
                        out=gsc[0].rearrange("(a p) c -> p a c", p=128)[:, :, c0:c0 + 2048],
                        in_=gb.rearrange("p (a q c) -> p a q c", a=4, q=4)), "so", 16)
                else:
                    base = TSEQ - c0 - 2048
                    for q in range(4):
                        p.inc(sync.dma_start(
                            out=gsc[1].rearrange("(a p) c -> p a c", p=128)[
                                :, :, base + (3 - q) * 512: base + (4 - q) * 512],
                            in_=gb.rearrange("p (a c) -> p a c", a=4)[
                                :, :, q * 512:(q + 1) * 512]), "so", 16)
            p.SY(gout)

            def war2(vector):
                p.wait(vector, "so")
            p.V(war2)

    for tci in range(TSEQ // 512):
        c0 = tci * 512

        def gin(sync, c0=c0):
            p.wait(sync, "v")
            for d in range(2):
                ldma(sync, gb[:, d * 2048:(d + 1) * 2048].rearrange(
                    "p (a c) -> p a c", a=4),
                    gsc[d].rearrange("(a p) c -> p a c", p=128)[:, :, c0:c0 + 512])
        p.SY(gin)

        p4_pe = {}
        p4_v = {-2: p.cnt["v"], -1: p.cnt["v"]}
        psO2l = [psO, psB[:, 512:1024]]
        for dm in range(8):
            def omm(tensor, c0=c0, dm=dm):
                if dm == 0:
                    p.wait(tensor, "sy", "v")
                else:
                    p.wait_tok(tensor, "sy", p.cnt["sy"])
                    p.wait_tok(tensor, "v", p4_v.get(dm - 2, 0))
                psOx = psO2l[dm % 2]
                first = True
                for d in range(2):
                    for pt in range(4):
                        ins = tensor.matmul(
                            psOx, cmbT[d][:, pt * 1024 + dm * 128:
                                          pt * 1024 + (dm + 1) * 128],
                            gb[:, (d * 4 + pt) * 512:(d * 4 + pt + 1) * 512],
                            start=first, stop=(d == 1 and pt == 3))
                        first = False
                p.inc(ins, "pe")
                p4_pe[dm] = p.cnt["pe"]
            p.T(omm)

            def oev(vector, c0=c0, dm=dm):
                p.wait_tok(vector, "pe", p4_pe[dm])
                psOx = psO2l[dm % 2]
                p.cop(vector, "v", vector.tensor_copy(out=yacc[:, dm * 512:
                                                               (dm + 1) * 512],
                                                      in_=psOx))
                p4_v[dm] = p.cnt["v"]
            p.V(oev)

        def yout(sync, c0=c0):
            p.wait(sync, "v")
            if yperm is not None:
                q, off = divmod(c0, 2048)
                dst = yperm[q].rearrange("(a p) c -> p a c", p=128)[:, :, off:off + 512]
            else:
                dst = o_y.rearrange("(a p) c -> p a c", p=128)[:, :, c0:c0 + 512]
            p.inc(sync.dma_start(
                out=dst,
                in_=yacc[:, 0:8 * 512].rearrange("p (a c) -> p a c", a=8)), "so", 16)
        p.SY(yout)

        def war3(vector):
            p.wait(vector, "so")
        p.V(war3)

    def g(gpsimd):
        p.wait(gpsimd, "so")
    p.G(g)
    p.io = None
    if standalone:
        return p.done()
    return None


def m1b_inputs(z, prep, m1a_results):
    maps = []
    dtwT = {0: as_bf(prep["dtwTf"]), 1: as_bf(prep["dtwTb"])}
    dtbr = {0: as_bf(prep["dtbrf"]), 1: as_bf(prep["dtbrb"])}
    alk = {0: z["L1_attn_fwd_A_log"], 1: z["L1_attn_bwd_A_log"]}
    ddk = {0: z["L1_attn_fwd_D"], 1: z["L1_attn_bwd_D"]}
    cmb = {0: as_bf(prep["combTf"]), 1: as_bf(prep["combTb"])}
    for c in range(NCORE):
        b, s4 = divmod(c, 4)
        lo = 512 * s4
        pp = np.stack([np.asarray(m1a_results[4 * b + q]["projp"]) for q in range(4)])
        m = {
            "xiT": np.asarray(m1a_results[c]["xiT"]),
            "zT": np.asarray(m1a_results[c]["zT"]),
            "ppall": np.ascontiguousarray(pp),
            "onesrowb": np.ones((1, 512), dtype=BF),
        }
        for d, tag in ((0, "f"), (1, "b")):
            m[f"dtw{tag}_sl"] = np.ascontiguousarray(dtwT[d][:, lo:lo + 512])
            m[f"dtb{tag}_r"] = np.ascontiguousarray(dtbr[d][:, lo:lo + 512])
            al = alk[d][lo:lo + 512]
            m[f"al{tag}"] = np.ascontiguousarray(
                al.reshape(4, 128, 16).transpose(1, 0, 2).reshape(128, 64)).astype(np.float32)
            m[f"dd{tag}"] = pcol(ddk[d][lo:lo + 512], 4)
            m[f"comb{tag}_sl"] = np.ascontiguousarray(cmb[d][lo:lo + 512, :])
        maps.append(m)
    return maps



def kernel_multi(**inputs):
    z = inputs
    prep = run_prep(z)

    x = np.asarray(z["L0_attn_x"])
    xT = np.ascontiguousarray(x.reshape(16384, 1024).T)

    fconsts = ffn_consts(z, prep)
    x1s = run_ffn(x_slices_halo(xT), fconsts, W1)
    x2s = run_attn0(x1s, attn0_consts(prep))
    x3s = run_ffn(x2s, fconsts, TCORE)

    moe1 = run_moe(x3s, moe_consts(z, prep, "first"), "first")
    x4s = [np.asarray(r["oT"]) for r in moe1]
    xnos = [np.asarray(r["xnoT"]) for r in moe1]
    vs = [np.asarray(r["vT"]) for r in moe1]

    v_by_b = [np.ascontiguousarray(np.concatenate(vs[4 * b:4 * b + 4], axis=1))
              for b in range(2)]
    m1a_maps = m1a_inputs(z, prep, v_by_b)
    res_a = run_bass_kernel_spmd(build_m1a(), m1a_maps, core_ids=list(range(NCORE)))
    m1b_maps = m1b_inputs(z, prep, res_a.results)
    res_b = run_bass_kernel_spmd(build_m1b(), m1b_maps, core_ids=list(range(NCORE)))

    parts = []
    for c in range(NCORE):
        b, q = divmod(c, 4)
        t0 = q * TCORE
        parts.append(np.stack(
            [np.asarray(res_b.results[4 * b + qq]["ypart"])[:, t0:t0 + TCORE]
             for qq in range(4)]))

    out2 = run_moe(x4s, moe_consts(z, prep, "second"), "second",
                   parts=parts, xno=xnos)
    outs = [np.asarray(r["oT"]) for r in out2]
    full = np.concatenate(outs, axis=1)
    return np.ascontiguousarray(full.T).reshape(2, 8192, 1024)





RG4 = [[0, 1, 2, 3], [4, 5, 6, 7]]


def build_fused():
    p = P(ndev=8)
    shared = {
        "onescolb": p.din("onescolb", [128, 1], BF16),
        "onesrowb": p.din("onesrowb", [1, 512], BF16),
        "onesrowf": p.din("onesrowf", [1, 128]),
        "idenf": p.din("idenf", [128, 128]),
        "iden": p.din("iden", [128, 128], BF16),
        "gam": p.din("gam", [128, 24]),
        "bet": p.din("bet", [128, 24]),
        "w1": p.din("w1", [8, D, 2816]),
        "w2": p.din("w2", [8, 1408, D]),
    }

    pr = {}
    for nm, shape in [("qwT", [6, D, D]), ("laT", [6, D, 32]), ("lbT", [6, 32, D]),
                      ("bwT", [D, D]), ("cT", [D, D]), ("bpow", [5, D, D]),
                      ("inwTf", [D, 4096]), ("inwTb", [D, 4096]),
                      ("combTf", [2048, D]), ("combTb", [2048, D]),
                      ("xpTf", [2048, 96]), ("xpTb", [2048, 96]),
                      ("dtwTf", [64, 2048]), ("dtwTb", [64, 2048]),
                      ("gwT", [D, 8]), ("brow", [1, 6 * D]),
                      ("dtbrf", [1, 2048]), ("dtbrb", [1, 2048]),
                      ("mfT", [D, D]), ("mbT", [D, D])]:
        pr[nm] = p.dram(nm, shape, BF16)
    pr["sig"] = p.dram("sig", [1, 4], F32)

    with p.scope():
        build_prep(p=p, io=dict(shared, slim=True, **pr))
    p.barrier()

    x1 = p.dram("x1T", [D, W1])
    with p.scope():
        io = dict(shared, oT=x1, qwT=pr["qwT"], laT=pr["laT"], lbT=pr["lbT"],
                  brow=pr["brow"])
        build_ffn(W1, p=p, io=io)
    p.barrier()

    x2 = p.dram("x2T", [D, TCORE])
    with p.scope():
        io = dict(shared, xT=x1, oT=x2, bwT=pr["bwT"], cT=pr["cT"], bpow=pr["bpow"])
        build_attn0(p=p, io=io)
    p.barrier()

    x3 = p.dram("x3T", [D, TCORE])
    with p.scope():
        io = dict(shared, xT=x2, oT=x3, qwT=pr["qwT"], laT=pr["laT"], lbT=pr["lbT"],
                  brow=pr["brow"])
        build_ffn(TCORE, p=p, io=io)
    p.barrier()

    x4 = p.dram("x4T", [D, TCORE])
    xno = p.dram("xnoT", [D, TCORE])
    vsl = p.dram("vsl", [D, TCORE], BF16)
    with p.scope():
        io = dict(shared, xT=x3, oT=x4, xnoT=xno, vT=vsl, gwT=pr["gwT"])
        build_moe("first", p=p, io=io)
    p.barrier()

    v4 = p.dram("v4", [4 * D, TCORE], BF16)

    def cc1(gpsimd):
        p.wait(gpsimd, "so", "v", "s", "pe")
        p.inc(gpsimd.collective_compute("AllGather", ALU.bypass,
                                        replica_groups=RG4,
                                        ins=[vsl.opt()], outs=[v4.opt()]), "cc", 1)
    p.G(cc1)
    p.barrier()

    xi_s = p.dram("xi_s", [2, 512, TSEQ])
    z_s = p.dram("z_s", [2, 512, TSEQ])
    pp_part = p.dram("pp_part", [2, 96, TSEQ])
    pp_sum = p.dram("pp_sum", [2, 96, TSEQ])
    with p.scope():
        io = dict(shared, vT4=v4.rearrange("(q a) c -> q a c", q=4),
                  xiT=xi_s, zT=z_s, projp=pp_part,
                  inwraw_f=p.din("inwraw_f", [D, 1024]),
                  inwraw_b=p.din("inwraw_b", [D, 1024]),
                  xpraw_f=p.din("xpraw_f", [96, 512]),
                  xpraw_b=p.din("xpraw_b", [96, 512]))
        build_m1a(p=p, io=io)

    def cc2(gpsimd):
        p.wait(gpsimd, "so", "v", "s", "pe")
        p.inc(gpsimd.collective_compute("AllReduce", ALU.add,
                                        replica_groups=RG4,
                                        ins=[pp_part.rearrange("a b c -> (a b) c").opt()],
                                        outs=[pp_sum.rearrange("a b c -> (a b) c").opt()]),
              "cc", 1)
    p.G(cc2)
    p.barrier()

    yperm = p.dram("yperm", [4 * D, TCORE])
    with p.scope():
        io = dict(shared, xiT=xi_s, zT=z_s, pp_sum=pp_sum,
                  yperm=yperm.rearrange("(q a) c -> q a c", q=4),
                  dtwraw_f=p.din("dtwraw_f", [512, 64]),
                  dtwraw_b=p.din("dtwraw_b", [512, 64]),
                  dtbraw_f=p.din("dtbraw_f", [1, 512]),
                  dtbraw_b=p.din("dtbraw_b", [1, 512]),
                  outwraw_f=p.din("outwraw_f", [D, 512]),
                  outwraw_b=p.din("outwraw_b", [D, 512]),
                  mfT=pr["mfT"], mbT=pr["mbT"])
        build_m1b(p=p, io=io)

    y_rs = p.dram("y_rs", [D, TCORE])

    def cc3(gpsimd):
        p.wait(gpsimd, "so", "v", "s", "pe")
        p.inc(gpsimd.collective_compute("ReduceScatter", ALU.add,
                                        replica_groups=RG4,
                                        ins=[yperm.opt()], outs=[y_rs.opt()]), "cc", 1)
    p.G(cc3)
    p.barrier()

    with p.scope():
        io = dict(shared, xT=x4, part1=y_rs, xnoT=xno, gwT=pr["gwT"])
        build_moe("second", p=p, io=io)
    return p.done()


def fused_inputs(z):
    x = np.asarray(z["L0_attn_x"])
    xT = np.ascontiguousarray(x.reshape(16384, 1024).T)
    xsl = x_slices_halo(xT)
    base = prep_inputs(z)
    cwk = {0: z["L1_attn_fwd_conv_w"], 1: z["L1_attn_bwd_conv_w"]}
    cbk = {0: z["L1_attn_fwd_conv_b"], 1: z["L1_attn_bwd_conv_b"]}
    alk = {0: z["L1_attn_fwd_A_log"], 1: z["L1_attn_bwd_A_log"]}
    ddk = {0: z["L1_attn_fwd_D"], 1: z["L1_attn_bwd_D"]}
    maps = []
    for c in range(NCORE):
        b, s4 = divmod(c, 4)
        lo = 512 * s4
        m = dict(base)
        m.update({
            "xT": xsl[c],
            "umask": (np.zeros((128, HALO), np.float32) if s4 == 0
                      else np.ones((128, HALO), np.float32)),
            "onescolb": np.ones((128, 1), dtype=BF),
            "onesrowb": np.ones((1, 512), dtype=BF),
            "onesrowf": np.ones((1, 128), dtype=np.float32),
            "idenf": np.eye(128, dtype=np.float32),
            "gam": np.concatenate([pcol(z["L0_ffn_ln_gamma"][bb], 8) for bb in range(3)], 1),
            "bet": np.concatenate([pcol(z["L0_ffn_ln_beta"][bb], 8) for bb in range(3)], 1),
            "w1": z["L1_ffn_w1"], "w2": z["L1_ffn_w2"],
            "lng": pcol(z["L1_attn_ln_g"], 8),
            "lnb": pcol(z["L1_attn_ln_b"], 8),
            "mergeb": pcol(z["L1_attn_merge_b"], 8),
        })
        inwk = {0: z["L1_attn_fwd_in_w"], 1: z["L1_attn_bwd_in_w"]}
        xpk = {0: z["L1_attn_fwd_xproj_w"], 1: z["L1_attn_bwd_xproj_w"]}
        dtwk = {0: z["L1_attn_fwd_dt_w"], 1: z["L1_attn_bwd_dt_w"]}
        dtbk = {0: z["L1_attn_fwd_dt_b"], 1: z["L1_attn_bwd_dt_b"]}
        owk = {0: z["L1_attn_fwd_out_w"], 1: z["L1_attn_bwd_out_w"]}
        for d, tag in ((0, "f"), (1, "b")):
            cwr = cwk[d][lo:lo + 512]
            m[f"convw{tag}"] = np.ascontiguousarray(
                cwr.reshape(4, 128, 4).transpose(1, 0, 2).reshape(128, 16)).astype(np.float32)
            m[f"convb{tag}"] = pcol(cbk[d][lo:lo + 512], 4)
            al = alk[d][lo:lo + 512]
            m[f"al{tag}"] = np.ascontiguousarray(
                al.reshape(4, 128, 16).transpose(1, 0, 2).reshape(128, 64)).astype(np.float32)
            m[f"dd{tag}"] = pcol(ddk[d][lo:lo + 512], 4)
            m[f"inwraw_{tag}"] = np.ascontiguousarray(np.concatenate(
                [inwk[d][lo:lo + 512, :], inwk[d][2048 + lo:2048 + lo + 512, :]], 0))
            m[f"xpraw_{tag}"] = np.ascontiguousarray(xpk[d][:, lo:lo + 512])
            m[f"dtwraw_{tag}"] = np.ascontiguousarray(dtwk[d][lo:lo + 512, :])
            m[f"dtbraw_{tag}"] = np.ascontiguousarray(dtbk[d][lo:lo + 512].reshape(1, 512))
            m[f"outwraw_{tag}"] = np.ascontiguousarray(owk[d][:, lo:lo + 512])
        maps.append(m)
    return maps


FUSED = True


def kernel(**inputs):
    z = inputs
    if FUSED:
        nc = build_fused()
        maps = fused_inputs(z)
        res = run_bass_kernel_spmd(nc, maps, core_ids=list(range(NCORE)))
        outs = [np.asarray(r["oT"]) for r in res.results]
        full = np.concatenate(outs, axis=1)
        return np.ascontiguousarray(full.T).reshape(2, 8192, 1024)
    return kernel_multi(**inputs)


# version 10
# speedup vs baseline: 1.4994x; 1.0835x over previous
import contextlib
import numpy as np
import ml_dtypes
import concourse.bass as bass
import concourse.mybir as mybir
from concourse.bass_utils import run_bass_kernel_spmd

F32 = mybir.dt.float32
BF16 = mybir.dt.bfloat16
I32 = mybir.dt.int32
AF = mybir.ActivationFunctionType
ALU = mybir.AluOpType
AX = mybir.AxisListType.X

NCORE = 8
D = 1024
TCORE = 2048
HALO = 64
W1 = HALO + TCORE
BF = ml_dtypes.bfloat16
EPS = float(1e-5)


def chunks_of(width, step=512):
    out = []
    o = 0
    while o < width:
        n = min(step, width - o)
        out.append((o, n))
        o += n
    return out


class P:

    def __init__(self, ndev=1):
        self.nc = bass.Bass("TRN2", target_bir_lowering=False, debug=False,
                            num_devices=ndev)
        self.st = contextlib.ExitStack()
        self.root_st = self.st
        self.cnt = {}
        self.sems = {}
        self._names = {}
        self.block = self.st.enter_context(self.nc.Block())

    def _uname(self, name):
        k = self._names.get(name, 0)
        self._names[name] = k + 1
        return name if k == 0 else f"{name}_{k}"

    @contextlib.contextmanager
    def scope(self):
        old = self.st
        self.st = contextlib.ExitStack()
        try:
            yield
        finally:
            self.st.close()
            self.st = old

    def dram(self, name, shape, dt=F32):
        return self.nc.dram_tensor(self._uname("dr_" + name), shape, dt,
                                   kind="Internal").ap()

    def barrier(self):
        names = [nm for nm in self.cnt if self.cnt[nm] > 0]
        for sec in (self.T, self.V, self.S, self.SY, self.G):
            def w(eng, names=tuple(names)):
                self.wait(eng, *names)
            sec(w)

    def sem(self, name):
        if name not in self.sems:
            self.sems[name] = self.st.enter_context(self.nc.semaphore(name))
            self.cnt[name] = 0
        return self.sems[name]

    def sb(self, name, shape, dt=F32):
        return self.st.enter_context(
            self.nc.sbuf_tensor(self._uname("sb_" + name), shape, dt))[:]

    def ps(self, name, shape, dt=F32):
        return self.st.enter_context(
            self.nc.psum_tensor(self._uname("ps_" + name), shape, dt))[:]

    io = None

    def din(self, name, shape, dt=F32):
        if self.io is not None and name in self.io:
            return self.io[name]
        return self.nc.dram_tensor(self._uname(name), shape, dt,
                                   kind="ExternalInput").ap()

    def dout(self, name, shape, dt=F32):
        if self.io is not None and name in self.io:
            return self.io[name]
        return self.nc.dram_tensor(self._uname(name), shape, dt,
                                   kind="ExternalOutput").ap()

    def inc(self, ins, name, n=1):
        ins.then_inc(self.sem(name), n)
        self.cnt[name] += n
        return ins

    def cop(self, eng, name, ins):
        self.inc(ins, name)
        eng.wait_ge(self.sems[name], self.cnt[name])
        return ins

    def wait_tok(self, eng, name, count):
        if count > 0:
            eng.wait_ge(self.sem(name), count)

    def wait(self, eng, *names):
        for nm in names:
            if nm in self.cnt and self.cnt[nm] > 0:
                eng.wait_ge(self.sem(nm), self.cnt[nm])

    def T(self, emit):
        @self.block.tensor
        def _(tensor):
            emit(tensor)

    def V(self, emit):
        @self.block.vector
        def _(vector):
            emit(vector)

    def S(self, emit):
        @self.block.scalar
        def _(scalar):
            emit(scalar)

    def SY(self, emit):
        @self.block.sync
        def _(sync):
            emit(sync)

    def G(self, emit):
        @self.block.gpsimd
        def _(gpsimd):
            emit(gpsimd)

    def done(self):
        self.st.close()
        return self.nc



def build_prep(p=None, io=None):
    standalone = p is None
    if standalone:
        p = P()
    p.io = io or {}
    i_WA = p.din("WA", [D, D]); i_WB = p.din("WB", [D, D]); i_WC = p.din("WC", [D, D])
    i_qw = p.din("qw", [6, D, D], I32)
    i_qs = p.din("qs", [6, D, 64])
    i_la = p.din("la", [6, 32, D])
    i_lb = p.din("lb", [6, D, 32])
    i_inwf = p.din("inwf", [4096, D]); i_inwb = p.din("inwb", [4096, D])
    i_outwf = p.din("outwf", [D, 2048]); i_outwb = p.din("outwb", [D, 2048])
    i_mw = p.din("mw", [D, 2048])
    i_xpf = p.din("xpf", [96, 2048]); i_xpb = p.din("xpb", [96, 2048])
    i_dtf = p.din("dtf", [2048, 64]); i_dtb_w = p.din("dtb_w", [2048, 64])
    i_gw = p.din("gw", [8, D])
    i_bias = p.din("bias", [6, D])
    i_dtbf = p.din("dtbf", [2048]); i_dtbb = p.din("dtbb", [2048])
    i_iden = p.din("iden", [128, 128], BF16)
    i_ones = p.din("onescol", [128, 1], BF16)
    i_onesf = p.din("onescolf", [128, 1], F32)
    i_onesrf = p.din("onesrowf", [1, 128], F32)

    o_qwT = p.dout("qwT", [6, D, D], BF16)
    o_laT = p.dout("laT", [6, D, 32], BF16)
    o_lbT = p.dout("lbT", [6, 32, D], BF16)
    o_bwT = p.dout("bwT", [D, D], BF16)
    o_cT = p.dout("cT", [D, D], BF16)
    o_bpow = p.dout("bpow", [5, D, D], BF16)
    o_inwTf = p.dout("inwTf", [D, 4096], BF16)
    o_inwTb = p.dout("inwTb", [D, 4096], BF16)
    o_combTf = p.dout("combTf", [2048, D], BF16)
    o_combTb = p.dout("combTb", [2048, D], BF16)
    o_xpTf = p.dout("xpTf", [2048, 96], BF16)
    o_xpTb = p.dout("xpTb", [2048, 96], BF16)
    o_dtwTf = p.dout("dtwTf", [64, 2048], BF16)
    o_dtwTb = p.dout("dtwTb", [64, 2048], BF16)
    o_gwT = p.dout("gwT", [D, 8], BF16)
    o_brow = p.dout("brow", [1, 6 * D], BF16)
    o_dtbrf = p.dout("dtbrf", [1, 2048], BF16)
    o_dtbrb = p.dout("dtbrb", [1, 2048], BF16)
    o_sig = p.dout("sig", [1, 4])

    src = p.sb("src", [128, 16 * 1024])
    srcb = p.sb("srcb", [128, 16 * 1024], BF16)
    dst = p.sb("dstb", [128, 16 * 1024], BF16)
    iden = p.sb("iden", [128, 128], BF16)
    onescol = p.sb("onescol", [128, 1], BF16)
    onescol_f = p.sb("onescol_f", [128, 1], F32)
    onesrow_f = p.sb("onesrow_f", [1, 128], F32)
    wbf = p.sb("wbf", [128, 8 * 1024], BF16)
    wf32 = src[:, 0:8 * 1024]
    dqf32 = src[:, 8 * 1024:16 * 1024]
    mp = p.sb("mp", [128, 8 * 1024], BF16)
    b1 = p.sb("b1", [128, 8 * 1024], BF16)
    b2 = p.sb("b2", [128, 8 * 1024], BF16)
    vec = p.sb("vecs", [128, 64])
    vecb = p.sb("vecsb", [128, 16], BF16)
    row = p.sb("rowsc", [1, 64])
    isig = p.sb("isig", [128, 4])
    qscale = p.sb("qscale", [128, 8 * 64])
    ps0 = p.ps("ps0", [128, 512])
    ps1 = p.ps("ps1", [128, 512])
    psT = p.ps("psT", [128, 128], BF16)
    psT2 = p.ps("psT2", [128, 128], BF16)
    psv = p.ps("psv", [128, 8])

    def din_dma(dst_ap, src_ap, wait=("v", "pe")):
        def f(sync):
            p.wait(sync, *wait)
            p.inc(sync.dma_start(out=dst_ap, in_=src_ap), "sy", 16)
        p.SY(f)

    def dout_dma(dst_ap, src_ap):
        def f(sync):
            p.wait(sync, "v")
            p.inc(sync.dma_start(out=dst_ap, in_=src_ap), "so", 16)
        p.SY(f)

    din_dma(iden, i_iden, wait=())
    din_dma(onescol, i_ones, wait=())
    din_dma(onescol_f, i_onesf, wait=())
    din_dma(onesrow_f, i_onesrf, wait=())

    def transpose_job(mat_in, R, C, out_dram, scale_ap=None):
        if R > 2048:
            for g0 in range(0, R, 2048):
                transpose_job_part(mat_in[g0:g0 + 2048, :], 2048, C,
                                   out_dram[:, g0:g0 + 2048], scale_ap)
        else:
            transpose_job_part(mat_in, R, C, out_dram, scale_ap)

    def transpose_job_part(mat_in, R, C, out_dram, scale_ap=None):
        rtiles = [(i * 128, min(128, R - i * 128)) for i in range((R + 127) // 128)]
        ctiles = [(i * 128, min(128, C - i * 128)) for i in range((C + 127) // 128)]
        nr = len(rtiles)
        Crnd = len(ctiles) * 128
        if R >= 128:
            din_dma(src[:, 0:nr * C].rearrange("p (a c) -> p a c", a=nr),
                    mat_in.rearrange("(a p) c -> p a c", p=128), wait=("v", "so"))
        else:
            din_dma(src[0:R, 0:C], mat_in, wait=("v", "so"))

        def cast(vector):
            p.wait(vector, "sy")
            if scale_ap is None:
                p.cop(vector, "v", vector.tensor_copy(out=srcb[0:min(R, 128), 0:nr * C],
                                         in_=src[0:min(R, 128), 0:nr * C]))
            else:
                p.cop(vector, "v", vector.tensor_scalar(out=srcb[0:min(R, 128), 0:nr * C],
                                           in0=src[0:min(R, 128), 0:nr * C],
                                           scalar1=scale_ap, scalar2=None,
                                           op0=ALU.mult))
        p.V(cast)

        t_v = {-2: p.cnt["v"], -1: p.cnt["v"]}
        t_pe = {}
        gi = 0
        for r, (r0, rd) in enumerate(rtiles):
            for (c0, cd) in ctiles:
                pst = psT if gi % 2 == 0 else psT2

                def tmm(tensor, r=r, rd=rd, c0=c0, cd=cd, gi=gi, pst=pst):
                    if gi < 2:
                        p.wait(tensor, "v")
                    else:
                        p.wait_tok(tensor, "v", t_v[gi - 2])
                    p.inc(tensor.transpose(pst[0:cd, 0:rd],
                                           srcb[0:rd, r * C + c0: r * C + c0 + cd],
                                           iden[0:rd, 0:rd]), "pe")
                    t_pe[gi] = p.cnt["pe"]
                p.T(tmm)

                def evac(vector, r=r, rd=rd, c0=c0, cd=cd, gi=gi, pst=pst):
                    p.wait_tok(vector, "pe", t_pe[gi])
                    p.cop(vector, "v", vector.tensor_copy(
                        out=dst[:, 0:len(ctiles) * ((nr * 128))].rearrange(
                            "p (a q) -> p a q", a=len(ctiles))[0:cd, c0 // 128,
                                                              r * 128: r * 128 + rd],
                        in_=pst[0:cd, 0:rd]))
                    t_v[gi] = p.cnt["v"]
                p.V(evac)
                gi += 1

        Rrnd = nr * 128
        na = len(ctiles)
        if C % 128 == 0:
            dout_dma(out_dram.rearrange("(a p) q -> p a q", p=128),
                     dst[:, 0:na * Rrnd].rearrange("p (a q) -> p a q", a=na)[:, :, 0:R])
        else:
            assert na == 1
            dout_dma(out_dram, dst[0:C, 0:R])

    def mm_square(out_sb, lh, rh):
        sq_v = {-2: p.cnt["v"], -1: p.cnt["v"]}
        sq_pe = {}
        for g, (di, half) in enumerate([(a_, b_) for a_ in range(8) for b_ in range(2)]):
            pss = ps0 if g % 2 == 0 else ps1

            def tmm(tensor, di=di, half=half, pss=pss, g=g):
                if g < 2:
                    p.wait(tensor, "v")
                else:
                    p.wait_tok(tensor, "v", sq_v[g - 2])
                for k in range(8):
                    ins = tensor.matmul(
                        pss,
                        lh[:, k * 1024 + di * 128: k * 1024 + (di + 1) * 128],
                        rh[:, k * 1024 + half * 512: k * 1024 + half * 512 + 512],
                        start=(k == 0), stop=(k == 7))
                p.inc(ins, "pe")
                sq_pe[g] = p.cnt["pe"]
            p.T(tmm)

            def evac(vector, di=di, half=half, pss=pss, g=g):
                p.wait_tok(vector, "pe", sq_pe[g])
                p.cop(vector, "v", vector.tensor_copy(
                    out=out_sb[:, di * 1024 + half * 512: di * 1024 + half * 512 + 512],
                    in_=pss))
                sq_v[g] = p.cnt["v"]
            p.V(evac)

    def sigma_chain(i_W, jcol):
        din_dma(wf32.rearrange("p (a c) -> p a c", a=8),
                i_W.rearrange("(a p) c -> p a c", p=128), wait=("v", "pe"))

        def cast(vector):
            p.wait(vector, "sy")
            p.cop(vector, "v", vector.tensor_copy(out=wbf, in_=wf32))
        p.V(cast)

        mm_square(mp, wbf, wbf)

        def t0mm(tensor):
            p.wait(tensor, "v")
            for di in range(8):
                for k in range(8):
                    ins = tensor.matmul(
                        psv[:, di:di + 1],
                        wbf[:, k * 1024 + di * 128: k * 1024 + (di + 1) * 128],
                        onescol, start=(k == 0), stop=(k == 7))
                p.inc(ins, "pe")
        p.T(t0mm)

        def t0ev(vector):
            p.wait(vector, "pe")
            p.cop(vector, "v", vector.tensor_copy(out=vec[:, 0:8], in_=psv[:, 0:8]))
            p.cop(vector, "v", vector.tensor_copy(out=vecb[:, 0:8], in_=vec[:, 0:8]))
        p.V(t0ev)

        def matvec(mat, outcol):
            def tmv(tensor):
                p.wait(tensor, "v")
                for di in range(8):
                    for k in range(8):
                        ins = tensor.matmul(
                            psv[:, di:di + 1],
                            mat[:, k * 1024 + di * 128: k * 1024 + (di + 1) * 128],
                            vecb[:, k:k + 1],
                            start=(k == 0), stop=(k == 7))
                    p.inc(ins, "pe")
            p.T(tmv)

            def emv(vector):
                p.wait(vector, "pe")
                p.cop(vector, "v", vector.tensor_copy(out=vec[:, outcol:outcol + 8],
                                         in_=psv[:, 0:8]))
                p.cop(vector, "v", vector.tensor_copy(out=vecb[:, 8:16],
                                         in_=vec[:, outcol:outcol + 8]))
                p.cop(vector, "v", vector.tensor_copy(out=vecb[:, 0:8], in_=vecb[:, 8:16]))
            p.V(emv)

        matvec(mp, 8)
        mm_square(b1, mp, mp)
        mm_square(b2, b1, b1)
        mm_square(b1, b2, b2)
        matvec(b1, 16)
        matvec(mp, 24)

        def dots(vector):
            p.cop(vector, "v", vector.tensor_mul(out=vec[:, 32:40], in0=vec[:, 24:32],
                                    in1=vec[:, 24:32]))
            p.cop(vector, "v", vector.tensor_mul(out=vec[:, 40:48], in0=vec[:, 16:24],
                                    in1=vec[:, 24:32]))
            p.cop(vector, "v", vector.reduce_sum(out=vec[:, 48:49], in_=vec[:, 32:40], axis=AX))
            p.cop(vector, "v", vector.reduce_sum(out=vec[:, 49:50], in_=vec[:, 40:48], axis=AX))
        p.V(dots)

        for col, dstc in ((48, 0), (49, 1)):
            def psum_sum(tensor, col=col):
                p.wait(tensor, "v")
                p.inc(tensor.matmul(psv[0:1, 0:1], vec[:, col:col + 1], onescol_f,
                                    start=True, stop=True), "pe")
            p.T(psum_sum)

            def getv(vector, dstc=dstc):
                p.wait(vector, "pe")
                p.cop(vector, "v", vector.tensor_copy(out=row[0:1, dstc:dstc + 1],
                                         in_=psv[0:1, 0:1]))
            p.V(getv)

        def sigmath(vector):
            p.cop(vector, "v", vector.reciprocal(out=row[0:1, 2:3], in_=row[0:1, 1:2]))
            p.cop(vector, "v", vector.tensor_mul(out=row[0:1, 3:4], in0=row[0:1, 0:1],
                                    in1=row[0:1, 2:3]))
        p.V(sigmath)

        def sq(scalar):
            p.wait(scalar, "v")
            p.cop(scalar, "s", scalar.activation(out=row[0:1, 4:5], in_=row[0:1, 3:4],
                                    func=AF.Sqrt))
        p.S(sq)

        def inv(vector):
            p.wait(vector, "s")
            p.cop(vector, "v", vector.reciprocal(out=row[0:1, 5:6], in_=row[0:1, 4:5]))
            p.cop(vector, "v", vector.tensor_copy(out=row[0:1, 8 + jcol: 9 + jcol],
                                     in_=row[0:1, 4:5]))
        p.V(inv)

        def bcast(tensor):
            p.wait(tensor, "v")
            p.inc(tensor.matmul(psv[:, 0:1], onesrow_f, row[0:1, 5:6],
                                start=True, stop=True), "pe")
        p.T(bcast)

        def getb(vector):
            p.wait(vector, "pe")
            p.cop(vector, "v", vector.tensor_copy(out=isig[:, jcol:jcol + 1], in_=psv[:, 0:1]))
        p.V(getb)

    sigma_chain(i_WB, 1)
    transpose_job(i_WB, D, D, o_bwT, scale_ap=isig[:, 1:2])
    sigma_chain(i_WC, 2)
    transpose_job(i_WC, D, D, o_cT, scale_ap=isig[:, 2:3])
    sigma_chain(i_WA, 0)

    def mkA(vector):
        p.cop(vector, "v", vector.tensor_scalar(out=b2, in0=wf32, scalar1=isig[:, 0:1],
                                   scalar2=None, op0=ALU.mult))
    p.V(mkA)

    for r in range(8):
        for c in range(8):
            def tmm(tensor, r=r, c=c):
                p.wait(tensor, "v")
                p.inc(tensor.transpose(
                    psT, b2[:, r * 1024 + c * 128: r * 1024 + (c + 1) * 128],
                    iden), "pe")
            p.T(tmm)

            def evac(vector, r=r, c=c):
                p.wait(vector, "pe")
                p.cop(vector, "v", vector.tensor_copy(
                    out=b1[:, c * 1024 + r * 128: c * 1024 + (r + 1) * 128],
                    in_=psT))
            p.V(evac)

    dout_dma(o_bpow[0].rearrange("(a p) c -> p a c", p=128),
             b1.rearrange("p (a c) -> p a c", a=8))

    abuf, bbuf, t1buf, t2buf = b2, b1, mp, wbf.bitcast(BF16)[:, 0:8 * 1024]
    t2buf = wbf
    for lvl in range(1, 5):
        mm_square(t1buf, bbuf, abuf)
        mm_square(t2buf, abuf, bbuf)
        abuf, bbuf, t1buf, t2buf = t1buf, t2buf, abuf, bbuf
        dout_dma(o_bpow[lvl].rearrange("(a p) c -> p a c", p=128),
                 bbuf.rearrange("p (a c) -> p a c", a=8))

    dout_dma(o_sig, row[0:1, 8:12])

    for j in range(6):
        din_dma(src.bitcast(I32)[:, 0:8 * 1024].rearrange("p (a c) -> p a c", a=8),
                i_qw[j].rearrange("(a p) c -> p a c", p=128), wait=("v", "pe", "so"))
        din_dma(qscale.rearrange("p (a c) -> p a c", a=8),
                i_qs[j].rearrange("(a p) c -> p a c", p=128), wait=())

        def dq(vector):
            p.wait(vector, "sy")
            p.cop(vector, "v", vector.tensor_copy(out=dqf32, in_=src.bitcast(I32)[:, 0:8 * 1024]))
            p.cop(vector, "v", vector.tensor_scalar(out=dqf32, in0=dqf32,
                                       scalar1=float(1.0 / 7.5), scalar2=-1.0,
                                       op0=ALU.mult, op1=ALU.add))
            for a in range(8):
                src_ap = dqf32[:, a * 1024:(a + 1) * 1024].rearrange(
                    "p (g o) -> p g o", o=16)
                scl_ap = qscale[:, a * 64:(a + 1) * 64].unsqueeze(2).broadcast_to(
                    (128, 64, 16))
                p.cop(vector, "v", vector.tensor_tensor(
                    out=srcb[:, a * 1024:(a + 1) * 1024].rearrange(
                        "p (g o) -> p g o", o=16),
                    in0=src_ap, in1=scl_ap, op=ALU.mult))
        p.V(dq)

        t_v = {-2: p.cnt["v"], -1: p.cnt["v"]}
        t_pe = {}
        gi = 0
        for r in range(8):
            for c in range(8):
                pst = psT if gi % 2 == 0 else psT2

                def tmm(tensor, r=r, c=c, gi=gi, pst=pst):
                    if gi < 2:
                        p.wait(tensor, "v")
                    else:
                        p.wait_tok(tensor, "v", t_v[gi - 2])
                    p.inc(tensor.transpose(
                        pst, srcb[:, r * 1024 + c * 128: r * 1024 + (c + 1) * 128],
                        iden), "pe")
                    t_pe[gi] = p.cnt["pe"]
                p.T(tmm)

                def evac(vector, r=r, c=c, gi=gi, pst=pst):
                    p.wait_tok(vector, "pe", t_pe[gi])
                    p.cop(vector, "v", vector.tensor_copy(
                        out=dst[:, c * 1024 + r * 128: c * 1024 + (r + 1) * 128],
                        in_=pst))
                    t_v[gi] = p.cnt["v"]
                p.V(evac)
                gi += 1

        dout_dma(o_qwT[j].rearrange("(a p) c -> p a c", p=128),
                 dst[:, 0:8 * 1024].rearrange("p (a c) -> p a c", a=8))

        din_dma(src[0:32, 0:1024], i_la[j], wait=("v", "pe", "so"))
        din_dma(src[0:128, 2048:2048 + 256].rearrange("p (a c) -> p a c", a=8),
                i_lb[j].rearrange("(a p) c -> p a c", p=128), wait=())

        def lcast(vector):
            p.wait(vector, "sy")
            p.cop(vector, "v", vector.tensor_copy(out=srcb[0:32, 0:1024], in_=src[0:32, 0:1024]))
            p.cop(vector, "v", vector.tensor_copy(out=srcb[:, 2048:2048 + 256],
                                     in_=src[:, 2048:2048 + 256]))
        p.V(lcast)

        for c in range(8):
            def tmm(tensor, c=c):
                p.wait(tensor, "v")
                p.inc(tensor.transpose(psT[0:128, 0:32],
                                       srcb[0:32, c * 128:(c + 1) * 128],
                                       iden[0:32, 0:32]), "pe")
            p.T(tmm)

            def evac(vector, c=c):
                p.wait(vector, "pe")
                p.cop(vector, "v", vector.tensor_copy(
                    out=dst[:, 8192 + c * 32: 8192 + (c + 1) * 32],
                    in_=psT[0:128, 0:32]))
            p.V(evac)
        for r in range(8):
            def tmm(tensor, r=r):
                p.wait(tensor, "v")
                p.inc(tensor.transpose(psT[0:32, 0:128],
                                       srcb[:, 2048 + r * 32: 2048 + (r + 1) * 32],
                                       iden), "pe")
            p.T(tmm)

            def evac(vector, r=r):
                p.wait(vector, "pe")
                p.cop(vector, "v", vector.tensor_copy(
                    out=dst[0:32, 9216 + r * 128: 9216 + (r + 1) * 128],
                    in_=psT[0:32, 0:128]))
            p.V(evac)

        dout_dma(o_laT[j].rearrange("(a p) c -> p a c", p=128),
                 dst[:, 8192:8192 + 256].rearrange("p (a c) -> p a c", a=8))
        dout_dma(o_lbT[j], dst[0:32, 9216:9216 + 1024])

    slim = bool((io or {}).get("slim"))
    if not slim:
        transpose_job(i_inwf, 4096, D, o_inwTf)
        transpose_job(i_inwb, 4096, D, o_inwTb)
        transpose_job(i_xpf, 96, 2048, o_xpTf)
        transpose_job(i_xpb, 96, 2048, o_xpTb)
        transpose_job(i_dtf, 2048, 64, o_dtwTf)
        transpose_job(i_dtb_w, 2048, 64, o_dtwTb)
    transpose_job(i_gw, 8, D, o_gwT)

    din_dma(src[:, 0:16 * 1024].rearrange("p (a c) -> p a c", a=8),
            i_mw.rearrange("(a p) c -> p a c", p=128), wait=("v", "pe", "so"))

    def mcast(vector):
        p.wait(vector, "sy")
        p.cop(vector, "v", vector.tensor_copy(out=srcb[:, 0:16 * 1024], in_=src[:, 0:16 * 1024]))
    p.V(mcast)

    for h, out_sb in ((0, b1), (1, b2)):
        for r in range(8):
            for c in range(8):
                def tmm(tensor, h=h, r=r, c=c):
                    p.wait(tensor, "v")
                    p.inc(tensor.transpose(
                        psT,
                        srcb[:, r * 2048 + h * 1024 + c * 128:
                             r * 2048 + h * 1024 + (c + 1) * 128],
                        iden), "pe")
                p.T(tmm)

                def evac(vector, out_sb=out_sb, r=r, c=c):
                    p.wait(vector, "pe")
                    p.cop(vector, "v", vector.tensor_copy(
                        out=out_sb[:, c * 1024 + r * 128: c * 1024 + (r + 1) * 128],
                        in_=psT))
                p.V(evac)

    o_mfT = p.dout("mfT", [D, D], BF16)
    o_mbT = p.dout("mbT", [D, D], BF16)
    dout_dma(o_mfT.rearrange("(a p) c -> p a c", p=128),
             b1.rearrange("p (a c) -> p a c", a=8))
    dout_dma(o_mbT.rearrange("(a p) c -> p a c", p=128),
             b2.rearrange("p (a c) -> p a c", a=8))

    for i_ow, mT, o_comb in (() if slim else
                             ((i_outwf, b1, o_combTf), (i_outwb, b2, o_combTb))):
        din_dma(src[:, 0:16 * 1024].rearrange("p (a c) -> p a c", a=8),
                i_ow.rearrange("(a p) c -> p a c", p=128), wait=("v", "pe", "so"))

        def ocast(vector):
            p.wait(vector, "sy")
            p.cop(vector, "v", vector.tensor_copy(out=srcb[:, 0:16 * 1024],
                                     in_=src[:, 0:16 * 1024]))
        p.V(ocast)

        for ct in range(16):
            for half in range(2):
                pss = ps0 if half == 0 else ps1

                def tmm(tensor, mT=mT, ct=ct, half=half, pss=pss):
                    p.wait(tensor, "v")
                    for k in range(8):
                        ins = tensor.matmul(
                            pss,
                            srcb[:, k * 2048 + ct * 128: k * 2048 + (ct + 1) * 128],
                            mT[:, k * 1024 + half * 512: k * 1024 + half * 512 + 512],
                            start=(k == 0), stop=(k == 7))
                    p.inc(ins, "pe")
                p.T(tmm)

                def evac(vector, ct=ct, half=half, pss=pss):
                    p.wait(vector, "pe")
                    p.cop(vector, "v", vector.tensor_copy(
                        out=dst[:, ct * 1024 + half * 512: ct * 1024 + half * 512 + 512],
                        in_=pss))
                p.V(evac)

        dout_dma(o_comb.rearrange("(a p) c -> p a c", p=128),
                 dst[:, 0:16 * 1024].rearrange("p (a c) -> p a c", a=16))

    din_dma(src[0:1, 0:6 * D], i_bias.rearrange("a b -> (a b)").rearrange(
        "(o a) -> o a", o=1), wait=("v", "pe", "so"))
    din_dma(src[0:1, 8192:8192 + 2048], i_dtbf.rearrange("(o a) -> o a", o=1), wait=())
    din_dma(src[0:1, 12288:12288 + 2048], i_dtbb.rearrange("(o a) -> o a", o=1), wait=())

    def bcastrows(vector):
        p.wait(vector, "sy")
        p.cop(vector, "v", vector.tensor_copy(out=srcb[0:1, 0:6 * D], in_=src[0:1, 0:6 * D]))
        p.cop(vector, "v", vector.tensor_copy(out=srcb[0:1, 8192:8192 + 2048],
                                 in_=src[0:1, 8192:8192 + 2048]))
        p.cop(vector, "v", vector.tensor_copy(out=srcb[0:1, 12288:12288 + 2048],
                                 in_=src[0:1, 12288:12288 + 2048]))
    p.V(bcastrows)

    dout_dma(o_brow, srcb[0:1, 0:6 * D])
    dout_dma(o_dtbrf, srcb[0:1, 8192:8192 + 2048])
    dout_dma(o_dtbrb, srcb[0:1, 12288:12288 + 2048])

    def fin(gpsimd):
        p.wait(gpsimd, "so")
    p.G(fin)

    return p.done()


def prep_inputs(z):
    return {
        "WA": z["L0_attn_W_A"], "WB": z["L0_attn_W_B"], "WC": z["L0_attn_W_C"],
        "qw": z["L0_ffn_qweights"], "qs": z["L0_ffn_scales"],
        "la": z["L0_ffn_lora_a"], "lb": z["L0_ffn_lora_b"],
        "inwf": z["L1_attn_fwd_in_w"], "inwb": z["L1_attn_bwd_in_w"],
        "outwf": z["L1_attn_fwd_out_w"], "outwb": z["L1_attn_bwd_out_w"],
        "mw": z["L1_attn_merge_w"],
        "xpf": z["L1_attn_fwd_xproj_w"], "xpb": z["L1_attn_bwd_xproj_w"],
        "dtf": z["L1_attn_fwd_dt_w"], "dtb_w": z["L1_attn_bwd_dt_w"],
        "gw": z["L1_ffn_gate_w"],
        "bias": z["L0_ffn_biases"],
        "dtbf": z["L1_attn_fwd_dt_b"], "dtbb": z["L1_attn_bwd_dt_b"],
        "iden": np.eye(128, dtype=BF),
        "onescol": np.ones((128, 1), dtype=BF),
        "onescolf": np.ones((128, 1), dtype=np.float32),
        "onesrowf": np.ones((1, 128), dtype=np.float32),
    }


def run_prep(z):
    nc = build_prep()
    res = run_bass_kernel_spmd(nc, [prep_inputs(z)], core_ids=[0])
    return res.results[0]






def build_ffn(W, p=None, io=None):
    standalone = p is None
    if standalone:
        p = P()
    p.io = io or {}
    i_x = p.din("xT", [D, W])
    i_qwT = p.din("qwT", [6, D, D], BF16)
    i_laT = p.din("laT", [6, D, 32], BF16)
    i_lbT = p.din("lbT", [6, 32, D], BF16)
    i_brow = p.din("brow", [1, 6 * D], BF16)
    i_gam = p.din("gam", [128, 24])
    i_bet = p.din("bet", [128, 24])
    i_ocb = p.din("onescolb", [128, 1], BF16)
    i_orb = p.din("onesrowb", [1, 512], BF16)
    i_orf = p.din("onesrowf", [1, 128])
    o_x = p.dout("oT", [D, W])

    qw = p.sb("qw", [128, 6 * 8 * 1024], BF16)
    laT = p.sb("laT", [128, 6 * 8 * 32], BF16)
    lbT = p.sb("lbT", [32, 6 * 1024], BF16)
    brow = p.sb("brow", [1, 6 * D], BF16)
    gam = p.sb("gam", [128, 24])
    bet = p.sb("bet", [128, 24])
    ocb = p.sb("ocb", [128, 1], BF16)
    orb = p.sb("orb", [1, 512], BF16)
    orf = p.sb("orf", [1, 128])
    xc = p.sb("xc", [128, 8 * 512])
    h = p.sb("h", [128, 8 * 512])
    y1 = p.sb("y1", [128, 8 * 512])
    hb = p.sb("hb", [128, 8 * 512], BF16)
    gb = p.sb("gb", [128, 8 * 512], BF16)
    sqb = gb
    t1 = p.sb("t1", [128, 512])
    ub = p.sb("ub", [32, 512], BF16)
    rows = p.sb("rows", [1, 5 * 512])
    rowb = p.sb("rowb", [1, 2 * 512], BF16)
    orfb = p.sb("orfb", [1, 128], BF16)
    cst = p.sb("cst", [1, 8])
    psA = p.ps("psA", [128, 512])
    psA2 = p.ps("psA2", [128, 512])
    psU = p.ps("psU", [32, 512])
    psS1 = p.ps("psS1", [1, 512])
    psS2 = p.ps("psS2", [1, 512])
    psBr = p.ps("psBr", [128, 512])
    psBm = p.ps("psBm", [128, 512])

    def ldma(sync, dst_ap, src_ap, *wait):
        p.wait(sync, *wait)
        p.inc(sync.dma_start(out=dst_ap, in_=src_ap), "sy", 16)

    def const_loads(sync):
        for j in range(6):
            ldma(sync, qw[:, j * 8192:(j + 1) * 8192].rearrange("p (a c) -> p a c", a=8),
                 i_qwT[j].rearrange("(a p) c -> p a c", p=128))
            ldma(sync, laT[:, j * 256:(j + 1) * 256].rearrange("p (a c) -> p a c", a=8),
                 i_laT[j].rearrange("(a p) c -> p a c", p=128))
            ldma(sync, lbT[:, j * 1024:(j + 1) * 1024], i_lbT[j])
        ldma(sync, brow, i_brow)
        ldma(sync, gam, i_gam)
        ldma(sync, bet, i_bet)
        ldma(sync, ocb, i_ocb)
        ldma(sync, orb, i_orb)
        ldma(sync, orf, i_orf)
    p.SY(const_loads)

    def mkconst(vector):
        p.cop(vector, "v", vector.memset(cst[0:1, 0:1], EPS))
        p.cop(vector, "v", vector.memset(cst[0:1, 1:2], -0.5))
        p.cop(vector, "v", vector.memset(orfb, 1.0))
    p.V(mkconst)

    def layernorm_from(src_sb, n, out_f32, out_b16, gj=None):
        def c1(vector):
            p.wait(vector, "pe", "s")
            p.cop(vector, "v", vector.tensor_copy(out=hb[:, 0:8 * n], in_=src_sb[:, 0:8 * n]))
            p.cop(vector, "v", vector.tensor_mul(out=sqb[:, 0:8 * n], in0=hb[:, 0:8 * n],
                                                 in1=hb[:, 0:8 * n]))
        p.V(c1)

        def stats(tensor):
            p.wait(tensor, "v")
            for k in range(8):
                i0 = tensor.matmul(psS1[0:1, 0:n], ocb, hb[:, k * n: (k + 1) * n],
                                   start=(k == 0), stop=(k == 7))
                i1 = tensor.matmul(psS2[0:1, 0:n], ocb, sqb[:, k * n: (k + 1) * n],
                                   start=(k == 0), stop=(k == 7))
            p.inc(i0, "pe")
            p.inc(i1, "pe")
        p.T(stats)

        def rowmath(vector):
            p.wait(vector, "pe")
            p.cop(vector, "v", vector.tensor_scalar(out=rows[0:1, 0:n], in0=psS1[0:1, 0:n],
                                                    scalar1=float(1.0 / D), scalar2=None,
                                                    op0=ALU.mult))
            p.cop(vector, "v", vector.tensor_scalar(out=rows[0:1, 512:512 + n],
                                                    in0=psS2[0:1, 0:n],
                                                    scalar1=float(1.0 / D), scalar2=None,
                                                    op0=ALU.mult))
            p.cop(vector, "v", vector.tensor_mul(out=rows[0:1, 1024:1024 + n],
                                                 in0=rows[0:1, 0:n], in1=rows[0:1, 0:n]))
            p.cop(vector, "v", vector.tensor_sub(out=rows[0:1, 512:512 + n],
                                                 in0=rows[0:1, 512:512 + n],
                                                 in1=rows[0:1, 1024:1024 + n]))
        p.V(rowmath)

        def rowact(scalar):
            p.wait(scalar, "v")
            p.cop(scalar, "s", scalar.activation(out=rows[0:1, 1024:1024 + n],
                                                 in_=rows[0:1, 512:512 + n],
                                                 func=AF.Ln, bias=cst[0:1, 0:1]))
            p.cop(scalar, "s", scalar.activation(out=rows[0:1, 1536:1536 + n],
                                                 in_=rows[0:1, 1024:1024 + n],
                                                 func=AF.Exp, scale=cst[0:1, 1:2]))
        p.S(rowact)

        def rown(vector):
            p.wait(vector, "s")
            p.cop(vector, "v", vector.tensor_mul(out=rows[0:1, 1024:1024 + n],
                                                 in0=rows[0:1, 0:n],
                                                 in1=rows[0:1, 1536:1536 + n]))
            p.cop(vector, "v", vector.tensor_scalar(out=rows[0:1, 2048:2048 + n],
                                                    in0=rows[0:1, 1024:1024 + n],
                                                    scalar1=-1.0, scalar2=None,
                                                    op0=ALU.mult))
            p.cop(vector, "v", vector.tensor_copy(out=rowb[0:1, 0:n],
                                                  in_=rows[0:1, 1536:1536 + n]))
            p.cop(vector, "v", vector.tensor_copy(out=rowb[0:1, 512:512 + n],
                                                  in_=rows[0:1, 2048:2048 + n]))
        p.V(rown)

        def bcast(tensor):
            p.wait(tensor, "v")
            p.inc(tensor.matmul(psBr[:, 0:n], orfb[0:1, :], rowb[0:1, 0:n],
                                start=True, stop=True), "pe")
            p.inc(tensor.matmul(psBm[:, 0:n], orfb[0:1, :], rowb[0:1, 512:512 + n],
                                start=True, stop=True), "pe")
        p.T(bcast)

        def apply(vector):
            p.wait(vector, "pe")
            for dt_i in range(8):
                sl = slice(dt_i * n, (dt_i + 1) * n)
                p.cop(vector, "v", vector.tensor_mul(out=t1[:, 0:n], in0=src_sb[:, sl],
                                                     in1=psBr[:, 0:n]))
                if out_f32 is not None:
                    p.cop(vector, "v", vector.tensor_add(out=out_f32[:, sl],
                                                         in0=t1[:, 0:n],
                                                         in1=psBm[:, 0:n]))
                    p.cop(vector, "v", vector.tensor_copy(out=out_b16[:, sl],
                                                          in_=out_f32[:, sl]))
                else:
                    p.cop(vector, "v", vector.tensor_add(out=y1[:, sl],
                                                         in0=t1[:, 0:n],
                                                         in1=psBm[:, 0:n]))
                    def gl(scalar, dt_i=dt_i, sl=sl):
                        p.wait(scalar, "v")
                        p.cop(scalar, "s", scalar.activation(
                            out=gb[:, sl], in_=y1[:, sl], func=AF.Gelu,
                            scale=gam[:, gj * 8 + dt_i: gj * 8 + dt_i + 1],
                            bias=bet[:, gj * 8 + dt_i: gj * 8 + dt_i + 1]))
                    p.S(gl)
        p.V(apply)

    def qlora_mm(jw, rhs_b, n, resid_to=None):
        def umm(tensor):
            p.wait(tensor, "v", "s")
            for k in range(8):
                ins = tensor.matmul(psU[:, 0:n],
                                    laT[:, jw * 256 + k * 32:(jw * 256 + (k + 1) * 32)],
                                    rhs_b[:, k * n:(k + 1) * n],
                                    start=(k == 0), stop=(k == 7))
            p.inc(ins, "pe")
        p.T(umm)

        def uev(vector):
            p.wait(vector, "pe")
            p.cop(vector, "v", vector.tensor_copy(out=ub[:, 0:n], in_=psU[:, 0:n]))
        p.V(uev)

        ev_tok = {}
        mm_tok = {}
        psA2l = [psA, psA2]
        for ot in range(8):
            def main(tensor, ot=ot):
                if ot < 2:
                    p.wait(tensor, "v")
                else:
                    p.wait_tok(tensor, "v", ev_tok.get(ot - 2, 0))
                psAx = psA2l[ot % 2]
                tensor.matmul(psAx[:, 0:n],
                              brow[0:1, jw * 1024 + ot * 128: jw * 1024 + (ot + 1) * 128],
                              orb[0:1, 0:n], start=True, stop=False)
                for k in range(8):
                    tensor.matmul(psAx[:, 0:n],
                                  qw[:, jw * 8192 + k * 1024 + ot * 128:
                                     jw * 8192 + k * 1024 + (ot + 1) * 128],
                                  rhs_b[:, k * n:(k + 1) * n],
                                  start=False, stop=False)
                ins = tensor.matmul(psAx[:, 0:n],
                                    lbT[:, jw * 1024 + ot * 128: jw * 1024 + (ot + 1) * 128],
                                    ub[:, 0:n], start=False, stop=True)
                p.inc(ins, "pe")
                mm_tok[ot] = p.cnt["pe"]
            p.T(main)

            def ev(vector, ot=ot):
                p.wait_tok(vector, "pe", mm_tok[ot])
                psAx = psA2l[ot % 2]
                sl = slice(ot * n, (ot + 1) * n)
                if resid_to is None:
                    p.cop(vector, "v", vector.tensor_copy(out=y1[:, sl], in_=psAx[:, 0:n]))
                else:
                    p.cop(vector, "v", vector.tensor_add(out=resid_to[:, sl],
                                                         in0=psAx[:, 0:n],
                                                         in1=resid_to[:, sl]))
                ev_tok[ot] = p.cnt["v"]
            p.V(ev)

    for (c0, n) in chunks_of(W):
        def xin(sync, c0=c0, n=n):
            p.wait(sync, "v")
            ldma(sync, xc[:, 0:8 * n].rearrange("p (a c) -> p a c", a=8),
                 i_x.rearrange("(a p) c -> p a c", p=128)[:, :, c0:c0 + n])
        p.SY(xin)

        def wsy(vector):
            p.wait(vector, "sy")
            pass
        p.V(wsy)

        layernorm_from(xc, n, h, hb)

        for blk in range(3):
            if blk > 0:
                def cast(vector, n=n):
                    p.wait(vector, "pe", "s")
                    p.cop(vector, "v", vector.tensor_copy(out=hb[:, 0:8 * n],
                                                          in_=h[:, 0:8 * n]))
                p.V(cast)
            qlora_mm(2 * blk, hb, n, resid_to=None)
            layernorm_from(y1, n, None, None, gj=blk)
            qlora_mm(2 * blk + 1, gb, n, resid_to=h)

        def fin(vector, n=n):
            p.wait(vector, "pe", "s")
            for dt_i in range(8):
                sl = slice(dt_i * n, (dt_i + 1) * n)
                p.cop(vector, "v", vector.scalar_tensor_tensor(
                    out=y1[:, sl], in0=h[:, sl], scalar=0.5, in1=xc[:, sl],
                    op0=ALU.mult, op1=ALU.add))
        p.V(fin)

        def xout(sync, c0=c0, n=n):
            p.wait(sync, "v")
            p.inc(sync.dma_start(
                out=o_x.rearrange("(a p) c -> p a c", p=128)[:, :, c0:c0 + n],
                in_=y1[:, 0:8 * n].rearrange("p (a c) -> p a c", a=8)), "so", 16)
        p.SY(xout)

    def g(gpsimd):
        p.wait(gpsimd, "so")
    p.G(g)
    p.io = None
    if standalone:
        return p.done()
    return None


def pcol(v, ntile):
    v = np.asarray(v, np.float32)
    return np.ascontiguousarray(v.reshape(ntile, 128).T)


FFN_CONST = None


def as_bf(a):
    a = np.asarray(a)
    if a.dtype == BF:
        return a
    return a.view(BF) if a.dtype.itemsize == 2 else a.astype(BF)


def ffn_consts(z, prep):
    return {
        "qwT": as_bf(prep["qwT"]),
        "laT": as_bf(prep["laT"]),
        "lbT": as_bf(prep["lbT"]),
        "brow": as_bf(prep["brow"]),
        "gam": np.concatenate([pcol(z["L0_ffn_ln_gamma"][b], 8) for b in range(3)], 1),
        "bet": np.concatenate([pcol(z["L0_ffn_ln_beta"][b], 8) for b in range(3)], 1),
        "onescolb": np.ones((128, 1), dtype=BF),
        "onesrowb": np.ones((1, 512), dtype=BF),
        "onesrowf": np.ones((1, 128), dtype=np.float32),
    }


def run_ffn(xT_slices, consts, W):
    nc = build_ffn(W)
    in_maps = [dict(consts, xT=np.ascontiguousarray(s)) for s in xT_slices]
    res = run_bass_kernel_spmd(nc, in_maps, core_ids=list(range(NCORE)))
    return [r["oT"] for r in res.results]


def x_slices_halo(xT_full):
    out = []
    for c in range(NCORE):
        b, q = divmod(c, 4)
        g0 = b * 8192 + q * TCORE
        sl = np.zeros((D, W1), np.float32)
        lo = g0 - HALO
        seq_lo = b * 8192
        pad = max(0, seq_lo - lo)
        sl[:, pad:] = xT_full[:, lo + pad: g0 + TCORE]
        out.append(sl)
    return out



PAD = 16
WB_ = PAD + W1


def build_attn0(p=None, io=None):
    standalone = p is None
    if standalone:
        p = P()
    p.io = io or {}
    i_x = p.din("xT", [D, W1])
    i_bwT = p.din("bwT", [D, D], BF16)
    i_cT = p.din("cT", [D, D], BF16)
    i_bpow = p.din("bpow", [5, D, D], BF16)
    i_umask = p.din("umask", [128, HALO])
    i_ocb = p.din("onescolb", [128, 1], BF16)
    i_orf = p.din("onesrowf", [1, 128])
    o_x = p.dout("oT", [D, TCORE])

    xf = p.sb("xf", [128, 8 * W1])
    hbuf = p.sb("hbuf", [128, 8 * WB_], BF16)
    wA = p.sb("wA", [128, 8 * 1024], BF16)
    xnb = p.sb("xnb", [128, 8 * 512], BF16)
    sqb = p.sb("sqb", [128, 8 * 512], BF16)
    umb = p.sb("umb", [128, HALO], BF16)
    umf = p.sb("umf", [128, HALO])
    ocb = p.sb("ocb", [128, 1], BF16)
    orf = p.sb("orf", [1, 128])
    t1 = p.sb("t1", [128, 512])
    outc = p.sb("outc", [128, 8 * 512])
    rows = p.sb("rows", [1, 5 * 512])
    rowb = p.sb("rowb", [1, 2 * 512], BF16)
    orfb = p.sb("orfb", [1, 128], BF16)
    cst = p.sb("cst", [1, 8])
    psD = [p.ps(f"psD{i}", [128, 512]) for i in range(8)]
    psS1 = psD[2][0:1, 0:512]
    psS2 = psD[3][0:1, 0:512]

    def ldma(sync, dst_ap, src_ap, *wait):
        p.wait(sync, *wait)
        p.inc(sync.dma_start(out=dst_ap, in_=src_ap), "sy", 16)

    def loads(sync):
        ldma(sync, xf.rearrange("p (a c) -> p a c", a=8),
             i_x.rearrange("(a p) c -> p a c", p=128))
        ldma(sync, umf, i_umask)
        ldma(sync, ocb, i_ocb)
        ldma(sync, orf, i_orf)
        ldma(sync, wA.rearrange("p (a c) -> p a c", a=8),
             i_bwT.rearrange("(a p) c -> p a c", p=128))
    p.SY(loads)

    def mkconst(vector):
        p.wait(vector, "sy")
        p.cop(vector, "v", vector.memset(cst[0:1, 0:1], EPS))
        p.cop(vector, "v", vector.memset(cst[0:1, 1:2], -0.5))
        p.cop(vector, "v", vector.memset(orfb, 1.0))
        p.cop(vector, "v", vector.tensor_copy(out=umb, in_=umf))
        for dt_i in range(8):
            p.cop(vector, "v", vector.memset(hbuf[:, dt_i * WB_: dt_i * WB_ + PAD], 0.0))
    p.V(mkconst)

    for (c0, n) in chunks_of(W1):
        def c1(vector, c0=c0, n=n):
            p.wait(vector, "pe", "s")
            for dt_i in range(8):
                p.cop(vector, "v", vector.tensor_copy(
                    out=xnb[:, dt_i * n:(dt_i + 1) * n],
                    in_=xf[:, dt_i * W1 + c0: dt_i * W1 + c0 + n]))
            p.cop(vector, "v", vector.tensor_mul(out=sqb[:, 0:8 * n],
                                                 in0=xnb[:, 0:8 * n],
                                                 in1=xnb[:, 0:8 * n]))
        p.V(c1)

        def stats(tensor, n=n):
            p.wait(tensor, "v")
            for k in range(8):
                i0 = tensor.matmul(psS1[0:1, 0:n], ocb, xnb[:, k * n:(k + 1) * n],
                                   start=(k == 0), stop=(k == 7))
                i1 = tensor.matmul(psS2[0:1, 0:n], ocb, sqb[:, k * n:(k + 1) * n],
                                   start=(k == 0), stop=(k == 7))
            p.inc(i0, "pe")
            p.inc(i1, "pe")
        p.T(stats)

        def rowmath(vector, n=n):
            p.wait(vector, "pe")
            p.cop(vector, "v", vector.tensor_scalar(out=rows[0:1, 0:n], in0=psS1[0:1, 0:n],
                                                    scalar1=float(1.0 / D), scalar2=None,
                                                    op0=ALU.mult))
            p.cop(vector, "v", vector.tensor_scalar(out=rows[0:1, 512:512 + n],
                                                    in0=psS2[0:1, 0:n],
                                                    scalar1=float(1.0 / D), scalar2=None,
                                                    op0=ALU.mult))
            p.cop(vector, "v", vector.tensor_mul(out=rows[0:1, 1024:1024 + n],
                                                 in0=rows[0:1, 0:n], in1=rows[0:1, 0:n]))
            p.cop(vector, "v", vector.tensor_sub(out=rows[0:1, 512:512 + n],
                                                 in0=rows[0:1, 512:512 + n],
                                                 in1=rows[0:1, 1024:1024 + n]))
        p.V(rowmath)

        def rowact(scalar, n=n):
            p.wait(scalar, "v")
            p.cop(scalar, "s", scalar.activation(out=rows[0:1, 1024:1024 + n],
                                                 in_=rows[0:1, 512:512 + n],
                                                 func=AF.Ln, bias=cst[0:1, 0:1]))
            p.cop(scalar, "s", scalar.activation(out=rows[0:1, 1536:1536 + n],
                                                 in_=rows[0:1, 1024:1024 + n],
                                                 func=AF.Exp, scale=cst[0:1, 1:2]))
        p.S(rowact)

        def rown(vector, n=n):
            p.wait(vector, "s")
            p.cop(vector, "v", vector.tensor_mul(out=rows[0:1, 1024:1024 + n],
                                                 in0=rows[0:1, 0:n],
                                                 in1=rows[0:1, 1536:1536 + n]))
            p.cop(vector, "v", vector.tensor_scalar(out=rows[0:1, 2048:2048 + n],
                                                    in0=rows[0:1, 1024:1024 + n],
                                                    scalar1=-1.0, scalar2=None,
                                                    op0=ALU.mult))
            p.cop(vector, "v", vector.tensor_copy(out=rowb[0:1, 0:n],
                                                  in_=rows[0:1, 1536:1536 + n]))
            p.cop(vector, "v", vector.tensor_copy(out=rowb[0:1, 512:512 + n],
                                                  in_=rows[0:1, 2048:2048 + n]))
        p.V(rown)

        def bcast(tensor, n=n):
            p.wait(tensor, "v")
            p.inc(tensor.matmul(psD[0][:, 0:n], orfb[0:1, :], rowb[0:1, 0:n],
                                start=True, stop=True), "pe")
            p.inc(tensor.matmul(psD[1][:, 0:n], orfb[0:1, :], rowb[0:1, 512:512 + n],
                                start=True, stop=True), "pe")
        p.T(bcast)

        def apply(vector, c0=c0, n=n):
            p.wait(vector, "pe")
            for dt_i in range(8):
                p.cop(vector, "v", vector.tensor_mul(
                    out=t1[:, 0:n], in0=xf[:, dt_i * W1 + c0: dt_i * W1 + c0 + n],
                    in1=psD[0][:, 0:n]))
                p.cop(vector, "v", vector.tensor_add(
                    out=xnb[:, dt_i * n:(dt_i + 1) * n], in0=t1[:, 0:n],
                    in1=psD[1][:, 0:n]))
        p.V(apply)

        def umm(tensor, n=n):
            p.wait(tensor, "v")
            for ot in range(8):
                for k in range(8):
                    ins = tensor.matmul(
                        psD[ot][:, 0:n],
                        wA[:, k * 1024 + ot * 128: k * 1024 + (ot + 1) * 128],
                        xnb[:, k * n:(k + 1) * n], start=(k == 0), stop=(k == 7))
                p.inc(ins, "pe")
        p.T(umm)

        def uev(vector, c0=c0, n=n):
            p.wait(vector, "pe")
            for ot in range(8):
                p.cop(vector, "v", vector.tensor_copy(
                    out=hbuf[:, ot * WB_ + PAD + c0: ot * WB_ + PAD + c0 + n],
                    in_=psD[ot][:, 0:n]))
        p.V(uev)

    def mask(vector):
        p.wait(vector, "pe")
        for dt_i in range(8):
            p.cop(vector, "v", vector.tensor_mul(
                out=hbuf[:, dt_i * WB_ + PAD: dt_i * WB_ + PAD + HALO],
                in0=hbuf[:, dt_i * WB_ + PAD: dt_i * WB_ + PAD + HALO],
                in1=umb))
    p.V(mask)

    for lvl in range(5):
        sh = 1 << lvl
        def wload(sync, lvl=lvl):
            p.wait(sync, "pe", "v")
            ldma(sync, wA.rearrange("p (a c) -> p a c", a=8),
                 i_bpow[lvl].rearrange("(a p) c -> p a c", p=128))
        p.SY(wload)

        def wsy(tensor):
            p.wait(tensor, "sy")
        p.T(wsy)

        for (c0, n) in reversed(chunks_of(W1)):
            def mms(tensor, c0=c0, n=n, sh=sh):
                p.wait(tensor, "v")
                for ot in range(8):
                    for k in range(8):
                        ins = tensor.matmul(
                            psD[ot][:, 0:n],
                            wA[:, k * 1024 + ot * 128: k * 1024 + (ot + 1) * 128],
                            hbuf[:, k * WB_ + PAD + c0 - sh: k * WB_ + PAD + c0 - sh + n],
                            start=(k == 0), stop=(k == 7))
                    p.inc(ins, "pe")
            p.T(mms)

            def evac(vector, c0=c0, n=n):
                p.wait(vector, "pe")
                for ot in range(8):
                    sl = slice(ot * WB_ + PAD + c0, ot * WB_ + PAD + c0 + n)
                    p.cop(vector, "v", vector.tensor_add(
                        out=hbuf[:, sl], in0=psD[ot][:, 0:n], in1=hbuf[:, sl]))
            p.V(evac)

    def cload(sync):
        p.wait(sync, "pe", "v")
        ldma(sync, wA.rearrange("p (a c) -> p a c", a=8),
             i_cT.rearrange("(a p) c -> p a c", p=128))
    p.SY(cload)

    def wsy2(tensor):
        p.wait(tensor, "sy")
    p.T(wsy2)

    for (c0, n) in chunks_of(TCORE):
        def ymm(tensor, c0=c0, n=n):
            p.wait(tensor, "v")
            for ot in range(8):
                for k in range(8):
                    ins = tensor.matmul(
                        psD[ot][:, 0:n],
                        wA[:, k * 1024 + ot * 128: k * 1024 + (ot + 1) * 128],
                        hbuf[:, k * WB_ + PAD + HALO + c0: k * WB_ + PAD + HALO + c0 + n],
                        start=(k == 0), stop=(k == 7))
                p.inc(ins, "pe")
        p.T(ymm)

        def yev(vector, c0=c0, n=n):
            p.wait(vector, "pe")
            for ot in range(8):
                p.cop(vector, "v", vector.tensor_add(
                    out=outc[:, ot * n:(ot + 1) * n], in0=psD[ot][:, 0:n],
                    in1=xf[:, ot * W1 + HALO + c0: ot * W1 + HALO + c0 + n]))
        p.V(yev)

        def xout(sync, c0=c0, n=n):
            p.wait(sync, "v")
            p.inc(sync.dma_start(
                out=o_x.rearrange("(a p) c -> p a c", p=128)[:, :, c0:c0 + n],
                in_=outc[:, 0:8 * n].rearrange("p (a c) -> p a c", a=8)), "so", 16)
        p.SY(xout)

    def g(gpsimd):
        p.wait(gpsimd, "so")
    p.G(g)
    p.io = None
    if standalone:
        return p.done()
    return None


def attn0_consts(prep):
    return {
        "bwT": as_bf(prep["bwT"]),
        "cT": as_bf(prep["cT"]),
        "bpow": as_bf(prep["bpow"]),
        "onescolb": np.ones((128, 1), dtype=BF),
        "onesrowf": np.ones((1, 128), dtype=np.float32),
    }


def run_attn0(x1_slices, consts):
    nc = build_attn0()
    in_maps = []
    for c in range(NCORE):
        q = c % 4
        um = np.zeros((128, HALO), np.float32) if q == 0 else np.ones((128, HALO), np.float32)
        in_maps.append(dict(consts, xT=np.ascontiguousarray(x1_slices[c]), umask=um))
    res = run_bass_kernel_spmd(nc, in_maps, core_ids=list(range(NCORE)))
    return [r["oT"] for r in res.results]



def build_moe(mode, p=None, io=None):
    W = TCORE
    standalone = p is None
    if standalone:
        p = P()
    p.io = io or {}
    i_x = p.din("xT", [D, W])
    i_w1 = p.din("w1", [8, D, 2816])
    i_w2 = p.din("w2", [8, 1408, D])
    i_gwT = p.din("gwT", [D, 8], BF16)
    i_ocb = p.din("onescolb", [128, 1], BF16)
    i_orf = p.din("onesrowf", [1, 128])
    i_idf = p.din("idenf", [128, 128])
    o_x = p.dout("oT", [D, W])
    if mode == "first":
        i_lng = p.din("lng", [128, 8])
        i_lnb = p.din("lnb", [128, 8])
        o_xno = p.dout("xnoT", [D, W])
        o_v = p.dout("vT", [D, W], BF16)
    else:
        single_part = io is not None and "part1" in io
        i_parts = p.din("part1", [D, W]) if single_part else p.din("parts", [4, D, W])
        i_xno = p.din("xnoT", [D, W])
        i_mb = p.din("mergeb", [128, 8])
        xs = p.nc.dram_tensor("xs_scratch", [D, W], F32, kind="Internal").ap()

    xnb = p.sb("xnb", [128, 8 * W], BF16)
    sact = p.sb("sact", [128, 11 * W], BF16)
    outacc = p.sb("outacc", [128, 8 * W])
    w1g = p.sb("w1g", [128, 8 * 128], BF16)
    w1u = p.sb("w1u", [128, 8 * 128], BF16)
    w2b = p.sb("w2b", [128, 11 * 128], BF16)
    sqs = sact[:, 0:8 * 512]
    gwT = p.sb("gwT", [128, 8 * 8], BF16)
    lgt = p.sb("lgt", [8, W])
    cmb = lgt
    crow = p.sb("crow", [1, 512])
    tl = p.sb("tl", [128, 8])
    tr = p.sb("trr", [128, 16])
    xc = p.sb("xc", [128, 8 * 512])
    sg = p.sb("sg", [128, 512])
    t1 = p.sb("t1", [128, 512])
    xnb2 = sact[:, 8 * 512:16 * 512]
    rows = p.sb("rows", [1, 5 * 512])
    rowb = p.sb("rowb", [1, 2 * 512], BF16)
    orfb = p.sb("orfb", [1, 128], BF16)
    cst = p.sb("cst", [1, 8])
    ocb = p.sb("ocb", [128, 1], BF16)
    orf = p.sb("orf", [1, 128])
    idf = p.sb("idf", [128, 128])
    if mode == "first":
        lng = p.sb("lng", [128, 8])
        lnb = p.sb("lnb", [128, 8])
    else:
        mb = p.sb("mb", [128, 8])
    psA = p.ps("psA", [128, 512])
    psB2 = p.ps("psB2", [128, 512])
    psC = p.ps("psC", [128, 512])
    psO = p.ps("psO", [128, 512])
    psBr = p.ps("psBr", [128, 512])
    psBm = p.ps("psBm", [128, 512])
    psS1 = p.ps("psS1", [1, 512])
    psS2 = p.ps("psS2", [1, 512])
    psL = psA[0:8, 0:512]
    psR = psB2[0:128, 0:8]
    psR2 = psC[0:8, 0:128]

    def ldma(sync, dst_ap, src_ap, *wait):
        p.wait(sync, *wait)
        p.inc(sync.dma_start(out=dst_ap, in_=src_ap), "sy", 16)

    def loads(sync):
        ldma(sync, gwT.rearrange("p (a c) -> p a c", a=8),
             i_gwT.rearrange("(a p) c -> p a c", p=128))
        ldma(sync, ocb, i_ocb)
        ldma(sync, orf, i_orf)
        ldma(sync, idf, i_idf)
        if mode == "first":
            ldma(sync, lng, i_lng)
            ldma(sync, lnb, i_lnb)
        else:
            ldma(sync, mb, i_mb)
    p.SY(loads)

    def mkconst(vector):
        p.wait(vector, "sy")
        p.cop(vector, "v", vector.memset(cst[0:1, 0:1], EPS))
        p.cop(vector, "v", vector.memset(cst[0:1, 1:2], -0.5))
        p.cop(vector, "v", vector.memset(orfb, 1.0))
    p.V(mkconst)

    xsrc = i_x
    if mode == "second":
        xsrc = xs
        sactf = sact.bitcast(F32)
        for (c0, n) in chunks_of(W):
            def pin(sync, c0=c0, n=n):
                p.wait(sync, "v", "cc")
                ldma(sync, xc[:, 0:8 * n].rearrange("p (a c) -> p a c", a=8),
                     i_x.rearrange("(a p) c -> p a c", p=128)[:, :, c0:c0 + n])
                ldma(sync, sactf[:, 0:8 * n].rearrange("p (a c) -> p a c", a=8),
                     i_xno.rearrange("(a p) c -> p a c", p=128)[:, :, c0:c0 + n])
                if single_part:
                    ldma(sync, outacc[:, 0:8 * n].rearrange("p (a c) -> p a c", a=8),
                         i_parts.rearrange("(a p) c -> p a c", p=128)[:, :, c0:c0 + n])
                else:
                    for q in range(4):
                        ldma(sync, outacc[:, q * 8 * n:(q + 1) * 8 * n].rearrange(
                            "p (a c) -> p a c", a=8),
                            i_parts[q].rearrange("(a p) c -> p a c", p=128)[:, :, c0:c0 + n])
            p.SY(pin)

            def comb(vector, c0=c0, n=n):
                p.wait(vector, "sy")
                for dt_i in range(8):
                    sl = slice(dt_i * n, (dt_i + 1) * n)
                    a0 = outacc[:, 0 * 8 * n + dt_i * n: 0 * 8 * n + (dt_i + 1) * n]
                    if single_part:
                        p.cop(vector, "v", vector.tensor_add(out=t1, in0=a0,
                                                             in1=sactf[:, sl]))
                    else:
                        a1 = outacc[:, 1 * 8 * n + dt_i * n: 1 * 8 * n + (dt_i + 1) * n]
                        a2 = outacc[:, 2 * 8 * n + dt_i * n: 2 * 8 * n + (dt_i + 1) * n]
                        a3 = outacc[:, 3 * 8 * n + dt_i * n: 3 * 8 * n + (dt_i + 1) * n]
                        p.cop(vector, "v", vector.tensor_add(out=t1, in0=a0, in1=a1))
                        p.cop(vector, "v", vector.tensor_add(out=t1, in0=t1, in1=a2))
                        p.cop(vector, "v", vector.tensor_add(out=t1, in0=t1, in1=a3))
                        p.cop(vector, "v", vector.tensor_add(out=t1, in0=t1,
                                                             in1=sactf[:, sl]))
                    p.cop(vector, "v", vector.tensor_add(out=t1, in0=t1, in1=xc[:, sl]))
                    p.cop(vector, "v", vector.tensor_scalar(
                        out=xc[:, sl], in0=t1, scalar1=mb[:, dt_i:dt_i + 1],
                        scalar2=None, op0=ALU.add))
            p.V(comb)

            def pout(sync, c0=c0, n=n):
                p.wait(sync, "v")
                p.inc(sync.dma_start(
                    out=xs.rearrange("(a p) c -> p a c", p=128)[:, :, c0:c0 + n],
                    in_=xc[:, 0:8 * n].rearrange("p (a c) -> p a c", a=8)), "so", 16)
            p.SY(pout)

        def psync(vector):
            p.wait(vector, "so")
        p.V(psync)

    def emit_ln_chunk(src_getter, c0, n, out_sb, out_off, gain=None, extra_cast=None):
        def c1(vector):
            p.wait(vector, "pe", "s", "sy")
            for dt_i in range(8):
                p.cop(vector, "v", vector.tensor_copy(
                    out=xnb2[:, dt_i * n:(dt_i + 1) * n], in_=src_getter(dt_i)))
            p.cop(vector, "v", vector.tensor_mul(out=sqs[:, 0:8 * n],
                                                 in0=xnb2[:, 0:8 * n],
                                                 in1=xnb2[:, 0:8 * n]))
        p.V(c1)

        def stats(tensor):
            p.wait(tensor, "v")
            for k in range(8):
                i0 = tensor.matmul(psS1[0:1, 0:n], ocb, xnb2[:, k * n:(k + 1) * n],
                                   start=(k == 0), stop=(k == 7))
                i1 = tensor.matmul(psS2[0:1, 0:n], ocb, sqs[:, k * n:(k + 1) * n],
                                   start=(k == 0), stop=(k == 7))
            p.inc(i0, "pe")
            p.inc(i1, "pe")
        p.T(stats)

        def rowmath(vector):
            p.wait(vector, "pe")
            p.cop(vector, "v", vector.tensor_scalar(out=rows[0:1, 0:n],
                                                    in0=psS1[0:1, 0:n],
                                                    scalar1=float(1.0 / D),
                                                    scalar2=None, op0=ALU.mult))
            p.cop(vector, "v", vector.tensor_scalar(out=rows[0:1, 512:512 + n],
                                                    in0=psS2[0:1, 0:n],
                                                    scalar1=float(1.0 / D),
                                                    scalar2=None, op0=ALU.mult))
            p.cop(vector, "v", vector.tensor_mul(out=rows[0:1, 1024:1024 + n],
                                                 in0=rows[0:1, 0:n],
                                                 in1=rows[0:1, 0:n]))
            p.cop(vector, "v", vector.tensor_sub(out=rows[0:1, 512:512 + n],
                                                 in0=rows[0:1, 512:512 + n],
                                                 in1=rows[0:1, 1024:1024 + n]))
        p.V(rowmath)

        def rowact(scalar):
            p.wait(scalar, "v")
            p.cop(scalar, "s", scalar.activation(out=rows[0:1, 1024:1024 + n],
                                                 in_=rows[0:1, 512:512 + n],
                                                 func=AF.Ln, bias=cst[0:1, 0:1]))
            p.cop(scalar, "s", scalar.activation(out=rows[0:1, 1536:1536 + n],
                                                 in_=rows[0:1, 1024:1024 + n],
                                                 func=AF.Exp, scale=cst[0:1, 1:2]))
        p.S(rowact)

        def rown(vector):
            p.wait(vector, "s")
            p.cop(vector, "v", vector.tensor_mul(out=rows[0:1, 1024:1024 + n],
                                                 in0=rows[0:1, 0:n],
                                                 in1=rows[0:1, 1536:1536 + n]))
            p.cop(vector, "v", vector.tensor_scalar(out=rows[0:1, 2048:2048 + n],
                                                    in0=rows[0:1, 1024:1024 + n],
                                                    scalar1=-1.0, scalar2=None,
                                                    op0=ALU.mult))
            p.cop(vector, "v", vector.tensor_copy(out=rowb[0:1, 0:n],
                                                  in_=rows[0:1, 1536:1536 + n]))
            p.cop(vector, "v", vector.tensor_copy(out=rowb[0:1, 512:512 + n],
                                                  in_=rows[0:1, 2048:2048 + n]))
        p.V(rown)

        def bcast(tensor):
            p.wait(tensor, "v")
            p.inc(tensor.matmul(psBr[:, 0:n], orfb[0:1, :], rowb[0:1, 0:n],
                                start=True, stop=True), "pe")
            p.inc(tensor.matmul(psBm[:, 0:n], orfb[0:1, :], rowb[0:1, 512:512 + n],
                                start=True, stop=True), "pe")
        p.T(bcast)

        def apply(vector):
            p.wait(vector, "pe")
            for dt_i in range(8):
                p.cop(vector, "v", vector.tensor_mul(out=t1[:, 0:n],
                                                     in0=src_getter(dt_i),
                                                     in1=psBr[:, 0:n]))
                if gain is None:
                    p.cop(vector, "v", vector.tensor_add(
                        out=out_sb[:, out_off + dt_i * (W if out_sb is xnb else n):
                                   out_off + dt_i * (W if out_sb is xnb else n) + n],
                        in0=t1[:, 0:n], in1=psBm[:, 0:n]))
                else:
                    p.cop(vector, "v", vector.tensor_add(out=xc[:, dt_i * n:
                                                             (dt_i + 1) * n],
                                                         in0=t1[:, 0:n],
                                                         in1=psBm[:, 0:n]))
                    p.cop(vector, "v", vector.tensor_scalar(
                        out=out_sb[:, out_off + dt_i * n: out_off + dt_i * n + n],
                        in0=xc[:, dt_i * n:(dt_i + 1) * n],
                        scalar1=gain[0][:, dt_i:dt_i + 1],
                        scalar2=gain[1][:, dt_i:dt_i + 1],
                        op0=ALU.mult, op1=ALU.add))
        p.V(apply)

    for (c0, n) in chunks_of(W):
        def xin(sync, c0=c0, n=n):
            p.wait(sync, "v")
            ldma(sync, xc[:, 0:8 * n].rearrange("p (a c) -> p a c", a=8),
                 xsrc.rearrange("(a p) c -> p a c", p=128)[:, :, c0:c0 + n])
        p.SY(xin)

        emit_ln_chunk(lambda dt_i, n=n: xc[:, dt_i * n:(dt_i + 1) * n],
                      c0, n, xnb, c0)

        def gmm(tensor, c0=c0, n=n):
            p.wait(tensor, "v")
            for k in range(8):
                ins = tensor.matmul(psL[:, 0:n], gwT[:, k * 8:(k + 1) * 8],
                                    xnb[:, k * W + c0: k * W + c0 + n],
                                    start=(k == 0), stop=(k == 7))
            p.inc(ins, "pe")
        p.T(gmm)

        def gev(vector, c0=c0, n=n):
            p.wait(vector, "pe")
            p.cop(vector, "v", vector.tensor_copy(out=lgt[0:8, c0:c0 + n],
                                                  in_=psL[:, 0:n]))
        p.V(gev)

    for g0 in range(0, W, 128):
        def rt(tensor, g0=g0):
            p.wait(tensor, "v")
            p.inc(tensor.transpose(psR, lgt[0:8, g0:g0 + 128], idf[0:8, 0:8]), "pe")
        p.T(rt)

        def rmath(vector, g0=g0):
            p.wait(vector, "pe")
            p.cop(vector, "v", vector.tensor_copy(out=tl, in_=psR))
            p.cop(vector, "v", vector.reduce_max(out=tr[:, 0:1], in_=tl, axis=AX))
            p.cop(vector, "v", vector.tensor_single_scalar(
                out=tr[:, 8:16], in_=tl, scalar=tr[:, 0:1], op=ALU.is_equal))
            p.cop(vector, "v", vector.scalar_tensor_tensor(
                out=tr[:, 8:16], in0=tr[:, 8:16], scalar=-1e30, in1=tl,
                op0=ALU.mult, op1=ALU.add))
            p.cop(vector, "v", vector.reduce_max(out=tr[:, 1:2], in_=tr[:, 8:16],
                                                 axis=AX))
            p.cop(vector, "v", vector.tensor_sub(out=tr[:, 2:3], in0=tr[:, 0:1],
                                                 in1=tr[:, 1:2]))
        p.V(rmath)

        def rsig(scalar, g0=g0):
            p.wait(scalar, "v")
            p.cop(scalar, "s", scalar.activation(out=tr[:, 3:4], in_=tr[:, 2:3],
                                                 func=AF.Sigmoid))
        p.S(rsig)

        def rfin(vector, g0=g0):
            p.wait(vector, "s")
            p.cop(vector, "v", vector.tensor_scalar(out=tr[:, 4:5], in0=tr[:, 3:4],
                                                    scalar1=-1.0, scalar2=1.0,
                                                    op0=ALU.mult, op1=ALU.add))
            p.cop(vector, "v", vector.tensor_single_scalar(
                out=tr[:, 8:16], in_=tl, scalar=tr[:, 0:1], op=ALU.is_equal))
            p.cop(vector, "v", vector.tensor_scalar(out=tr[:, 8:16], in0=tr[:, 8:16],
                                                    scalar1=tr[:, 3:4], scalar2=None,
                                                    op0=ALU.mult))
            p.cop(vector, "v", vector.tensor_single_scalar(
                out=tl, in_=tl, scalar=tr[:, 1:2], op=ALU.is_equal))
            p.cop(vector, "v", vector.scalar_tensor_tensor(
                out=tl, in0=tl, scalar=tr[:, 4:5], in1=tr[:, 8:16],
                op0=ALU.mult, op1=ALU.add))
        p.V(rfin)

        def rt2(tensor, g0=g0):
            p.wait(tensor, "v")
            p.inc(tensor.transpose(psR2, tl, idf), "pe")
        p.T(rt2)

        def rev2(vector, g0=g0):
            p.wait(vector, "pe")
            p.cop(vector, "v", vector.tensor_copy(out=cmb[0:8, g0:g0 + 128],
                                                  in_=psR2))
        p.V(rev2)

    w1g2 = [w1g, p.sb("w1gB", [128, 8 * 128], BF16)]
    w1u2 = [w1u, p.sb("w1uB", [128, 8 * 128], BF16)]
    sg2 = [sg, p.sb("sgB", [128, 512])]
    t1b2 = [t1, p.sb("t1B", [128, 512])]
    crow2 = [crow, p.sb("crowB", [1, 512])]
    w2b2 = [w2b, p.sb("w2bB", [128, 11 * 128], BF16)]
    gA = [psA, psC]
    uA = [psB2, psO]
    cA = [psBr, psBm]
    g_tok = {}
    pe_tok = {}
    s_tok = {}
    v_tok = {}
    sy_tok = {}
    it = 0
    boundary_v = [p.cnt["v"]]
    for e in range(8):
        for jp in range(11):
            grp = it // 4
            wbk = grp % 2

            def w1load(gpsimd, e=e, jp=jp, wbk=wbk, it=it, grp=grp):
                p.wait_tok(gpsimd, "pe", pe_tok.get((grp - 2) * 4 + 3, 0))
                p.inc(gpsimd.dma_start(
                    out=w1g2[wbk].rearrange("p (a c) -> p a c", a=8),
                    in_=i_w1[e].rearrange("(a p) c -> p a c", p=128)[
                        :, :, jp * 128:(jp + 1) * 128]), "g", 16)
                p.inc(gpsimd.dma_start(
                    out=w1u2[wbk].rearrange("p (a c) -> p a c", a=8),
                    in_=i_w1[e].rearrange("(a p) c -> p a c", p=128)[
                        :, :, (11 + jp) * 128:(12 + jp) * 128]), "g", 16)
                g_tok[it] = p.cnt["g"]
            p.G(w1load)

            for (c0, n) in chunks_of(W):
                bk = it % 2

                def crowd(sync, e=e, c0=c0, n=n, bk=bk, it=it):
                    p.wait_tok(sync, "pe", pe_tok.get(it - 2, 0))
                    p.inc(sync.dma_start(out=crow2[bk][0:1, 0:n],
                                         in_=cmb[e:e + 1, c0:c0 + n]), "sy", 16)
                    sy_tok[it] = p.cnt["sy"]
                p.SY(crowd)

                def fmm(tensor, e=e, jp=jp, c0=c0, n=n, bk=bk, wbk=wbk, it=it):
                    p.wait_tok(tensor, "g", g_tok[(it // 4) * 4])
                    p.wait_tok(tensor, "sy", sy_tok[it])
                    p.wait_tok(tensor, "s", s_tok.get(it - 2, 0))
                    p.wait_tok(tensor, "v", max(v_tok.get(it - 2, 0), boundary_v[0]))
                    for k in range(8):
                        i0 = tensor.matmul(gA[bk][:, 0:n], w1g2[wbk][:, k * 128:(k + 1) * 128],
                                           xnb[:, k * W + c0: k * W + c0 + n],
                                           start=(k == 0), stop=(k == 7))
                        i1 = tensor.matmul(uA[bk][:, 0:n], w1u2[wbk][:, k * 128:(k + 1) * 128],
                                           xnb[:, k * W + c0: k * W + c0 + n],
                                           start=(k == 0), stop=(k == 7))
                    i2 = tensor.matmul(cA[bk][:, 0:n], orf[0:1, :],
                                       crow2[bk][0:1, 0:n], start=True, stop=True)
                    p.inc(i0, "pe")
                    p.inc(i1, "pe")
                    p.inc(i2, "pe")
                    pe_tok[it] = p.cnt["pe"]
                p.T(fmm)

                def gact(scalar, c0=c0, n=n, bk=bk, it=it):
                    p.wait_tok(scalar, "pe", pe_tok[it])
                    p.wait_tok(scalar, "v", v_tok.get(it - 2, 0))
                    p.cop(scalar, "s", scalar.activation(out=sg2[bk][:, 0:n],
                                                         in_=gA[bk][:, 0:n],
                                                         func=AF.Silu))
                    s_tok[it] = p.cnt["s"]
                p.S(gact)

                def gmul(vector, e=e, jp=jp, c0=c0, n=n, bk=bk, it=it):
                    p.wait_tok(vector, "s", s_tok[it])
                    p.cop(vector, "v", vector.tensor_mul(out=t1b2[bk][:, 0:n],
                                                         in0=sg2[bk][:, 0:n],
                                                         in1=uA[bk][:, 0:n]))
                    p.cop(vector, "v", vector.tensor_mul(
                        out=sact[:, jp * W + c0: jp * W + c0 + n],
                        in0=t1b2[bk][:, 0:n], in1=cA[bk][:, 0:n]))
                    v_tok[it] = p.cnt["v"]
                p.V(gmul)
                it += 1

        w2_tok = {}
        o_pe_tok = {}
        o_v_tok = {}
        oA = [psBr, psBm]
        for dm in range(8):
            wb = dm % 2

            def w2load(gpsimd, e=e, dm=dm, wb=wb):
                p.wait_tok(gpsimd, "pe", o_pe_tok.get((dm - 2) * 4 + 3, 0))
                p.inc(gpsimd.dma_start(
                    out=w2b2[wb].rearrange("p (a c) -> p a c", a=11),
                    in_=i_w2[e].rearrange("(a p) c -> p a c", p=128)[
                        :, :, dm * 128:(dm + 1) * 128]), "g", 16)
                w2_tok[dm] = p.cnt["g"]
            p.G(w2load)

            for ci, (c0, n) in enumerate(chunks_of(W)):
                ob = (dm * 4 + ci) % 2

                def omm(tensor, e=e, c0=c0, n=n, dm=dm, wb=wb, ob=ob, ci=ci):
                    p.wait_tok(tensor, "g", w2_tok[dm])
                    p.wait_tok(tensor, "v", o_v_tok.get((dm * 4 + ci) - 2, 0))
                    if dm == 0 and ci == 0:
                        p.wait(tensor, "v", "s")
                    for k in range(11):
                        ins = tensor.matmul(
                            oA[ob][:, 0:n], w2b2[wb][:, k * 128:(k + 1) * 128],
                            sact[:, k * W + c0: k * W + c0 + n],
                            start=(k == 0), stop=(k == 10))
                    p.inc(ins, "pe")
                    o_pe_tok[dm * 4 + ci] = p.cnt["pe"]
                p.T(omm)

                def oev(vector, e=e, c0=c0, n=n, dm=dm, ob=ob, ci=ci):
                    p.wait_tok(vector, "pe", o_pe_tok[dm * 4 + ci])
                    sl = slice(dm * W + c0, dm * W + c0 + n)
                    if e == 0:
                        p.cop(vector, "v", vector.tensor_copy(out=outacc[:, sl],
                                                              in_=oA[ob][:, 0:n]))
                    else:
                        p.cop(vector, "v", vector.tensor_add(out=outacc[:, sl],
                                                             in0=oA[ob][:, 0:n],
                                                             in1=outacc[:, sl]))
                    o_v_tok[dm * 4 + ci] = p.cnt["v"]
                p.V(oev)
        boundary_v[0] = p.cnt["v"]

    for (c0, n) in chunks_of(W):
        def fin_in(sync, c0=c0, n=n):
            p.wait(sync, "v")
            ldma(sync, xc[:, 0:8 * n].rearrange("p (a c) -> p a c", a=8),
                 xsrc.rearrange("(a p) c -> p a c", p=128)[:, :, c0:c0 + n])
        p.SY(fin_in)

        def fin(vector, c0=c0, n=n):
            p.wait(vector, "sy", "pe")
            for dt_i in range(8):
                sl = slice(dt_i * n, (dt_i + 1) * n)
                p.cop(vector, "v", vector.scalar_tensor_tensor(
                    out=xc[:, sl], in0=outacc[:, dt_i * W + c0: dt_i * W + c0 + n],
                    scalar=0.5, in1=xc[:, sl], op0=ALU.mult, op1=ALU.add))
        p.V(fin)

        def xout_d(sync, c0=c0, n=n):
            p.wait(sync, "v")
            p.inc(sync.dma_start(
                out=o_x.rearrange("(a p) c -> p a c", p=128)[:, :, c0:c0 + n],
                in_=xc[:, 0:8 * n].rearrange("p (a c) -> p a c", a=8)), "so", 16)
        p.SY(xout_d)

        if mode == "first":
            emit_ln_chunk(lambda dt_i, n=n: xc[:, dt_i * n:(dt_i + 1) * n],
                          c0, n, xnb2, 0)
            def xno_f(vector, c0=c0, n=n):
                p.wait(vector, "pe")
                for dt_i in range(8):
                    p.cop(vector, "v", vector.tensor_mul(
                        out=t1[:, 0:n], in0=xc[:, dt_i * n:(dt_i + 1) * n],
                        in1=psBr[:, 0:n]))
                    p.cop(vector, "v", vector.tensor_add(
                        out=outacc[:, dt_i * W + c0: dt_i * W + c0 + n],
                        in0=t1[:, 0:n], in1=psBm[:, 0:n]))
            p.V(xno_f)

            def xno_d(sync, c0=c0, n=n):
                p.wait(sync, "v")
                p.inc(sync.dma_start(
                    out=o_xno.rearrange("(a p) c -> p a c", p=128)[:, :, c0:c0 + n],
                    in_=outacc.rearrange("p (a c) -> p a c", a=8)[:, :, c0:c0 + n]),
                    "so", 16)
            p.SY(xno_d)

            emit_ln_chunk(
                lambda dt_i, n=n, c0=c0: outacc[:, dt_i * W + c0: dt_i * W + c0 + n],
                c0, n, xnb2, 0, gain=(lng, lnb))

            def v_d(sync, c0=c0, n=n):
                p.wait(sync, "s", "v")
                p.inc(sync.dma_start(
                    out=o_v.rearrange("(a p) c -> p a c", p=128)[:, :, c0:c0 + n],
                    in_=xnb2[:, 0:8 * n].rearrange("p (a c) -> p a c", a=8)), "so", 16)
            p.SY(v_d)

    def g(gpsimd):
        p.wait(gpsimd, "so")
    p.G(g)
    p.io = None
    if standalone:
        return p.done()
    return None


def moe_consts(z, prep, mode):
    c = {
        "w1": z["L1_ffn_w1"], "w2": z["L1_ffn_w2"],
        "gwT": as_bf(prep["gwT"]),
        "onescolb": np.ones((128, 1), dtype=BF),
        "onesrowf": np.ones((1, 128), dtype=np.float32),
        "idenf": np.eye(128, dtype=np.float32),
    }
    if mode == "first":
        c["lng"] = pcol(z["L1_attn_ln_g"], 8)
        c["lnb"] = pcol(z["L1_attn_ln_b"], 8)
    else:
        c["mergeb"] = pcol(z["L1_attn_merge_b"], 8)
    return c


def run_moe(x_slices, consts, mode, parts=None, xno=None):
    nc = build_moe(mode)
    in_maps = []
    for ci in range(NCORE):
        m = dict(consts, xT=np.ascontiguousarray(x_slices[ci]))
        if mode == "second":
            m["parts"] = np.ascontiguousarray(parts[ci])
            m["xnoT"] = np.ascontiguousarray(xno[ci])
        in_maps.append(m)
    res = run_bass_kernel_spmd(nc, in_maps, core_ids=list(range(NCORE)))
    return res.results



TSEQ = 8192


def build_m1a(p=None, io=None):
    standalone = p is None
    if standalone:
        p = P()
    p.io = io or {}
    vrank = io is not None and "vT4" in io
    rawsl = io is not None and "inwraw_f" in io
    i_v = p.din("vT4", [4, D, 2048], BF16) if vrank else p.din("vT", [D, TSEQ], BF16)
    if rawsl:
        i_inwR = [io["inwraw_f"], io["inwraw_b"]]
        i_xpR = [io["xpraw_f"], io["xpraw_b"]]
        i_iden = io["iden"]
    else:
        i_inw = [p.din("inwf_sl", [D, 1024], BF16), p.din("inwb_sl", [D, 1024], BF16)]
        i_xp = [p.din("xpf_sl", [512, 96], BF16), p.din("xpb_sl", [512, 96], BF16)]
    i_cw = [p.din("convwf", [128, 16]), p.din("convwb", [128, 16])]
    i_cb = [p.din("convbf", [128, 4]), p.din("convbb", [128, 4])]
    o_xi = p.dout("xiT", [2, 512, TSEQ])
    o_z = p.dout("zT", [2, 512, TSEQ])
    o_pp = p.dout("projp", [2, 96, TSEQ])

    vc2b = [p.sb("vcA", [128, 8 * 512], BF16), p.sb("vcB", [128, 8 * 512], BF16)]
    vc2 = p.sb("vc2", [128, 8 * 512], BF16)
    inw = [p.sb("inwf", [128, 8 * 1024], BF16), p.sb("inwb", [128, 8 * 1024], BF16)]
    cw = [p.sb("cwf", [128, 16]), p.sb("cwb", [128, 16])]
    cb = [p.sb("cbf", [128, 4]), p.sb("cbb", [128, 4])]
    xp = [p.sb("xpf", [128, 4 * 96], BF16), p.sb("xpb", [128, 4 * 96], BF16)]
    xze = p.sb("xze", [128, 4 * 516])
    last3 = p.sb("last3", [128, 4 * 4])
    xi = p.sb("xi", [128, 4 * 512])
    xib = p.sb("xib", [128, 4 * 512], BF16)
    zz = p.sb("zz", [128, 4 * 512])
    prj = p.sb("prj", [96, 512])
    psM = [p.ps("psM0", [128, 512]), p.ps("psM1", [128, 512])]
    psP = p.ps("psP", [96, 512])

    def ldma(sync, dst_ap, src_ap, *wait):
        p.wait(sync, *wait)
        p.inc(sync.dma_start(out=dst_ap, in_=src_ap), "sy", 16)

    if rawsl:
        iden_sb = p.sb("idenm", [128, 128], BF16)
        scr = p.sb("scrm", [128, 8 * 1024])
        scrb = p.sb("scrmb", [128, 8 * 1024], BF16)
        psT = p.ps("psTm", [128, 128], BF16)

        def lid(sync):
            ldma(sync, iden_sb, i_iden)
        p.SY(lid)

        for d in range(2):
            def lraw(sync, d=d):
                p.wait(sync, "pe")
                ldma(sync, scr.rearrange("p (a c) -> p a c", a=8),
                     i_inwR[d].rearrange("(a p) c -> p a c", p=128))
                ldma(sync, cw[d], i_cw[d])
                ldma(sync, cb[d], i_cb[d])
            p.SY(lraw)

            def cst_(vector, d=d):
                p.wait(vector, "sy")
                p.cop(vector, "v", vector.tensor_copy(out=scrb, in_=scr))
            p.V(cst_)

            for fr in range(8):
                for dc in range(8):
                    def tmm(tensor, d=d, fr=fr, dc=dc):
                        p.wait(tensor, "v")
                        p.inc(tensor.transpose(
                            psT, scrb[:, fr * 1024 + dc * 128: fr * 1024 + (dc + 1) * 128],
                            iden_sb), "pe")
                    p.T(tmm)

                    def tev(vector, d=d, fr=fr, dc=dc):
                        p.wait(vector, "pe")
                        p.cop(vector, "v", vector.tensor_copy(
                            out=inw[d][:, dc * 1024 + fr * 128: dc * 1024 + (fr + 1) * 128],
                            in_=psT), "v") if False else p.cop(
                            vector, "v", vector.tensor_copy(
                                out=inw[d][:, dc * 1024 + fr * 128:
                                           dc * 1024 + (fr + 1) * 128],
                                in_=psT))
                    p.V(tev)

            def lxp(sync, d=d):
                p.wait(sync, "pe", "v")
                ldma(sync, scr[0:96, 0:512], i_xpR[d])
            p.SY(lxp)

            def cxp(vector, d=d):
                p.wait(vector, "sy")
                p.cop(vector, "v", vector.tensor_copy(out=scrb[0:96, 0:512],
                                                      in_=scr[0:96, 0:512]))
            p.V(cxp)

            for a in range(4):
                def txp(tensor, d=d, a=a):
                    p.wait(tensor, "v")
                    p.inc(tensor.transpose(psT[0:128, 0:96],
                                           scrb[0:96, a * 128:(a + 1) * 128],
                                           iden_sb[0:96, 0:96]), "pe")
                p.T(txp)

                def exp_(vector, d=d, a=a):
                    p.wait(vector, "pe")
                    p.cop(vector, "v", vector.tensor_copy(
                        out=xp[d][:, a * 96:(a + 1) * 96], in_=psT[0:128, 0:96]))
                p.V(exp_)
    else:
        def loads(sync):
            for d in range(2):
                ldma(sync, inw[d].rearrange("p (a c) -> p a c", a=8),
                     i_inw[d].rearrange("(a p) c -> p a c", p=128))
                ldma(sync, cw[d], i_cw[d])
                ldma(sync, cb[d], i_cb[d])
                ldma(sync, xp[d].rearrange("p (a c) -> p a c", a=4),
                     i_xp[d].rearrange("(a p) c -> p a c", p=128))
        p.SY(loads)

    for d in range(2):
        def zhead(vector, d=d):
            p.wait(vector, "sy", "so")
            for pt in range(4):
                p.cop(vector, "v", vector.memset(xze[:, pt * 516: pt * 516 + 4], 0.0))
            p.cop(vector, "v", vector.memset(last3, 0.0))
        p.V(zhead)

        lastmm_tok = {-1: 0}
        vin_tok = {}
        ev_tok = {6: p.cnt["v"], 7: p.cnt["v"]}
        for tci in range(TSEQ // 512):
            c0 = tci * 512
            n = 512
            vcb = vc2b[tci % 2]

            def vin(sync, d=d, c0=c0, n=n, tci=tci, vcb=vcb):
                p.wait_tok(sync, "pe", lastmm_tok.get(tci - 2, 0))
                if d == 1:
                    p.wait(sync, "v")
                src_c0 = c0 if d == 0 else TSEQ - c0 - n
                if vrank:
                    q, off = divmod(src_c0, 2048)
                    ap = i_v[q].rearrange("(a p) c -> p a c", p=128)[:, :, off:off + n]
                else:
                    ap = i_v.rearrange("(a p) c -> p a c", p=128)[:, :, src_c0:src_c0 + n]
                ldma(sync, (vcb if d == 0 else vc2).rearrange("p (a c) -> p a c", a=8), ap)
                vin_tok[tci] = p.cnt["sy"]
            p.SY(vin)

            if d == 1:
                def vflip(vector, n=n, tci=tci, vcb=vcb):
                    p.wait_tok(vector, "sy", vin_tok[tci])
                    p.wait_tok(vector, "pe", lastmm_tok.get(tci - 2, 0))
                    for dt_i in range(8):
                        p.cop(vector, "v", vector.tensor_copy(
                            out=vcb[:, dt_i * n:(dt_i + 1) * n],
                            in_=vc2[:, dt_i * n:(dt_i + 1) * n][:, ::-1]))
                p.V(vflip)

            mm_tok = {}
            for m in range(8):
                def mm(tensor, d=d, m=m, n=n, tci=tci, vcb=vcb):
                    if m == 0:
                        if d == 0:
                            p.wait_tok(tensor, "sy", vin_tok[tci])
                        else:
                            p.wait(tensor, "v")
                    p.wait_tok(tensor, "v", ev_tok.get((m - 2) % 8, 0))
                    pss = psM[m % 2]
                    col = (m % 4) * 128 + (512 if m >= 4 else 0)
                    for k in range(8):
                        ins = tensor.matmul(
                            pss[:, 0:n], inw[d][:, k * 1024 + col: k * 1024 + col + 128],
                            vcb[:, k * n:(k + 1) * n], start=(k == 0), stop=(k == 7))
                    p.inc(ins, "pe")
                    mm_tok[m] = p.cnt["pe"]
                    lastmm_tok[tci] = p.cnt["pe"]
                p.T(mm)

                if m < 4:
                    def ev(vector, d=d, m=m, n=n):
                        p.wait_tok(vector, "pe", mm_tok[m])
                        pss = psM[m % 2]
                        p.cop(vector, "v", vector.tensor_copy(
                            out=xze[:, m * 516 + 1: m * 516 + 4],
                            in_=last3[:, m * 4: m * 4 + 3]))
                        p.cop(vector, "v", vector.tensor_copy(
                            out=xze[:, m * 516 + 4: m * 516 + 4 + n], in_=pss[:, 0:n]))
                        p.cop(vector, "v", vector.tensor_copy(
                            out=last3[:, m * 4: m * 4 + 3],
                            in_=xze[:, m * 516 + 1 + n: m * 516 + 4 + n]))
                        ev_tok[m] = p.cnt["v"]
                    p.V(ev)
                else:
                    def evz(vector, d=d, m=m, n=n):
                        p.wait_tok(vector, "pe", mm_tok[m])
                        pss = psM[m % 2]
                        p.cop(vector, "v", vector.tensor_copy(
                            out=zz[:, (m - 4) * n:(m - 3) * n], in_=pss[:, 0:n]))
                        ev_tok[m] = p.cnt["v"]
                    p.V(evz)

            def conv(vector, d=d, n=n):
                for pt in range(4):
                    base = pt * 516
                    p.cop(vector, "v", vector.tensor_scalar(
                        out=xi[:, pt * n:(pt + 1) * n],
                        in0=xze[:, base + 1: base + 1 + n],
                        scalar1=cw[d][:, pt * 4: pt * 4 + 1], scalar2=None,
                        op0=ALU.mult))
                    for k in range(1, 4):
                        p.cop(vector, "v", vector.scalar_tensor_tensor(
                            out=xi[:, pt * n:(pt + 1) * n],
                            in0=xze[:, base + 1 + k: base + 1 + k + n],
                            scalar=cw[d][:, pt * 4 + k: pt * 4 + k + 1],
                            in1=xi[:, pt * n:(pt + 1) * n],
                            op0=ALU.mult, op1=ALU.add))
            p.V(conv)

            def silu(scalar, d=d, n=n):
                p.wait(scalar, "v")
                for pt in range(4):
                    p.cop(scalar, "s", scalar.activation(
                        out=xi[:, pt * n:(pt + 1) * n], in_=xi[:, pt * n:(pt + 1) * n],
                        func=AF.Silu, bias=cb[d][:, pt:pt + 1]))
            p.S(silu)

            def cast(vector, n=n):
                p.wait(vector, "s")
                p.cop(vector, "v", vector.tensor_copy(out=xib[:, 0:4 * n],
                                                      in_=xi[:, 0:4 * n]))
            p.V(cast)

            def xpm(tensor, d=d, n=n):
                p.wait(tensor, "v")
                for k in range(4):
                    ins = tensor.matmul(psP[:, 0:n], xp[d][:, k * 96:(k + 1) * 96],
                                        xib[:, k * n:(k + 1) * n],
                                        start=(k == 0), stop=(k == 3))
                p.inc(ins, "pe")
            p.T(xpm)

            def pev(vector, n=n):
                p.wait(vector, "pe")
                p.cop(vector, "v", vector.tensor_copy(out=prj[:, 0:n], in_=psP[:, 0:n]))
            p.V(pev)

            def outs(sync, d=d, c0=c0, n=n):
                p.wait(sync, "v")
                p.inc(sync.dma_start(
                    out=o_xi[d].rearrange("(a p) c -> p a c", p=128)[:, :, c0:c0 + n],
                    in_=xi.rearrange("p (a c) -> p a c", a=4)), "so", 16)
                p.inc(sync.dma_start(
                    out=o_z[d].rearrange("(a p) c -> p a c", p=128)[:, :, c0:c0 + n],
                    in_=zz.rearrange("p (a c) -> p a c", a=4)), "so", 16)
                p.inc(sync.dma_start(out=o_pp[d][:, c0:c0 + n], in_=prj[:, 0:n]),
                      "so", 16)
            p.SY(outs)

            def war(vector):
                p.wait(vector, "so")
            p.V(war)

    def g(gpsimd):
        p.wait(gpsimd, "so")
    p.G(g)
    p.io = None
    if standalone:
        return p.done()
    return None


def m1a_inputs(z, prep, vT_full_by_b):
    maps = []
    inwT = {0: as_bf(prep["inwTf"]), 1: as_bf(prep["inwTb"])}
    xpT = {0: as_bf(prep["xpTf"]), 1: as_bf(prep["xpTb"])}
    cwk = {0: z["L1_attn_fwd_conv_w"], 1: z["L1_attn_bwd_conv_w"]}
    cbk = {0: z["L1_attn_fwd_conv_b"], 1: z["L1_attn_bwd_conv_b"]}
    for c in range(NCORE):
        b, s4 = divmod(c, 4)
        lo = 512 * s4
        m = {"vT": vT_full_by_b[b]}
        for d, tag in ((0, "f"), (1, "b")):
            iw = inwT[d]
            m[f"inw{tag}_sl"] = np.ascontiguousarray(
                np.concatenate([iw[:, lo:lo + 512], iw[:, 2048 + lo:2048 + lo + 512]], 1))
            cwr = cwk[d][lo:lo + 512]
            m[f"convw{tag}"] = np.ascontiguousarray(
                cwr.reshape(4, 128, 4).transpose(1, 0, 2).reshape(128, 16)).astype(np.float32)
            m[f"convb{tag}"] = pcol(cbk[d][lo:lo + 512], 4)
            m[f"xp{tag}_sl"] = np.ascontiguousarray(xpT[d][lo:lo + 512, :])
        maps.append(m)
    return maps



def build_m1b(p=None, io=None):
    standalone = p is None
    if standalone:
        p = P()
    p.io = io or {}
    i_xi = p.din("xiT", [2, 512, TSEQ])
    i_z = p.din("zT", [2, 512, TSEQ])
    fused_pp = io is not None and "pp_sum" in io
    i_pp = p.din("pp_sum", [2, 96, TSEQ]) if fused_pp else p.din("ppall", [4, 2, 96, TSEQ])
    yperm = io.get("yperm") if io is not None else None
    rawsl = io is not None and "dtwraw_f" in io
    if rawsl:
        i_dtwR = [io["dtwraw_f"], io["dtwraw_b"]]
        i_dtbR = [io["dtbraw_f"], io["dtbraw_b"]]
        i_owR = [io["outwraw_f"], io["outwraw_b"]]
        i_mdT = [io["mfT"], io["mbT"]]
        i_iden = io["iden"]
    else:
        i_dtw = [p.din("dtwf_sl", [64, 512], BF16), p.din("dtwb_sl", [64, 512], BF16)]
        i_dtb = [p.din("dtbf_r", [1, 512], BF16), p.din("dtbb_r", [1, 512], BF16)]
        i_cmb = [p.din("combf_sl", [512, 1024], BF16), p.din("combb_sl", [512, 1024], BF16)]
        i_iden = p.din("idenm1b", [128, 128], BF16)
    i_al = [p.din("alf", [128, 64]), p.din("alb", [128, 64])]
    i_dd = [p.din("ddf", [128, 4]), p.din("ddb", [128, 4])]
    i_orb = p.din("onesrowb", [1, 512], BF16)
    o_y = p.dout("ypart", [D, TSEQ])
    dsc = p.nc.dram_tensor("delta_s", [2, 512, TSEQ], F32, kind="Internal").ap()
    gsc = p.nc.dram_tensor("g_s", [2, 512, TSEQ], BF16, kind="Internal").ap()

    proj = p.sb("proj", [96, TSEQ])
    prjb = p.sb("prjb", [64, 512], BF16)
    dtw = [p.sb("dtwf", [64, 512], BF16), p.sb("dtwb", [64, 512], BF16)]
    dtb = [p.sb("dtbf", [1, 512], BF16), p.sb("dtbb", [1, 512], BF16)]
    negA = [p.sb("negAf", [128, 64]), p.sb("negAb", [128, 64])]
    dd = [p.sb("ddf", [128, 4]), p.sb("ddb", [128, 4])]
    cmbT = [p.sb("cmbf", [128, 4 * 1024], BF16), p.sb("cmbb", [128, 4 * 1024], BF16)]
    orb = p.sb("orb", [1, 512], BF16)
    cst = p.sb("cst", [1, 8])
    cstp = p.sb("cstp", [128, 2])
    dchunk = p.sb("dchunk", [128, 4 * 2048])
    xichunk = p.sb("xichunk", [128, 4 * 2048])
    dx = p.sb("dx", [128, 4 * 2048], BF16)
    scr = p.sb("scr", [128, 1024])
    ybb = p.sb("ybb", [128, 3 * 2048], BF16)
    yacc = p.sb("yacc", [128, 4 * 2048])
    at = xichunk[:, 0:2048]
    bt = ybb[:, 0:2048]
    ybuf2 = [ybb[:, 2048:4096], ybb[:, 4096:6144]]
    carry = p.sb("carry", [128, 64], BF16)
    gb = p.sb("gb", [128, 4 * 2048], BF16)
    t128 = scr[:, 0:512]
    iden_sb = p.sb("idenm", [128, 128], BF16)
    psY2 = [p.ps("psYa", [128, 2048]), p.ps("psYb", [128, 2048])]
    psD_ = psY2[0][:, 0:512]
    psO = psY2[1][:, 0:512]
    if not fused_pp:
        ppsc = p.nc.dram_tensor("pps_bc", [2, 32, TSEQ], F32, kind="Internal").ap()

    if fused_pp:
        def brow_ap(d, s):
            return i_pp[d][64 + s:65 + s, :]

        def crow_ap(d, s):
            return i_pp[d][80 + s:81 + s, :]
    else:
        def brow_ap(d, s):
            return ppsc[d][s:s + 1, :]

        def crow_ap(d, s):
            return ppsc[d][16 + s:17 + s, :]

    def ldma(sync, dst_ap, src_ap, *wait):
        p.wait(sync, *wait)
        p.inc(sync.dma_start(out=dst_ap, in_=src_ap), "sy", 16)

    def loads(sync):
        for d in range(2):
            if not rawsl:
                ldma(sync, dtw[d], i_dtw[d])
                ldma(sync, dtb[d], i_dtb[d])
                ldma(sync, cmbT[d].rearrange("p (a c) -> p a c", a=4),
                     i_cmb[d].rearrange("(a p) c -> p a c", p=128))
            ldma(sync, negA[d], i_al[d])
            ldma(sync, dd[d], i_dd[d])
        ldma(sync, orb, i_orb)
    p.SY(loads)

    def lid2(sync):
        ldma(sync, iden_sb, i_iden)
    p.SY(lid2)

    if rawsl:
        psT2 = psY2[0].bitcast(BF16)[0:128, 0:128]
        psW = psY2[1][:, 0:512]
        mdt_sb = gb[:, 2048:2048 + 4096]

        for d in range(2):
            def ldt(sync, d=d):
                p.wait(sync, "pe", "v")
                ldma(sync, yacc[:, 0:4 * 64].rearrange("p (a c) -> p a c", a=4),
                     i_dtwR[d].rearrange("(a p) c -> p a c", p=128))
                ldma(sync, yacc[:, 512:512 + 512][0:1, :], i_dtbR[d])
            p.SY(ldt)

            def cdt(vector, d=d):
                p.wait(vector, "sy")
                p.cop(vector, "v", vector.tensor_copy(
                    out=gb[:, 0:4 * 64], in_=yacc[:, 0:4 * 64]))
                p.cop(vector, "v", vector.tensor_copy(
                    out=dtb[d], in_=yacc[0:1, 512:1024]))
            p.V(cdt)

            for a in range(4):
                def tdt(tensor, d=d, a=a):
                    p.wait(tensor, "v")
                    p.inc(tensor.transpose(psT2[0:64, 0:128],
                                           gb[:, a * 64:(a + 1) * 64],
                                           iden_sb), "pe")
                p.T(tdt)

                def edt(vector, d=d, a=a):
                    p.wait(vector, "pe")
                    p.cop(vector, "v", vector.tensor_copy(
                        out=dtw[d][:, a * 128:(a + 1) * 128], in_=psT2[0:64, 0:128]))
                p.V(edt)

            for half in range(2):
                def lmd(sync, d=d, half=half):
                    p.wait(sync, "pe", "v")
                    ldma(sync, mdt_sb.rearrange("p (a c) -> p a c", a=8),
                         i_mdT[d].rearrange("(a p) c -> p a c", p=128)[
                             :, :, half * 512:(half + 1) * 512])
                p.SY(lmd)

                for ct in range(4):
                    def low(sync, d=d, ct=ct):
                        p.wait(sync, "pe", "v")
                        ldma(sync, dchunk[:, 0:8 * 128].rearrange("p (a c) -> p a c", a=8),
                             i_owR[d].rearrange("(a p) c -> p a c", p=128)[
                                 :, :, ct * 128:(ct + 1) * 128])
                    p.SY(low)

                    def cow(vector, d=d, ct=ct):
                        p.wait(vector, "sy")
                        p.cop(vector, "v", vector.tensor_copy(
                            out=gb[:, 0:8 * 128], in_=dchunk[:, 0:8 * 128]))
                    p.V(cow)

                    def pmm(tensor, d=d, ct=ct, half=half):
                        p.wait(tensor, "v")
                        for k in range(8):
                            ins = tensor.matmul(
                                psW, gb[:, k * 128:(k + 1) * 128],
                                mdt_sb[:, k * 512:(k + 1) * 512],
                                start=(k == 0), stop=(k == 7))
                        p.inc(ins, "pe")
                    p.T(pmm)

                    def pev2(vector, d=d, ct=ct, half=half):
                        p.wait(vector, "pe")
                        p.cop(vector, "v", vector.tensor_copy(
                            out=cmbT[d][:, ct * 1024 + half * 512:
                                        ct * 1024 + half * 512 + 512],
                            in_=psW))
                    p.V(pev2)

    def mkconst(vector):
        p.wait(vector, "sy")
        p.cop(vector, "v", vector.memset(cst[0:1, 0:1], -1.0))
        p.cop(vector, "v", vector.memset(cst[0:1, 1:2], 1.0))
        p.cop(vector, "v", vector.memset(cstp[:, 0:1], -1.0))
        p.cop(vector, "v", vector.memset(cstp[:, 1:2], 1.0))
    p.V(mkconst)

    def mkA(scalar):
        p.wait(scalar, "v")
        for d in range(2):
            p.cop(scalar, "s", scalar.activation(out=negA[d], in_=negA[d], func=AF.Exp))
    p.S(mkA)

    def negit(vector):
        p.wait(vector, "s")
        for d in range(2):
            p.cop(vector, "v", vector.tensor_scalar(out=negA[d], in0=negA[d],
                                                    scalar1=-1.0, scalar2=None,
                                                    op0=ALU.mult))
    p.V(negit)

    NCH = TSEQ // 2048

    for d in range(2):
        if fused_pp:
            def pin2(sync, d=d):
                p.wait(sync, "v", "cc")
                ldma(sync, proj, i_pp[d])
            p.SY(pin2)

            def psy(vector):
                p.wait(vector, "sy")
            p.V(psy)
        else:
          for tci in range(TSEQ // 512):
            c0 = tci * 512

            def pin(sync, d=d, c0=c0):
                p.wait(sync, "v")
                for q in range(4):
                    ldma(sync, dchunk[0:96, q * 512:(q + 1) * 512], i_pp[q, d][:, c0:c0 + 512])
            p.SY(pin)

            def padd(vector, c0=c0):
                p.wait(vector, "sy")
                p.cop(vector, "v", vector.tensor_add(out=proj[0:96, c0:c0 + 512],
                                                     in0=dchunk[0:96, 0:512],
                                                     in1=dchunk[0:96, 512:1024]))
                p.cop(vector, "v", vector.tensor_add(out=proj[0:96, c0:c0 + 512],
                                                     in0=proj[0:96, c0:c0 + 512],
                                                     in1=dchunk[0:96, 1024:1536]))
                p.cop(vector, "v", vector.tensor_add(out=proj[0:96, c0:c0 + 512],
                                                     in0=proj[0:96, c0:c0 + 512],
                                                     in1=dchunk[0:96, 1536:2048]))
            p.V(padd)

          def pout(sync, d=d):
              p.wait(sync, "v")
              p.inc(sync.dma_start(out=ppsc[d], in_=proj[64:96, :]), "so", 16)
          p.SY(pout)

        dp_ps = [psD_, psY2[1][:, 512:1024]]
        t1282 = [t128, scr[:, 512:1024]]
        dp_pe = {}
        dp_v = {-2: p.cnt["v"], -1: p.cnt["v"]}
        for tci in range(TSEQ // 512):
            c0 = tci * 512

            def dcast(vector, c0=c0):
                p.wait(vector, "pe")
                p.cop(vector, "v", vector.tensor_copy(out=prjb,
                                                      in_=proj[0:64, c0:c0 + 512]))
            p.V(dcast)

            for pt in range(4):
                def dmm(tensor, d=d, pt=pt, c0=c0):
                    if pt == 0:
                        p.wait(tensor, "v")
                    else:
                        p.wait_tok(tensor, "v", dp_v.get(pt - 2, 0))
                    psDx = dp_ps[pt % 2]
                    tensor.matmul(psDx, dtb[d][0:1, pt * 128:(pt + 1) * 128], orb,
                                  start=True, stop=False)
                    ins = tensor.matmul(psDx, dtw[d][:, pt * 128:(pt + 1) * 128],
                                        prjb, start=False, stop=True)
                    p.inc(ins, "pe")
                    dp_pe[pt] = p.cnt["pe"]
                p.T(dmm)

                def sp1(vector, pt=pt):
                    p.wait_tok(vector, "pe", dp_pe[pt])
                    psDx = dp_ps[pt % 2]
                    p.cop(vector, "v", vector.tensor_scalar_max(
                        out=t1282[pt % 2], in0=psDx, scalar1=0.0))
                p.V(sp1)

                def sp2(scalar, pt=pt):
                    p.wait(scalar, "v")
                    psDx = dp_ps[pt % 2]
                    p.cop(scalar, "s", scalar.activation(
                        out=dchunk[:, pt * 2048: pt * 2048 + 512], in_=psDx, func=AF.Abs))
                    p.cop(scalar, "s", scalar.activation(
                        out=dchunk[:, pt * 2048: pt * 2048 + 512],
                        in_=dchunk[:, pt * 2048: pt * 2048 + 512],
                        func=AF.Exp, scale=cstp[:, 0:1]))
                    p.cop(scalar, "s", scalar.activation(
                        out=dchunk[:, pt * 2048: pt * 2048 + 512],
                        in_=dchunk[:, pt * 2048: pt * 2048 + 512],
                        func=AF.Ln, bias=cstp[:, 1:2]))
                p.S(sp2)

                def sp3(vector, pt=pt):
                    p.wait(vector, "s")
                    p.cop(vector, "v", vector.tensor_add(
                        out=dchunk[:, pt * 2048: pt * 2048 + 512], in0=t1282[pt % 2],
                        in1=dchunk[:, pt * 2048: pt * 2048 + 512]))
                    dp_v[pt] = p.cnt["v"]
                p.V(sp3)

            def dout_(sync, d=d, c0=c0):
                p.wait(sync, "v")
                p.inc(sync.dma_start(
                    out=dsc[d].rearrange("(a p) c -> p a c", p=128)[:, :, c0:c0 + 512],
                    in_=dchunk.rearrange("p (a c) -> p a c", a=4)[:, :, 0:512]), "so", 16)
            p.SY(dout_)

            def war(vector):
                p.wait(vector, "so")
            p.V(war)

        def czero(vector, d=d):
            p.cop(vector, "v", vector.memset(carry, 0.0))
        p.V(czero)

        for cc in range(NCH):
            c0 = cc * 2048

            def sin(sync, d=d, c0=c0):
                p.wait(sync, "v")
                ldma(sync, dchunk.rearrange("p (a c) -> p a c", a=4),
                     dsc[d].rearrange("(a p) c -> p a c", p=128)[:, :, c0:c0 + 2048])
                ldma(sync, xichunk.rearrange("p (a c) -> p a c", a=4),
                     i_xi[d].rearrange("(a p) c -> p a c", p=128)[:, :, c0:c0 + 2048])
            p.SY(sin)

            def dxm(vector, d=d, c0=c0):
                p.wait(vector, "sy")
                p.cop(vector, "v", vector.tensor_mul(out=dx, in0=dchunk, in1=xichunk))
                for pt in range(4):
                    p.cop(vector, "v", vector.tensor_scalar(
                        out=yacc[:, pt * 2048:(pt + 1) * 2048],
                        in0=xichunk[:, pt * 2048:(pt + 1) * 2048],
                        scalar1=dd[d][:, pt:pt + 1], scalar2=None, op0=ALU.mult))
            p.V(dxm)

            at2 = [at, xichunk[:, 6144:8192]]
            sbB2 = [gb[:, 0:2048], gb[:, 2048:4096]]
            sbC2 = [gb[:, 4096:6144], gb[:, 6144:8192]]
            a_tok = {-2: p.cnt["v"], -1: p.cnt["v"]}
            v_tok2 = {-2: p.cnt["v"], -1: p.cnt["v"]}
            evac_v = {-2: p.cnt["v"], -1: p.cnt["v"]}
            pey_tok = {-2: p.cnt["pe"], -1: p.cnt["pe"]}
            g_tok = {}
            for pt in range(4):
                for s in range(16):
                    idx = pt * 16 + s

                    def tiles(gpsimd, d=d, s=s, idx=idx, c0=c0):
                        if idx < 2:
                            p.wait(gpsimd, "so")
                            p.wait_tok(gpsimd, "pe", p.cnt["pe"])
                        p.wait_tok(gpsimd, "v", v_tok2.get(idx - 2, 0))
                        p.inc(gpsimd.dma_start(
                            out=sbB2[idx % 2],
                            in_=brow_ap(d, s)[:, c0:c0 + 2048].broadcast_to(
                                (128, 2048))), "g", 16)
                        p.inc(gpsimd.dma_start(
                            out=sbC2[idx % 2],
                            in_=crow_ap(d, s)[:, c0:c0 + 2048].broadcast_to(
                                (128, 2048))), "g", 16)
                        g_tok[idx] = p.cnt["g"]
                    p.G(tiles)

                    def apass(scalar, d=d, s=s, pt=pt, idx=idx):
                        p.wait_tok(scalar, "v", a_tok.get(idx - 2, 0))
                        p.cop(scalar, "s", scalar.activation(
                            out=at2[idx % 2], in_=dchunk[:, pt * 2048:(pt + 1) * 2048],
                            func=AF.Exp, scale=negA[d][:, pt * 16 + s: pt * 16 + s + 1]))
                    p.S(apass)

                    def bpass(vector, s=s, pt=pt, idx=idx):
                        p.wait_tok(vector, "g", g_tok[idx])
                        p.wait(vector, "s")
                        p.cop(vector, "v", vector.tensor_mul(
                            out=bt, in0=dx[:, pt * 2048:(pt + 1) * 2048],
                            in1=sbB2[idx % 2]))
                        p.cop(vector, "v", vector.tensor_tensor_scan(
                            out=bt, data0=at2[idx % 2], data1=bt,
                            initial=carry[:, pt * 16 + s: pt * 16 + s + 1],
                            op0=ALU.mult, op1=ALU.add))
                        a_tok[idx] = p.cnt["v"]
                        p.cop(vector, "v", vector.tensor_copy(
                            out=carry[:, pt * 16 + s: pt * 16 + s + 1],
                            in_=bt[:, 2047:2048]))
                        p.wait_tok(vector, "pe", pey_tok.get(idx - 2, 0))
                        p.cop(vector, "v", vector.tensor_mul(
                            out=ybuf2[idx % 2], in0=bt, in1=sbC2[idx % 2]))
                        v_tok2[idx] = p.cnt["v"]
                    p.V(bpass)

                    def ymm(tensor, s=s, pt=pt, idx=idx):
                        p.wait_tok(tensor, "v", v_tok2[idx])
                        if s == 0:
                            p.wait_tok(tensor, "v", evac_v.get(pt - 2, 0))
                        ins = None
                        for q in range(4):
                            ins = tensor.matmul(
                                psY2[pt % 2][:, q * 512:(q + 1) * 512], iden_sb,
                                ybuf2[idx % 2][:, q * 512:(q + 1) * 512],
                                start=(s == 0), stop=(s == 15))
                        p.inc(ins, "pe")
                        pey_tok[idx] = p.cnt["pe"]
                    p.T(ymm)

                def yev(vector, pt=pt):
                    p.wait_tok(vector, "pe", pey_tok[pt * 16 + 15])
                    p.cop(vector, "v", vector.tensor_add(
                        out=yacc[:, pt * 2048:(pt + 1) * 2048],
                        in0=yacc[:, pt * 2048:(pt + 1) * 2048], in1=psY2[pt % 2]))
                    evac_v[pt] = p.cnt["v"]
                p.V(yev)

            def zin(sync, d=d, c0=c0):
                p.wait(sync, "v")
                ldma(sync, dchunk.rearrange("p (a c) -> p a c", a=4),
                     i_z[d].rearrange("(a p) c -> p a c", p=128)[:, :, c0:c0 + 2048])
            p.SY(zin)

            def ep2(scalar, d=d):
                p.wait(scalar, "v", "sy")
                for pt in range(4):
                    p.cop(scalar, "s", scalar.activation(
                        out=dchunk[:, pt * 2048:(pt + 1) * 2048],
                        in_=dchunk[:, pt * 2048:(pt + 1) * 2048], func=AF.Silu))
            p.S(ep2)

            def ep3(vector, d=d, c0=c0):
                p.wait(vector, "s")
                for pt in range(4):
                    for q in range(4):
                        sl_in = slice(pt * 2048 + q * 512, pt * 2048 + (q + 1) * 512)
                        if d == 0:
                            p.cop(vector, "v", vector.tensor_mul(
                                out=gb[:, (pt * 4 + q) * 512:(pt * 4 + q + 1) * 512],
                                in0=yacc[:, sl_in], in1=dchunk[:, sl_in]))
                        else:
                            p.cop(vector, "v", vector.tensor_mul(
                                out=t128, in0=yacc[:, sl_in], in1=dchunk[:, sl_in]))
                            p.cop(vector, "v", vector.tensor_copy(
                                out=gb[:, (pt * 4 + q) * 512:(pt * 4 + q + 1) * 512],
                                in_=t128[:, ::-1]))
            p.V(ep3)

            def gout(sync, d=d, c0=c0):
                p.wait(sync, "v")
                if d == 0:
                    p.inc(sync.dma_start(
                        out=gsc[0].rearrange("(a p) c -> p a c", p=128)[:, :, c0:c0 + 2048],
                        in_=gb.rearrange("p (a q c) -> p a q c", a=4, q=4)), "so", 16)
                else:
                    base = TSEQ - c0 - 2048
                    for q in range(4):
                        p.inc(sync.dma_start(
                            out=gsc[1].rearrange("(a p) c -> p a c", p=128)[
                                :, :, base + (3 - q) * 512: base + (4 - q) * 512],
                            in_=gb.rearrange("p (a c) -> p a c", a=4)[
                                :, :, q * 512:(q + 1) * 512]), "so", 16)
            p.SY(gout)

            def war2(vector):
                p.wait(vector, "so")
            p.V(war2)

    for tci in range(TSEQ // 512):
        c0 = tci * 512

        def gin(sync, c0=c0):
            p.wait(sync, "v")
            for d in range(2):
                ldma(sync, gb[:, d * 2048:(d + 1) * 2048].rearrange(
                    "p (a c) -> p a c", a=4),
                    gsc[d].rearrange("(a p) c -> p a c", p=128)[:, :, c0:c0 + 512])
        p.SY(gin)

        p4_pe = {}
        p4_v = {-2: p.cnt["v"], -1: p.cnt["v"]}
        psO2l = [psO, psY2[0][:, 512:1024]]
        for dm in range(8):
            def omm(tensor, c0=c0, dm=dm):
                if dm == 0:
                    p.wait(tensor, "sy", "v")
                else:
                    p.wait_tok(tensor, "sy", p.cnt["sy"])
                    p.wait_tok(tensor, "v", p4_v.get(dm - 2, 0))
                psOx = psO2l[dm % 2]
                first = True
                for d in range(2):
                    for pt in range(4):
                        ins = tensor.matmul(
                            psOx, cmbT[d][:, pt * 1024 + dm * 128:
                                          pt * 1024 + (dm + 1) * 128],
                            gb[:, (d * 4 + pt) * 512:(d * 4 + pt + 1) * 512],
                            start=first, stop=(d == 1 and pt == 3))
                        first = False
                p.inc(ins, "pe")
                p4_pe[dm] = p.cnt["pe"]
            p.T(omm)

            def oev(vector, c0=c0, dm=dm):
                p.wait_tok(vector, "pe", p4_pe[dm])
                psOx = psO2l[dm % 2]
                p.cop(vector, "v", vector.tensor_copy(out=yacc[:, dm * 512:
                                                               (dm + 1) * 512],
                                                      in_=psOx))
                p4_v[dm] = p.cnt["v"]
            p.V(oev)

        def yout(sync, c0=c0):
            p.wait(sync, "v")
            if yperm is not None:
                q, off = divmod(c0, 2048)
                dst = yperm[q].rearrange("(a p) c -> p a c", p=128)[:, :, off:off + 512]
            else:
                dst = o_y.rearrange("(a p) c -> p a c", p=128)[:, :, c0:c0 + 512]
            p.inc(sync.dma_start(
                out=dst,
                in_=yacc[:, 0:8 * 512].rearrange("p (a c) -> p a c", a=8)), "so", 16)
        p.SY(yout)

        def war3(vector):
            p.wait(vector, "so")
        p.V(war3)

    def g(gpsimd):
        p.wait(gpsimd, "so")
    p.G(g)
    p.io = None
    if standalone:
        return p.done()
    return None


def m1b_inputs(z, prep, m1a_results):
    maps = []
    dtwT = {0: as_bf(prep["dtwTf"]), 1: as_bf(prep["dtwTb"])}
    dtbr = {0: as_bf(prep["dtbrf"]), 1: as_bf(prep["dtbrb"])}
    alk = {0: z["L1_attn_fwd_A_log"], 1: z["L1_attn_bwd_A_log"]}
    ddk = {0: z["L1_attn_fwd_D"], 1: z["L1_attn_bwd_D"]}
    cmb = {0: as_bf(prep["combTf"]), 1: as_bf(prep["combTb"])}
    for c in range(NCORE):
        b, s4 = divmod(c, 4)
        lo = 512 * s4
        pp = np.stack([np.asarray(m1a_results[4 * b + q]["projp"]) for q in range(4)])
        m = {
            "xiT": np.asarray(m1a_results[c]["xiT"]),
            "zT": np.asarray(m1a_results[c]["zT"]),
            "ppall": np.ascontiguousarray(pp),
            "onesrowb": np.ones((1, 512), dtype=BF),
            "idenm1b": np.eye(128, dtype=BF),
        }
        for d, tag in ((0, "f"), (1, "b")):
            m[f"dtw{tag}_sl"] = np.ascontiguousarray(dtwT[d][:, lo:lo + 512])
            m[f"dtb{tag}_r"] = np.ascontiguousarray(dtbr[d][:, lo:lo + 512])
            al = alk[d][lo:lo + 512]
            m[f"al{tag}"] = np.ascontiguousarray(
                al.reshape(4, 128, 16).transpose(1, 0, 2).reshape(128, 64)).astype(np.float32)
            m[f"dd{tag}"] = pcol(ddk[d][lo:lo + 512], 4)
            m[f"comb{tag}_sl"] = np.ascontiguousarray(cmb[d][lo:lo + 512, :])
        maps.append(m)
    return maps



def kernel_multi(**inputs):
    z = inputs
    prep = run_prep(z)

    x = np.asarray(z["L0_attn_x"])
    xT = np.ascontiguousarray(x.reshape(16384, 1024).T)

    fconsts = ffn_consts(z, prep)
    x1s = run_ffn(x_slices_halo(xT), fconsts, W1)
    x2s = run_attn0(x1s, attn0_consts(prep))
    x3s = run_ffn(x2s, fconsts, TCORE)

    moe1 = run_moe(x3s, moe_consts(z, prep, "first"), "first")
    x4s = [np.asarray(r["oT"]) for r in moe1]
    xnos = [np.asarray(r["xnoT"]) for r in moe1]
    vs = [np.asarray(r["vT"]) for r in moe1]

    v_by_b = [np.ascontiguousarray(np.concatenate(vs[4 * b:4 * b + 4], axis=1))
              for b in range(2)]
    m1a_maps = m1a_inputs(z, prep, v_by_b)
    res_a = run_bass_kernel_spmd(build_m1a(), m1a_maps, core_ids=list(range(NCORE)))
    m1b_maps = m1b_inputs(z, prep, res_a.results)
    res_b = run_bass_kernel_spmd(build_m1b(), m1b_maps, core_ids=list(range(NCORE)))

    parts = []
    for c in range(NCORE):
        b, q = divmod(c, 4)
        t0 = q * TCORE
        parts.append(np.stack(
            [np.asarray(res_b.results[4 * b + qq]["ypart"])[:, t0:t0 + TCORE]
             for qq in range(4)]))

    out2 = run_moe(x4s, moe_consts(z, prep, "second"), "second",
                   parts=parts, xno=xnos)
    outs = [np.asarray(r["oT"]) for r in out2]
    full = np.concatenate(outs, axis=1)
    return np.ascontiguousarray(full.T).reshape(2, 8192, 1024)





RG4 = [[0, 1, 2, 3], [4, 5, 6, 7]]


def build_fused():
    p = P(ndev=8)
    shared = {
        "onescolb": p.din("onescolb", [128, 1], BF16),
        "onesrowb": p.din("onesrowb", [1, 512], BF16),
        "onesrowf": p.din("onesrowf", [1, 128]),
        "idenf": p.din("idenf", [128, 128]),
        "iden": p.din("iden", [128, 128], BF16),
        "gam": p.din("gam", [128, 24]),
        "bet": p.din("bet", [128, 24]),
        "w1": p.din("w1", [8, D, 2816]),
        "w2": p.din("w2", [8, 1408, D]),
    }

    pr = {}
    for nm, shape in [("qwT", [6, D, D]), ("laT", [6, D, 32]), ("lbT", [6, 32, D]),
                      ("bwT", [D, D]), ("cT", [D, D]), ("bpow", [5, D, D]),
                      ("inwTf", [D, 4096]), ("inwTb", [D, 4096]),
                      ("combTf", [2048, D]), ("combTb", [2048, D]),
                      ("xpTf", [2048, 96]), ("xpTb", [2048, 96]),
                      ("dtwTf", [64, 2048]), ("dtwTb", [64, 2048]),
                      ("gwT", [D, 8]), ("brow", [1, 6 * D]),
                      ("dtbrf", [1, 2048]), ("dtbrb", [1, 2048]),
                      ("mfT", [D, D]), ("mbT", [D, D])]:
        pr[nm] = p.dram(nm, shape, BF16)
    pr["sig"] = p.dram("sig", [1, 4], F32)

    with p.scope():
        build_prep(p=p, io=dict(shared, slim=True, **pr))
    p.barrier()

    x1 = p.dram("x1T", [D, W1])
    with p.scope():
        io = dict(shared, oT=x1, qwT=pr["qwT"], laT=pr["laT"], lbT=pr["lbT"],
                  brow=pr["brow"])
        build_ffn(W1, p=p, io=io)
    p.barrier()

    x2 = p.dram("x2T", [D, TCORE])
    with p.scope():
        io = dict(shared, xT=x1, oT=x2, bwT=pr["bwT"], cT=pr["cT"], bpow=pr["bpow"])
        build_attn0(p=p, io=io)
    p.barrier()

    x3 = p.dram("x3T", [D, TCORE])
    with p.scope():
        io = dict(shared, xT=x2, oT=x3, qwT=pr["qwT"], laT=pr["laT"], lbT=pr["lbT"],
                  brow=pr["brow"])
        build_ffn(TCORE, p=p, io=io)
    p.barrier()

    x4 = p.dram("x4T", [D, TCORE])
    xno = p.dram("xnoT", [D, TCORE])
    vsl = p.dram("vsl", [D, TCORE], BF16)
    with p.scope():
        io = dict(shared, xT=x3, oT=x4, xnoT=xno, vT=vsl, gwT=pr["gwT"])
        build_moe("first", p=p, io=io)
    p.barrier()

    v4 = p.dram("v4", [4 * D, TCORE], BF16)

    def cc1(gpsimd):
        p.wait(gpsimd, "so", "v", "s", "pe")
        p.inc(gpsimd.collective_compute("AllGather", ALU.bypass,
                                        replica_groups=RG4,
                                        ins=[vsl.opt()], outs=[v4.opt()]), "cc", 1)
    p.G(cc1)
    p.barrier()

    xi_s = p.dram("xi_s", [2, 512, TSEQ])
    z_s = p.dram("z_s", [2, 512, TSEQ])
    pp_part = p.dram("pp_part", [2, 96, TSEQ])
    pp_sum = p.dram("pp_sum", [2, 96, TSEQ])
    with p.scope():
        io = dict(shared, vT4=v4.rearrange("(q a) c -> q a c", q=4),
                  xiT=xi_s, zT=z_s, projp=pp_part,
                  inwraw_f=p.din("inwraw_f", [D, 1024]),
                  inwraw_b=p.din("inwraw_b", [D, 1024]),
                  xpraw_f=p.din("xpraw_f", [96, 512]),
                  xpraw_b=p.din("xpraw_b", [96, 512]))
        build_m1a(p=p, io=io)

    def cc2(gpsimd):
        p.wait(gpsimd, "so", "v", "s", "pe")
        p.inc(gpsimd.collective_compute("AllReduce", ALU.add,
                                        replica_groups=RG4,
                                        ins=[pp_part.rearrange("a b c -> (a b) c").opt()],
                                        outs=[pp_sum.rearrange("a b c -> (a b) c").opt()]),
              "cc", 1)
    p.G(cc2)
    p.barrier()

    yperm = p.dram("yperm", [4 * D, TCORE])
    with p.scope():
        io = dict(shared, xiT=xi_s, zT=z_s, pp_sum=pp_sum,
                  yperm=yperm.rearrange("(q a) c -> q a c", q=4),
                  dtwraw_f=p.din("dtwraw_f", [512, 64]),
                  dtwraw_b=p.din("dtwraw_b", [512, 64]),
                  dtbraw_f=p.din("dtbraw_f", [1, 512]),
                  dtbraw_b=p.din("dtbraw_b", [1, 512]),
                  outwraw_f=p.din("outwraw_f", [D, 512]),
                  outwraw_b=p.din("outwraw_b", [D, 512]),
                  mfT=pr["mfT"], mbT=pr["mbT"])
        build_m1b(p=p, io=io)

    y_rs = p.dram("y_rs", [D, TCORE])

    def cc3(gpsimd):
        p.wait(gpsimd, "so", "v", "s", "pe")
        p.inc(gpsimd.collective_compute("ReduceScatter", ALU.add,
                                        replica_groups=RG4,
                                        ins=[yperm.opt()], outs=[y_rs.opt()]), "cc", 1)
    p.G(cc3)
    p.barrier()

    with p.scope():
        io = dict(shared, xT=x4, part1=y_rs, xnoT=xno, gwT=pr["gwT"])
        build_moe("second", p=p, io=io)
    return p.done()


def fused_inputs(z):
    x = np.asarray(z["L0_attn_x"])
    xT = np.ascontiguousarray(x.reshape(16384, 1024).T)
    xsl = x_slices_halo(xT)
    base = prep_inputs(z)
    cwk = {0: z["L1_attn_fwd_conv_w"], 1: z["L1_attn_bwd_conv_w"]}
    cbk = {0: z["L1_attn_fwd_conv_b"], 1: z["L1_attn_bwd_conv_b"]}
    alk = {0: z["L1_attn_fwd_A_log"], 1: z["L1_attn_bwd_A_log"]}
    ddk = {0: z["L1_attn_fwd_D"], 1: z["L1_attn_bwd_D"]}
    maps = []
    for c in range(NCORE):
        b, s4 = divmod(c, 4)
        lo = 512 * s4
        m = dict(base)
        m.update({
            "xT": xsl[c],
            "umask": (np.zeros((128, HALO), np.float32) if s4 == 0
                      else np.ones((128, HALO), np.float32)),
            "onescolb": np.ones((128, 1), dtype=BF),
            "onesrowb": np.ones((1, 512), dtype=BF),
            "onesrowf": np.ones((1, 128), dtype=np.float32),
            "idenf": np.eye(128, dtype=np.float32),
            "gam": np.concatenate([pcol(z["L0_ffn_ln_gamma"][bb], 8) for bb in range(3)], 1),
            "bet": np.concatenate([pcol(z["L0_ffn_ln_beta"][bb], 8) for bb in range(3)], 1),
            "w1": z["L1_ffn_w1"], "w2": z["L1_ffn_w2"],
            "lng": pcol(z["L1_attn_ln_g"], 8),
            "lnb": pcol(z["L1_attn_ln_b"], 8),
            "mergeb": pcol(z["L1_attn_merge_b"], 8),
        })
        inwk = {0: z["L1_attn_fwd_in_w"], 1: z["L1_attn_bwd_in_w"]}
        xpk = {0: z["L1_attn_fwd_xproj_w"], 1: z["L1_attn_bwd_xproj_w"]}
        dtwk = {0: z["L1_attn_fwd_dt_w"], 1: z["L1_attn_bwd_dt_w"]}
        dtbk = {0: z["L1_attn_fwd_dt_b"], 1: z["L1_attn_bwd_dt_b"]}
        owk = {0: z["L1_attn_fwd_out_w"], 1: z["L1_attn_bwd_out_w"]}
        for d, tag in ((0, "f"), (1, "b")):
            cwr = cwk[d][lo:lo + 512]
            m[f"convw{tag}"] = np.ascontiguousarray(
                cwr.reshape(4, 128, 4).transpose(1, 0, 2).reshape(128, 16)).astype(np.float32)
            m[f"convb{tag}"] = pcol(cbk[d][lo:lo + 512], 4)
            al = alk[d][lo:lo + 512]
            m[f"al{tag}"] = np.ascontiguousarray(
                al.reshape(4, 128, 16).transpose(1, 0, 2).reshape(128, 64)).astype(np.float32)
            m[f"dd{tag}"] = pcol(ddk[d][lo:lo + 512], 4)
            m[f"inwraw_{tag}"] = np.ascontiguousarray(np.concatenate(
                [inwk[d][lo:lo + 512, :], inwk[d][2048 + lo:2048 + lo + 512, :]], 0))
            m[f"xpraw_{tag}"] = np.ascontiguousarray(xpk[d][:, lo:lo + 512])
            m[f"dtwraw_{tag}"] = np.ascontiguousarray(dtwk[d][lo:lo + 512, :])
            m[f"dtbraw_{tag}"] = np.ascontiguousarray(dtbk[d][lo:lo + 512].reshape(1, 512))
            m[f"outwraw_{tag}"] = np.ascontiguousarray(owk[d][:, lo:lo + 512])
        maps.append(m)
    return maps


FUSED = True


def kernel(**inputs):
    z = inputs
    if FUSED:
        nc = build_fused()
        maps = fused_inputs(z)
        res = run_bass_kernel_spmd(nc, maps, core_ids=list(range(NCORE)))
        outs = [np.asarray(r["oT"]) for r in res.results]
        full = np.concatenate(outs, axis=1)
        return np.ascontiguousarray(full.T).reshape(2, 8192, 1024)
    return kernel_multi(**inputs)
